# version 02
# baseline (speedup 1.0000x reference)
import numpy as np
from contextlib import ExitStack

import concourse.bass as bass
import concourse.mybir as mybir
from concourse.bass_utils import run_bass_kernel_spmd

F32 = mybir.dt.float32
BF16 = mybir.dt.bfloat16
AF = mybir.ActivationFunctionType
ALU = mybir.AluOpType
AX = mybir.AxisListType

B, D, HH, E, FF = 4, 1024, 8, 8, 4096
DCN = D // 128
EPS_L2, EPS_RMS, EPS_LN, TEMP = 1e-6, 1e-5, 1e-5, 1.5
RSK = float(1.0 / np.sqrt(128.0))


class Sems:
    def __init__(self, nc, ctx):
        self.nc, self.ctx, self.m = nc, ctx, {}

    def new(self, *names):
        for n in names:
            self.m[n] = [self.ctx.enter_context(self.nc.semaphore(n)), 0]

    def inc(self, name, instr, k=1):
        e = self.m[name]
        instr.then_inc(e[0], k)
        e[1] += k
        return e[1]

    def tok(self, name):
        return self.m[name][1]

    def wait(self, eng, name, tok=None):
        e = self.m[name]
        v = e[1] if tok is None else tok
        if v > 0:
            eng.wait_ge(e[0], v)

    def toks(self):
        return {k: v[1] for k, v in self.m.items()}


def bc2(ap2d, n, w=128):
    return ap2d.unsqueeze(2).broadcast_to([ap2d.shape[0], n, w])


def mbc(m, n):
    return m[:, :].unsqueeze(1).broadcast_to([128, n, 128])


def build(T=4096, debug=False, p3lim=None):
    TGD = 256
    NGD = T // TGD
    CPG = 2
    HT = T // 2
    NTT = HT // 128
    TGM = 256
    NGM = HT // TGM
    TGL = 256
    NGL = HT // TGL

    nc = bass.Bass("TRN2", target_bir_lowering=False,
                   dynamic_dma_scratch_size=16384)

    din = lambda name, shape, dt=F32: nc.dram_tensor(name, shape, dt, kind="ExternalInput")
    x_d = din("xb", [T, D])
    xh_d = din("xh", [HT, D])
    wvT_d = din("wvT", [128, DCN, D])
    wgT_d = din("wgT", [128, DCN, D])
    woT_d = din("woT", [128, DCN, D])
    wbT_d = din("wbT", [128, DCN, HH])
    waT_d = din("waT", [128, DCN, HH])
    dtb_d = din("dtb", [HH, 1])
    alog_d = din("alog", [HH, 1])
    normw_d = din("normw", [128, 1])
    convw_d = din("convw", [128, DCN, 4])
    convb_d = din("convb", [128, DCN])
    gam_d = din("gam", [128, DCN])
    betf_d = din("betf", [128, DCN])
    rwT_d = din("rwT", [128, DCN, E])
    w1_d = din("w1", [E, 128, DCN, FF])
    w2_d = din("w2", [E, FF, D])
    msel_d = din("msel", [128, 2])
    y_d = nc.dram_tensor("y", [HT, D], F32, kind="ExternalOutput")
    if debug:
        dbg_om = nc.dram_tensor("dbg_om", [128, HH * (HT // 128) * 128], BF16,
                                kind="ExternalOutput")
        dbg_x1 = nc.dram_tensor("dbg_x1", [128, DCN, HT], F32, kind="ExternalOutput")
        dbg_xn = nc.dram_tensor("dbg_xn", [128, DCN, HT], BF16, kind="ExternalOutput")
        dbg_wd = nc.dram_tensor("dbg_wd", [8, HT], F32, kind="ExternalOutput")
        dbg_acc = nc.dram_tensor("dbg_acc", [128, DCN, HT], F32, kind="ExternalOutput")
        dbg_acc0 = nc.dram_tensor("dbg_acc0", [128, DCN, HT], F32, kind="ExternalOutput")
        dbg_w2 = nc.dram_tensor("dbg_w2", [2, 128, 2, D], BF16, kind="ExternalOutput")
        dbg_h = nc.dram_tensor("dbg_h", [128, 4, TGM], BF16, kind="ExternalOutput")
        dbg_wbc = nc.dram_tensor("dbg_wbc", [128, TGM], F32, kind="ExternalOutput")
        dbg_ps = nc.dram_tensor("dbg_ps", [128, 2, TGM], F32, kind="ExternalOutput")

    with ExitStack() as ctx:
        S = Sems(nc, ctx)
        S.new("dX", "dS", "dW0", "dWVG", "dWO", "T", "V", "Sc", "G", "st0", "st1",
              *[f"dW1p{i}" for i in range(8)], "dW2s0", "dW2s1")
        sbR = lambda name, shape, dt=F32: ctx.enter_context(
            nc.sbuf_tensor(name, shape, dt, side="right"))

        identb = sbR("identb", [128, 128], BF16)
        identf = sbR("identf", [128, 128])
        ones_colf = sbR("ones_colf", [128, 1])
        ones_row = sbR("ones_row", [1, 128])
        triuF = sbR("triuF", [128, 128])
        trilP = sbR("trilP", [128, 128])
        strN = sbR("strN", [128, 128])
        ntriP = sbR("ntriP", [128, 128])
        triK = sbR("triK", [128, 128])
        e127 = sbR("e127", [128, 1])
        e8c = sbR("e8c", [8, 8])
        msel_s = sbR("msel_s", [128, 2])
        normw_s = sbR("normw_s", [128, 1])
        dtb_s = sbR("dtb_s", [HH, 1])
        nea_s = sbR("nea_s", [HH, 1])
        convb_s = sbR("convb_s", [128, DCN])
        convw_s = sbR("convw_s", [128, DCN, 4])
        gam_s = sbR("gam_s", [128, DCN])
        betf_s = sbR("betf_s", [128, DCN])
        rwT_s = sbR("rwT_s", [128, DCN, E])
        rwg_s = sbR("rwg_s", [128, DCN, E])
        rbc_s = sbR("rbc_s", [8, 1])
        wbT_s = sbR("wbT_s", [128, DCN, HH], BF16)
        waT_s = sbR("waT_s", [128, DCN, HH], BF16)
        fc = sbR("fconst", [128, 4])
        rb_row = sbR("rb_row", [1, E])
        rwtmp = sbR("rwtmp", [128, E])
        om_ctx = ExitStack()
        o_mine = om_ctx.enter_context(
            nc.sbuf_tensor("o_mine", [128, HH, NTT, 128], BF16, side="right"))

        pb = [ctx.enter_context(nc.psum_tensor(f"pb{i}", [128, 512], F32))
              for i in range(6)]
        pbB0 = pb[0]
        px = [ctx.enter_context(nc.psum_tensor(f"px{i}", [128, 1024], BF16))
              for i in range(2)]

        blk = ctx.enter_context(nc.Block())

        C = dict(identb=identb, identf=identf, ones_colf=ones_colf,
                 ones_row=ones_row, triuF=triuF, trilP=trilP, strN=strN,
                 ntriP=ntriP, triK=triK, e127=e127, e8c=e8c, msel=msel_s,
                 normw=normw_s, dtb=dtb_s, nea=nea_s, convb=convb_s,
                 convw=convw_s, gam=gam_s, betf=betf_s, rwT=rwT_s,
                 rwg=rwg_s, rbc=rbc_s, wbT=wbT_s, waT=waT_s, fc=fc)

        @blk.vector
        def _(v):
            for buf, val in [(identb, 1.0), (identf, 1.0), (ones_colf, 1.0),
                             (ones_row, 1.0), (triuF, 1.0), (trilP, 1.0),
                             (strN, -1.0), (ntriP, -1.0), (triK, RSK),
                             (e127, 1.0), (e8c, 1.0)]:
                v.memset(buf[:, :], val)
            v.memset(fc[:, 0:1], EPS_L2)
            v.memset(fc[:, 1:2], 1.0)
            v.memset(fc[:, 2:3], EPS_RMS)
            v.memset(fc[:, 3:4], EPS_LN)
            i = v.memset(o_mine[:, :, :, :], 0.0)
            S.inc("V", i)

        @blk.gpsimd
        def _(g):
            S.wait(g, "V")
            aff = lambda buf, base, pat, cm, op=ALU.is_ge: g.affine_select(
                out=buf[:, :], in_=buf[:, :], compare_op=op,
                fill=0.0, base=base, pattern=pat, channel_multiplier=cm)
            aff(identb, 0, [[-1, 128]], 1, ALU.is_equal)
            aff(identf, 0, [[-1, 128]], 1, ALU.is_equal)
            aff(triuF, 0, [[1, 128]], -1)
            aff(trilP, 0, [[-1, 128]], 1)
            aff(strN, -1, [[-1, 128]], 1)
            aff(ntriP, 0, [[1, 128]], -1)
            aff(triK, 0, [[1, 128]], -1)
            aff(e127, -127, [[0, 1]], 1, ALU.is_equal)
            i = aff(e8c, 0, [[-1, 8]], 1, ALU.is_equal)
            S.inc("G", i)
            S.inc("dW0", g.dma_start(rwT_s[:, :, :], rwT_d[:, :, :]), 16)
            S.inc("dW0", g.dma_start(wbT_s[:, :, :], wbT_d[:, :, :]), 16)
            S.inc("dW0", g.dma_start(waT_s[:, :, :], waT_d[:, :, :]), 16)

        @blk.sync
        def _(sy):
            for dst, srcd in [(msel_s, msel_d), (normw_s, normw_d),
                              (dtb_s, dtb_d), (nea_s, alog_d),
                              (convb_s, convb_d), (convw_s, convw_d),
                              (gam_s, gam_d), (betf_s, betf_d)]:
                S.inc("dS", sy.dma_start(dst.ap(), srcd.ap()), 16)

        @blk.scalar
        def _(s):
            S.wait(s, "dS")
            i = s.activation(nea_s[:, :], nea_s[:, :], AF.Exp, scale=1.0)
            S.inc("Sc", i)

        @blk.vector
        def _(v):
            S.wait(v, "Sc")
            i = v.tensor_scalar_mul(nea_s[:, :], nea_s[:, :], -1.0)
            S.inc("V", i)
            S.wait(v, "dW0")
            for dc in range(DCN):
                i = v.tensor_scalar_mul(rwg_s[:, dc, :], rwT_s[:, dc, :],
                                        C["gam"][:, dc:dc + 1])
                S.inc("V", i)
        rwg_tok = S.tok("V")

        @blk.vector
        def _(v):
            pass

        @blk.scalar
        def _(s):
            pass

        @blk.tensor
        def _(t):
            S.wait(t, "V", rwg_tok)

        for dc in range(DCN):
            @blk.vector
            def _(v, dc=dc):
                S.wait(v, "T")
                i = v.tensor_scalar_mul(rwtmp[:, :], rwT_s[:, dc, :],
                                        C["betf"][:, dc:dc + 1])
                S.inc("V", i)
            vq = S.tok("V")

            @blk.tensor
            def _(t, dc=dc, vq=vq):
                S.wait(t, "V", vq)
                i = t.matmul(px[0].bitcast(F32)[0:1, 0:E] if False else
                             None, None, None) if False else t.matmul(
                    pbB0[0:1, 0:E], C["ones_colf"][:, :], rwtmp[:, :],
                    start=(dc == 0), stop=(dc == DCN - 1))
                S.inc("T", i)
        rb_T = S.tok("T")

        @blk.scalar
        def _(s):
            S.wait(s, "T", rb_T)
            i = s.copy(rb_row[:, :], pbB0[0:1, 0:E])
            S.inc("Sc", i)
        rb_Sc = S.tok("Sc")

        @blk.tensor
        def _(t):
            S.wait(t, "Sc", rb_Sc)
            i = t.transpose(pbB0[0:8, 64:65], rb_row[:, :], C["identf"][0:1, 0:1])
            S.inc("T", i)
        rbt_T = S.tok("T")

        @blk.vector
        def _(v):
            S.wait(v, "T", rbt_T)
            i = v.tensor_copy(rbc_s[:, :], pbB0[0:8, 64:65])
            S.inc("V", i)

        _p1(nc, blk, S, C, o_mine, pb, px,
            dict(T=T, NGD=NGD, CPG=CPG, TGD=TGD, NTT=NTT), x_d, wvT_d, wgT_d, ctx)

        _barrier(blk, S)

        sbL = lambda name, shape, dt=F32: ctx.enter_context(nc.sbuf_tensor(name, shape, dt))
        out_acc = sbL("out_acc", [128, DCN, HT])
        xn_s = sbL("xn", [128, DCN, HT], BF16)
        wdT_s = sbL("wdT", [8, HT])
        _p2(nc, blk, S, C, o_mine, out_acc, xn_s, wdT_s, pb, px,
            dict(HT=HT, NGL=NGL, TGL=TGL), xh_d, woT_d)

        if debug:
            S.new("dbg")

            @blk.sync
            def _(g):
                for k, v_ in S.toks().items():
                    if k != "dbg":
                        S.wait(g, k, v_)
                S.inc("dbg", g.dma_start(dbg_om[:, :], o_mine[:, :, :, :].rearrange("p a b c -> p (a b c)")), 16)
                S.inc("dbg", g.dma_start(dbg_x1[:, :, :], out_acc[:, :, :]), 16)
                S.inc("dbg", g.dma_start(dbg_xn[:, :, :], xn_s[:, :, :]), 16)
                S.inc("dbg", g.dma_start(dbg_wd[:, :], wdT_s[:, :]), 16)
                S.wait(g, "dbg")

        _barrier(blk, S)
        om_ctx.close()

        _p3(nc, blk, S, C, out_acc, xn_s, wdT_s, pb,
            dict(HT=HT, NGM=NGM, TGM=TGM, p3lim=p3lim), w1_d, w2_d,
            (dbg_acc0, dbg_w2, dbg_h, dbg_wbc, dbg_ps) if debug else None)

        if debug:
            @blk.sync
            def _(g):
                for k, v_ in S.toks().items():
                    if k != "dbg":
                        S.wait(g, k, v_)
                S.inc("dbg", g.dma_start(dbg_acc[:, :, :], out_acc[:, :, :]), 16)
                S.wait(g, "dbg")

        _barrier(blk, S)

        _p4(nc, blk, S, C, out_acc, pb, dict(HT=HT), y_d)

        @blk.sync
        def _(sy):
            S.wait(sy, "st0")
            S.wait(sy, "st1")

    return nc


def _barrier(blk, S):
    t = S.toks()
    for deco in (blk.tensor, blk.vector, blk.scalar, blk.gpsimd, blk.sync):
        def f(eng, _t=t):
            for k, v in _t.items():
                S.wait(eng, k, v)
        deco(f)



def _p1(nc, blk, S, C, o_mine, pb, px, P, x_d, wvT_d, wgT_d, octx):
    T, NGD, CPG, TGD, NTT = P["T"], P["NGD"], P["CPG"], P["TGD"], P["NTT"]
    with ExitStack() as ctx:
        sb = lambda name, shape, dt=F32: ctx.enter_context(nc.sbuf_tensor(name, shape, dt))
        wvT_s = sb("wvT_s", [128, DCN, D], BF16)
        wgT_s = sb("wgT_s", [128, DCN, D], BF16)
        S_s = sb("S_s", [128, HH, 128])
        S16 = sb("S16", [128, HH, 128], BF16)
        x_tm = sb("x_tm", [128, 2, D])
        xfm = [sb(f"xfm{i}", [128, DCN, TGD], BF16) for i in range(2)]
        kmj = [sb(f"kmj{i}", [128, HH, TGD + 1], BF16) for i in range(2)]
        l2tmp = sb("l2tmp", [128, 2, D], BF16)
        sqt = sb("sqt", [128, D])
        ssq = sb("ssq", [128, 2 * HH])
        rsq = sb("rsq", [128, 2 * HH])
        vpre = [sb(f"vpre{i}", [128, DCN, TGD + 3]) for i in range(2)]
        cvt = [sb(f"cvt{i}", [128, TGD]) for i in range(2)]
        vfm = sb("vfm", [128, DCN, TGD], BF16)
        vtm = [sb(f"vtm{i}", [128, 2, HH, 128], BF16) for i in range(2)]
        gfm = sb("gfm", [128, DCN, TGD], BF16)
        gtm = [sb(f"gtm{i}", [128, 2, HH, 128]) for i in range(2)]
        wktm = [sb(f"wktm{i}", [128, 2, HH, 128], BF16) for i in range(2)]
        bdr = sb("bdr", [8, 2, TGD])
        beta_cm = sb("beta_cm", [128, 2 * HH])
        dec_cm = sb("dec_cm", [128, 2 * HH])
        dec_cum = sb("dec_cum", [128, 2 * HH])
        dlast = sb("dlast", [1, 2 * HH])
        elsr = sb("elsr", [1, 2 * HH])
        esl = sb("esl", [128, 2 * HH])
        dwv = sb("dwv", [128, 2 * HH])
        edec = sb("edec", [128, 2 * HH])
        edk = sb("edk", [128, 2 * HH])
        bev = sb("bev", [128, 2 * HH])
        mt1 = sb("mt1", [128, 2, 128])
        mt2 = sb("mt2", [128, 2, 128])
        lsnB = sb("lsnB", [128, 2, 128])
        ltk = sb("ltk", [128, 2, 128])
        abuf = sb("abuf", [128, 2, 128], BF16)
        atbuf = sb("atbuf", [128, 2, 128], BF16)
        pw = [sb(f"pw{i}", [128, 2, 128], BF16) for i in range(2)]
        pwT = [sb(f"pwT{i}", [128, 2, 128], BF16) for i in range(2)]
        gch = sb("gch", [128, 2, 128], BF16)
        vbb = sb("vbb", [128, 2, 128], BF16)
        wkbe = sb("wkbe", [128, 2, 128], BF16)
        vpr = sb("vpr", [128, HH, 2, 128], BF16)
        wkc = sb("wkc", [128, 2, 128], BF16)
        wkcT = sb("wkcT", [128, HH, 2, 128], BF16)
        atn = sb("atn", [128, HH, 2, 128], BF16)
        vnew = sb("vnew", [128, 128], BF16)
        wkdw = sb("wkdw", [128, 128], BF16)
        o_grp = sb("o_grp", [128, HH, TGD])
        ogt = sb("ogt", [128, TGD])
        rrs = sb("rrs", [128, 2 * HH])

        @blk.gpsimd
        def _(gp):
            S.wait(gp, "G")
            S.inc("dWVG", gp.dma_start(wvT_s[:, :, :], wvT_d[:, :, :]), 16)
            S.inc("dWVG", gp.dma_start(wgT_s[:, :, :], wgT_d[:, :, :]), 16)
        wload = S.tok("dWVG")

        @blk.vector
        def _(v):
            v.memset(S_s[:, :, :], 0.0)
            v.memset(S16[:, :, :], 0.0)
            v.memset(kmj[0][:, :, 0:1], 0.0)
            i = v.memset(vpre[0][:, :, 0:3], 0.0)
            S.inc("V", i)
        init_tok = S.tok("V")

        prev = {}

        for g in range(NGD):
            par = g % 2

            @blk.sync
            def _(sy):
                if prev:
                    S.wait(sy, "V", prev["x_V"])
                    S.wait(sy, "T", prev["x_T"])
                i = sy.dma_start(
                    x_tm[:, :, :],
                    x_d[g * TGD:(g + 1) * TGD, :].rearrange("(tt p) d -> p tt d", p=128))
                S.inc("dX", i, 16)
            x_dma = S.tok("dX")

            for deco in (blk.tensor, blk.vector, blk.scalar):
                def f(eng):
                    S.wait(eng, "dX", x_dma)
                    if prev:
                        S.wait(eng, "T", prev["end_T"])
                        S.wait(eng, "V", prev["end_V"])
                        S.wait(eng, "Sc", prev["end_Sc"])
                    else:
                        S.wait(eng, "V", init_tok)
                        S.wait(eng, "G")
                        S.wait(eng, "dWVG", wload)
                        S.wait(eng, "Sc")
                        S.wait(eng, "dS")
                        S.wait(eng, "dW0")
                deco(f)

            @blk.vector
            def _(v):
                for tt in range(2):
                    S.wait(v, "V")
                    i = v.tensor_mul(sqt[:, :], x_tm[:, tt, :], x_tm[:, tt, :])
                    S.inc("V", i)
                    S.wait(v, "V")
                    i = v.reduce_sum(ssq[:, tt * HH:(tt + 1) * HH],
                                     sqt[:, :].rearrange("p (c k) -> p c k", c=HH), AX.X)
                    S.inc("V", i)
            sumsq_tok = S.tok("V")

            @blk.scalar
            def _(s):
                S.wait(s, "V", sumsq_tok)
                i = s.activation(rsq[:, :], ssq[:, :], AF.Ln, bias=C['fc'][:, 0:1])
                S.inc("Sc", i)
                S.wait(s, "Sc")
                i = s.activation(rsq[:, :], rsq[:, :], AF.Exp, scale=-0.5)
                S.inc("Sc", i)
            rsq_tok = S.tok("Sc")

            @blk.vector
            def _(v):
                S.wait(v, "Sc", rsq_tok)
                for tt in range(2):
                    i = v.tensor_tensor(
                        l2tmp[:, tt, :].rearrange("p (c k) -> p c k", c=HH),
                        x_tm[:, tt, :].rearrange("p (c k) -> p c k", c=HH),
                        bc2(rsq[:, tt * HH:(tt + 1) * HH], HH), op=ALU.mult)
                    S.inc("V", i)
            l2n_tok = S.tok("V")

            ev = []
            for tt in range(2):
                for hf in range(2):
                    bank = pb[hf]

                    @blk.tensor
                    def _(t, tt=tt, hf=hf, bank=bank):
                        if len(ev) >= 2:
                            S.wait(t, "Sc", ev[-2])
                        for q in range(4):
                            dc = hf * 4 + q
                            i = t.transpose(bank[:, q * 128:(q + 1) * 128],
                                            x_tm[:, tt, dc * 128:(dc + 1) * 128],
                                            C["identf"][:, :])
                        S.inc("T", i)
                    ttok = S.tok("T")

                    @blk.scalar
                    def _(s, tt=tt, hf=hf, bank=bank, ttok=ttok):
                        S.wait(s, "T", ttok)
                        i = s.copy(
                            xfm[par][:, hf * 4:(hf + 1) * 4, tt * 128:(tt + 1) * 128],
                            bank[:, :].rearrange("p (c k) -> p c k", c=4))
                        S.inc("Sc", i)
                    ev.append(S.tok("Sc"))
            xfm_tok = S.tok("Sc")
            x_T_tok = S.tok("T")

            @blk.scalar
            def _(s):
                if g >= 1:
                    i = s.copy(kmj[par][:, :, 0:1], kmj[1 - par][:, :, TGD:TGD + 1])
                    S.inc("Sc", i)
            halo_tok = S.tok("Sc")

            for tt in range(2):
                bank = px[tt]

                @blk.tensor
                def _(t, tt=tt, bank=bank):
                    S.wait(t, "V", l2n_tok)
                    for h in range(HH):
                        i = t.transpose(bank[:, h * 128:(h + 1) * 128],
                                        l2tmp[:, tt, h * 128:(h + 1) * 128],
                                        C["identb"][:, :])
                    S.inc("T", i)
                kt = S.tok("T")

                @blk.scalar
                def _(s, tt=tt, bank=bank, kt=kt):
                    S.wait(s, "T", kt)
                    i = s.copy(kmj[par][:, :, 1 + tt * 128:1 + (tt + 1) * 128],
                               bank[:, :].rearrange("p (c k) -> p c k", c=HH))
                    S.inc("Sc", i)
            kmj_tok = S.tok("Sc")
            l2_T_tok = S.tok("T")

            @blk.vector
            def _(v):
                if g >= 1:
                    i = v.tensor_copy(vpre[par][:, :, 0:3],
                                      vpre[1 - par][:, :, TGD:TGD + 3])
                    S.inc("V", i)
            vhalo_tok = S.tok("V")

            vev = []
            for dv in range(DCN):
                bank = pb[2 + (dv % 2)]

                @blk.tensor
                def _(t, dv=dv, bank=bank):
                    S.wait(t, "Sc", xfm_tok)
                    if len(vev) >= 2:
                        S.wait(t, "V", vev[-2])
                    for dc in range(DCN):
                        i = t.matmul(bank[:, 0:TGD],
                                     wvT_s[:, dc, dv * 128:(dv + 1) * 128],
                                     xfm[par][:, dc, :],
                                     start=(dc == 0), stop=(dc == DCN - 1))
                    S.inc("T", i)
                vt = S.tok("T")

                @blk.vector
                def _(v, dv=dv, bank=bank, vt=vt):
                    S.wait(v, "T", vt)
                    S.wait(v, "V", vhalo_tok)
                    i = v.tensor_copy(vpre[par][:, dv, 3:3 + TGD], bank[:, 0:TGD])
                    S.inc("V", i)
                vev.append(S.tok("V"))
            vproj_tok = S.tok("V")

            gev = []
            for dv in range(DCN):
                bank = pb[2 + (dv % 2)]

                @blk.tensor
                def _(t, dv=dv, bank=bank):
                    S.wait(t, "V", vev[dv])
                    if len(gev) >= 2:
                        S.wait(t, "Sc", gev[-2])
                    for dc in range(DCN):
                        i = t.matmul(bank[:, 256:512],
                                     wgT_s[:, dc, dv * 128:(dv + 1) * 128],
                                     xfm[par][:, dc, :],
                                     start=(dc == 0), stop=(dc == DCN - 1))
                    S.inc("T", i)
                gt_ = S.tok("T")

                @blk.scalar
                def _(s, dv=dv, bank=bank, gt_=gt_):
                    S.wait(s, "T", gt_)
                    i = s.activation(gfm[:, dv, :], bank[:, 256:512], AF.Silu)
                    S.inc("Sc", i)
                gev.append(S.tok("Sc"))
            gact_tok = S.tok("Sc")

            @blk.vector
            def _(v):
                S.wait(v, "Sc", gact_tok)
                i = v.tensor_scalar_mul(gfm[:, :, :], gfm[:, :, :], C["normw"][:, :])
                S.inc("V", i)
            gnorm_tok = S.tok("V")

            @blk.tensor
            def _(t):
                S.wait(t, "Sc", xfm_tok)
                for dc in range(DCN):
                    t.matmul(pb[4][0:HH, 0:TGD], C["wbT"][:, dc, :], xfm[par][:, dc, :],
                             start=(dc == 0), stop=(dc == DCN - 1))
                for dc in range(DCN):
                    i = t.matmul(pb[4][0:HH, 256:512], C["waT"][:, dc, :],
                                 xfm[par][:, dc, :],
                                 start=(dc == 0), stop=(dc == DCN - 1))
                S.inc("T", i)
            bdz_tok = S.tok("T")

            @blk.scalar
            def _(s):
                S.wait(s, "T", bdz_tok)
                i = s.activation(bdr[:, 0, :], pb[4][0:HH, 0:TGD], AF.Exp, scale=-1.0)
                S.inc("Sc", i)
                i = s.activation(bdr[:, 1, :], pb[4][0:HH, 256:512], AF.Exp,
                                 bias=C["dtb"][:, :], scale=1.0)
                S.inc("Sc", i)
                S.wait(s, "Sc")
                i = s.activation(bdr[:, 1, :], bdr[:, 1, :], AF.Ln, bias=C['fc'][0:HH, 1:2], scale=1.0)
                S.inc("Sc", i)
            bdact_tok = S.tok("Sc")

            @blk.vector
            def _(v):
                S.wait(v, "Sc", bdact_tok)
                i = v.tensor_scalar_add(bdr[:, 0, :], bdr[:, 0, :], 1.0)
                S.inc("V", i)
                S.wait(v, "V")
                i = v.reciprocal(bdr[:, 0, :], bdr[:, 0, :])
                S.inc("V", i)
                i = v.tensor_scalar_mul(bdr[:, 1, :], bdr[:, 1, :], C["nea"][:, :])
                S.inc("V", i)
            bdrows_tok = S.tok("V")

            @blk.tensor
            def _(t):
                S.wait(t, "V", bdrows_tok)
                for cl in range(CPG):
                    t.transpose(pb[5][:, cl * 128:cl * 128 + 8],
                                bdr[:, 0, cl * 128:(cl + 1) * 128],
                                C["identf"][0:HH, 0:HH])
                    i = t.transpose(pb[5][:, 256 + cl * 128:256 + cl * 128 + 8],
                                    bdr[:, 1, cl * 128:(cl + 1) * 128],
                                    C["identf"][0:HH, 0:HH])
                S.inc("T", i)
            bdtr_tok = S.tok("T")

            @blk.vector
            def _(v):
                S.wait(v, "T", bdtr_tok)
                for cl in range(CPG):
                    v.tensor_copy(beta_cm[:, cl * HH:(cl + 1) * HH],
                                  pb[5][:, cl * 128:cl * 128 + 8])
                    i = v.tensor_copy(dec_cm[:, cl * HH:(cl + 1) * HH],
                                      pb[5][:, 256 + cl * 128:256 + cl * 128 + 8])
                S.inc("V", i)
            bdcm_tok = S.tok("V")

            @blk.tensor
            def _(t):
                S.wait(t, "V", bdcm_tok)
                i = t.matmul(pb[5][:, 0:2 * HH], C["triuF"][:, :], dec_cm[:, :],
                             start=True, stop=True)
                S.inc("T", i)
            cum_tok = S.tok("T")

            @blk.vector
            def _(v):
                S.wait(v, "T", cum_tok)
                i = v.tensor_copy(dec_cum[:, :], pb[5][:, 0:2 * HH])
                S.inc("V", i)
            deccum_tok = S.tok("V")

            @blk.tensor
            def _(t):
                S.wait(t, "V", deccum_tok)
                i = t.matmul(pb[5][0:1, 64:64 + 2 * HH], C["e127"][:, :], dec_cum[:, :],
                             start=True, stop=True)
                S.inc("T", i)
            dlast_tok = S.tok("T")

            @blk.scalar
            def _(s):
                S.wait(s, "T", dlast_tok)
                i = s.copy(dlast[:, :], pb[5][0:1, 64:64 + 2 * HH])
                S.inc("Sc", i)
                i = s.activation(elsr[:, :], pb[5][0:1, 64:64 + 2 * HH], AF.Exp)
                S.inc("Sc", i)
            dlrow_tok = S.tok("Sc")

            @blk.tensor
            def _(t):
                S.wait(t, "Sc", dlrow_tok)
                t.matmul(pb[5][:, 128:128 + 2 * HH], C["ones_row"][:, :], elsr[:, :],
                         start=True, stop=True)
                i = t.matmul(pb[5][:, 192:192 + 2 * HH], C["ones_row"][:, :],
                             dlast[:, :], start=True, stop=True)
                S.inc("T", i)
            dbc_tok = S.tok("T")

            @blk.vector
            def _(v):
                S.wait(v, "T", dbc_tok)
                i = v.tensor_copy(esl[:, :], pb[5][:, 128:128 + 2 * HH])
                S.inc("V", i)
                i = v.tensor_sub(dwv[:, :], pb[5][:, 192:192 + 2 * HH], dec_cum[:, :])
                S.inc("V", i)
            dwsub_tok = S.tok("V")

            @blk.scalar
            def _(s):
                S.wait(s, "V", dwsub_tok)
                i = s.activation(dwv[:, :], dwv[:, :], AF.Exp)
                S.inc("Sc", i)
                i = s.activation(edec[:, :], dec_cum[:, :], AF.Exp)
                S.inc("Sc", i)
            dfam_sc = S.tok("Sc")

            @blk.vector
            def _(v):
                S.wait(v, "Sc", dfam_sc)
                i = v.tensor_scalar_mul(edk[:, :], edec[:, :], RSK)
                S.inc("V", i)
                i = v.tensor_mul(bev[:, :], beta_cm[:, :], edec[:, :])
                S.inc("V", i)
            dfam_v = S.tok("V")

            csl = []
            for dc in range(DCN):
                @blk.vector
                def _(v, dc=dc):
                    S.wait(v, "V", vproj_tok)
                    if len(csl) >= 2:
                        S.wait(v, "Sc", csl[-2])
                    buf = cvt[dc % 2]
                    i = v.tensor_scalar(buf[:, :], vpre[par][:, dc, 0:TGD],
                                        C["convw"][:, dc, 0:1], None, op0=ALU.mult)
                    S.inc("V", i)
                    for j in range(1, 4):
                        S.wait(v, "V")
                        i = v.scalar_tensor_tensor(
                            buf[:, :], vpre[par][:, dc, j:j + TGD],
                            C["convw"][:, dc, j:j + 1], buf[:, :],
                            op0=ALU.mult, op1=ALU.add)
                        S.inc("V", i)
                cvtok = S.tok("V")

                @blk.scalar
                def _(s, dc=dc, cvtok=cvtok):
                    S.wait(s, "V", cvtok)
                    i = s.activation(vfm[:, dc, :], cvt[dc % 2][:, :], AF.Silu,
                                     bias=C["convb"][:, dc:dc + 1])
                    S.inc("Sc", i)
                csl.append(S.tok("Sc"))
            vsilu_tok = S.tok("Sc")

            tmev = []
            for kind in range(3):
                for tt in range(2):
                    bank = px[tt]

                    @blk.tensor
                    def _(t, kind=kind, tt=tt, bank=bank):
                        if kind == 0:
                            S.wait(t, "Sc", vsilu_tok)
                        elif kind == 1:
                            S.wait(t, "V", gnorm_tok)
                        else:
                            S.wait(t, "Sc", max(kmj_tok, halo_tok))
                        if len(tmev) >= 2:
                            S.wait(t, "V", tmev[-2])
                        for h in range(HH):
                            if kind == 0:
                                src = vfm[:, h, tt * 128:(tt + 1) * 128]
                            elif kind == 1:
                                src = gfm[:, h, tt * 128:(tt + 1) * 128]
                            else:
                                src = kmj[par][:, h, tt * 128:tt * 128 + 128]
                            i = t.transpose(bank[:, h * 128:(h + 1) * 128], src,
                                            C["identb"][:, :])
                        S.inc("T", i)
                    trt = S.tok("T")

                    @blk.vector
                    def _(v, kind=kind, tt=tt, bank=bank, trt=trt):
                        S.wait(v, "T", trt)
                        dst = [vtm, gtm, wktm][kind]
                        i = v.tensor_copy(
                            dst[par][:, tt, :, :],
                            bank[:, :].rearrange("p (c k) -> p c k", c=HH))
                        S.inc("V", i)
                    tmev.append(S.tok("V"))
            tm_tok = S.tok("V")

            pre = []
            for h in range(HH):
                cols = slice(h, 2 * HH, HH)
                startV = pre[-1]["endV"] if pre else max(tm_tok, dfam_v)
                startSc = pre[-1]["endSc"] if pre else S.tok("Sc")

                @blk.tensor
                def _(t, h=h, startV=startV, startSc=startSc):
                    S.wait(t, "V", max(startV, tm_tok, dfam_v))
                    S.wait(t, "Sc", startSc)
                    for cl in range(CPG):
                        i = t.transpose(
                            pb[3][:, cl * 128:(cl + 1) * 128],
                            dec_cum[:, cl * HH + h:cl * HH + h + 1].broadcast_to(
                                [128, 128]),
                            C["identf"][:, :])
                    S.inc("T", i)
                mtr_tok = S.tok("T")

                @blk.vector
                def _(v, cols=cols, mtr_tok=mtr_tok):
                    S.wait(v, "T", mtr_tok)
                    i = v.tensor_tensor(mt1[:, :, :], bc2(dec_cum[:, cols], CPG),
                                        pb[3][:, 0:256].rearrange(
                                            "p (c k) -> p c k", c=CPG),
                                        op=ALU.subtract)
                    S.inc("V", i)
                    S.wait(v, "V")
                    i = v.tensor_mul(mt2[:, :, :], mt1[:, :, :], mbc(C["ntriP"], CPG))
                    S.inc("V", i)
                    i = v.tensor_mul(mt1[:, :, :], mt1[:, :, :], mbc(C["trilP"], CPG))
                    S.inc("V", i)
                mmul_tok = S.tok("V")

                @blk.scalar
                def _(s, mmul_tok=mmul_tok):
                    S.wait(s, "V", mmul_tok)
                    i = s.activation(mt2[:, :, :], mt2[:, :, :], AF.Exp)
                    S.inc("Sc", i)
                    i = s.activation(mt1[:, :, :], mt1[:, :, :], AF.Exp)
                    S.inc("Sc", i)
                mexp_tok = S.tok("Sc")

                @blk.vector
                def _(v, cols=cols, mexp_tok=mexp_tok):
                    S.wait(v, "Sc", mexp_tok)
                    i = v.tensor_mul(ltk[:, :, :], mt2[:, :, :], mbc(C["triK"], CPG))
                    S.inc("V", i)
                    i = v.tensor_mul(lsnB[:, :, :], mt1[:, :, :], mbc(C["strN"], CPG))
                    S.inc("V", i)
                    S.wait(v, "V")
                    i = v.tensor_tensor(lsnB[:, :, :], lsnB[:, :, :],
                                        bc2(beta_cm[:, cols], CPG), op=ALU.mult)
                    S.inc("V", i)
                mask_V = S.tok("V")

                @blk.tensor
                def _(t, h=h):
                    S.wait(t, "Sc", bdact_tok)
                    for cl in range(CPG):
                        wkk = kmj[par][:, h, cl * 128:cl * 128 + 128]
                        i = t.matmul(pb[4][:, cl * 128:(cl + 1) * 128], wkk, wkk,
                                     start=True, stop=True)
                    S.inc("T", i)
                raw_tok = S.tok("T")

                @blk.vector
                def _(v, raw_tok=raw_tok, mask_V=mask_V):
                    S.wait(v, "T", raw_tok)
                    S.wait(v, "V", mask_V)
                    i = v.tensor_tensor(abuf[:, :, :],
                                        pb[4][:, 0:256].rearrange(
                                            "p (c k) -> p c k", c=CPG),
                                        lsnB[:, :, :], op=ALU.mult)
                    S.inc("V", i)
                A_V = S.tok("V")

                @blk.tensor
                def _(t, A_V=A_V):
                    S.wait(t, "V", A_V)
                    for cl in range(CPG):
                        i = t.transpose(px[0][:, cl * 128:(cl + 1) * 128],
                                        abuf[:, cl, :], C["identb"][:, :])
                    S.inc("T", i)
                at_tok = S.tok("T")

                @blk.scalar
                def _(s, at_tok=at_tok):
                    S.wait(s, "T", at_tok)
                    i = s.copy(atbuf[:, :, :],
                               px[0][:, 0:256].rearrange("p (c k) -> p c k", c=CPG))
                    S.inc("Sc", i)
                atev_tok = S.tok("Sc")

                @blk.vector
                def _(v, atev_tok=atev_tok):
                    S.wait(v, "Sc", atev_tok)
                    i = v.tensor_tensor(gch[:, :, :], atbuf[:, :, :],
                                        mbc(C["identb"], CPG), op=ALU.add)
                    S.inc("V", i)
                g_tok = S.tok("V")

                cur, curT, cur_tok = abuf, atbuf, atev_tok
                for lvl in range(1, 7):
                    @blk.tensor
                    def _(t, cur=cur, curT=curT, lvl=lvl, cur_tok=cur_tok,
                          g_tok=g_tok):
                        S.wait(t, "Sc", cur_tok)
                        S.wait(t, "V", g_tok)
                        for cl in range(CPG):
                            i = t.matmul(pb[4][:, 256 + cl * 128:256 + (cl + 1) * 128],
                                         curT[:, cl, :], cur[:, cl, :],
                                         start=True, stop=True)
                        if lvl < 6:
                            for cl in range(CPG):
                                i = t.matmul(pb[4][:, cl * 128:(cl + 1) * 128],
                                             cur[:, cl, :], curT[:, cl, :],
                                             start=True, stop=True)
                        S.inc("T", i)
                    pw_tok = S.tok("T")

                    @blk.scalar
                    def _(s, lvl=lvl, pw_tok=pw_tok):
                        S.wait(s, "T", pw_tok)
                        i = s.copy(pw[lvl % 2][:, :, :],
                                   pb[4][:, 256:512].rearrange(
                                       "p (c k) -> p c k", c=CPG))
                        S.inc("Sc", i)
                        if lvl < 6:
                            i = s.copy(pwT[lvl % 2][:, :, :],
                                       pb[4][:, 0:256].rearrange(
                                           "p (c k) -> p c k", c=CPG))
                            S.inc("Sc", i)
                    pwev_tok = S.tok("Sc")

                    @blk.tensor
                    def _(t, lvl=lvl, pwev_tok=pwev_tok):
                        S.wait(t, "Sc", pwev_tok)
                        for cl in range(CPG):
                            i = t.matmul(pb[5][:, 256 + cl * 128:256 + (cl + 1) * 128],
                                         pw[lvl % 2][:, cl, :], gch[:, cl, :],
                                         start=True, stop=True)
                        S.inc("T", i)
                    gm_tok = S.tok("T")

                    @blk.vector
                    def _(v, gm_tok=gm_tok):
                        S.wait(v, "T", gm_tok)
                        i = v.tensor_tensor(
                            gch[:, :, :], gch[:, :, :],
                            pb[5][:, 256:512].rearrange("p (c k) -> p c k", c=CPG),
                            op=ALU.add)
                        S.inc("V", i)
                    g_tok = S.tok("V")
                    cur, curT, cur_tok = pw[lvl % 2], pwT[lvl % 2], pwev_tok

                @blk.vector
                def _(v, h=h, cols=cols, g_tok=g_tok):
                    S.wait(v, "V", g_tok)
                    i = v.tensor_tensor(vbb[:, :, :], vtm[par][:, :, h, :],
                                        bc2(beta_cm[:, cols], CPG), op=ALU.mult)
                    S.inc("V", i)
                    i = v.tensor_tensor(wkbe[:, :, :], wktm[par][:, :, h, :],
                                        bc2(bev[:, cols], CPG), op=ALU.mult)
                    S.inc("V", i)
                vbw_tok = S.tok("V")

                @blk.tensor
                def _(t, vbw_tok=vbw_tok):
                    S.wait(t, "V", vbw_tok)
                    for cl in range(CPG):
                        t.matmul(pb[4][:, cl * 128:(cl + 1) * 128],
                                 gch[:, cl, :], vbb[:, cl, :], start=True, stop=True)
                    for cl in range(CPG):
                        i = t.matmul(pb[4][:, 256 + cl * 128:256 + (cl + 1) * 128],
                                     gch[:, cl, :], wkbe[:, cl, :],
                                     start=True, stop=True)
                    S.inc("T", i)
                ap_tok = S.tok("T")

                @blk.scalar
                def _(s, h=h, ap_tok=ap_tok):
                    S.wait(s, "T", ap_tok)
                    i = s.copy(vpr[:, h, :, :],
                               pb[4][:, 0:256].rearrange("p (c k) -> p c k", c=CPG))
                    S.inc("Sc", i)
                    i = s.copy(wkc[:, :, :],
                               pb[4][:, 256:512].rearrange("p (c k) -> p c k", c=CPG))
                    S.inc("Sc", i)
                apev_tok = S.tok("Sc")

                @blk.tensor
                def _(t, apev_tok=apev_tok):
                    S.wait(t, "Sc", apev_tok)
                    for cl in range(CPG):
                        i = t.transpose(px[0][:, 256 + cl * 128:256 + (cl + 1) * 128],
                                        wkc[:, cl, :], C["identb"][:, :])
                    S.inc("T", i)
                wt_tok = S.tok("T")

                @blk.scalar
                def _(s, h=h, wt_tok=wt_tok):
                    S.wait(s, "T", wt_tok)
                    i = s.copy(wkcT[:, h, :, :],
                               px[0][:, 256:512].rearrange("p (c k) -> p c k", c=CPG))
                    S.inc("Sc", i)
                wtev_tok = S.tok("Sc")

                @blk.tensor
                def _(t, h=h):
                    for cl in range(CPG):
                        wkk = kmj[par][:, h, cl * 128:cl * 128 + 128]
                        rkk = kmj[par][:, h, cl * 128 + 1:cl * 128 + 129]
                        i = t.matmul(pb[3][:, cl * 128:(cl + 1) * 128], wkk, rkk,
                                     start=True, stop=True)
                    S.inc("T", i)
                att_tok = S.tok("T")

                @blk.vector
                def _(v, h=h, att_tok=att_tok, mask_V=mask_V):
                    S.wait(v, "T", att_tok)
                    S.wait(v, "V", mask_V)
                    i = v.tensor_tensor(atn[:, h, :, :],
                                        pb[3][:, 0:256].rearrange(
                                            "p (c k) -> p c k", c=CPG),
                                        ltk[:, :, :], op=ALU.mult)
                    S.inc("V", i)
                atn_tok = S.tok("V")

                pre.append(dict(endV=S.tok("V"), endSc=S.tok("Sc"),
                                apev=apev_tok, wkcT=wtev_tok, atn=atn_tok))

            pre_V = S.tok("V")
            pre_Sc = S.tok("Sc")

            last_V = pre_V
            last_Sc = pre_Sc
            for cl in range(CPG):
                for h in range(HH):
                    col = cl * HH + h

                    @blk.tensor
                    def _(t, h=h, cl=cl, last_V=last_V, last_Sc=last_Sc):
                        S.wait(t, "Sc", max(pre[h]["wkcT"], last_Sc))
                        S.wait(t, "V", max(pre[h]["atn"], last_V))
                        i = t.matmul(pb[5][:, 0:128], wkcT[:, h, cl, :], S16[:, h, :],
                                     start=True, stop=True)
                        S.inc("T", i)
                    vn_tok = S.tok("T")

                    @blk.vector
                    def _(v, h=h, cl=cl, vn_tok=vn_tok):
                        S.wait(v, "T", vn_tok)
                        i = v.tensor_tensor(vnew[:, :], vpr[:, h, cl, :],
                                            pb[5][:, 0:128], op=ALU.subtract)
                        S.inc("V", i)
                    vnew_tok = S.tok("V")

                    @blk.scalar
                    def _(s, h=h, cl=cl, col=col, vnew_tok=vnew_tok):
                        S.wait(s, "V", vnew_tok)
                        i = s.activation(wkdw[:, :], wktm[par][:, cl, h, :], AF.Copy,
                                         scale=dwv[:, col:col + 1])
                        S.inc("Sc", i)
                    wkdw_tok = S.tok("Sc")

                    @blk.tensor
                    def _(t, h=h, cl=cl, vnew_tok=vnew_tok, wkdw_tok=wkdw_tok):
                        S.wait(t, "V", vnew_tok)
                        S.wait(t, "Sc", wkdw_tok)
                        rkk = kmj[par][:, h, cl * 128 + 1:cl * 128 + 129]
                        t.matmul(pb[5][:, 128:256], rkk, S16[:, h, :],
                                 start=True, stop=True)
                        t.matmul(pb[5][:, 256:384], atn[:, h, cl, :], vnew[:, :],
                                 start=True, stop=True)
                        i = t.matmul(pb[5][:, 384:512], wkdw[:, :], vnew[:, :],
                                     start=True, stop=True)
                        S.inc("T", i)
                    smm_tok = S.tok("T")

                    @blk.scalar
                    def _(s, col=col, smm_tok=smm_tok):
                        S.wait(s, "T", smm_tok)
                        i = s.activation(ogt[:, 0:128], pb[5][:, 128:256], AF.Copy,
                                         scale=edk[:, col:col + 1])
                        S.inc("Sc", i)
                    o1_tok = S.tok("Sc")

                    @blk.vector
                    def _(v, h=h, cl=cl, col=col, o1_tok=o1_tok):
                        S.wait(v, "Sc", o1_tok)
                        i = v.tensor_tensor(o_grp[:, h, cl * 128:(cl + 1) * 128],
                                            ogt[:, 0:128], pb[5][:, 256:384],
                                            op=ALU.add)
                        S.inc("V", i)
                        i = v.scalar_tensor_tensor(
                            S_s[:, h, :], S_s[:, h, :], esl[:, col:col + 1],
                            pb[5][:, 384:512], op0=ALU.mult, op1=ALU.add)
                        S.inc("V", i)
                        S.wait(v, "V")
                        i = v.tensor_copy(S16[:, h, :], S_s[:, h, :])
                        S.inc("V", i)
                    last_V = S.tok("V")
                    last_Sc = S.tok("Sc")
            scan_V = last_V
            scan_Sc = last_Sc

            @blk.vector
            def _(v):
                S.wait(v, "V", scan_V)
                S.wait(v, "Sc", scan_Sc)
                for h in range(HH):
                    S.wait(v, "V")
                    i = v.tensor_mul(ogt[:, :], o_grp[:, h, :], o_grp[:, h, :])
                    S.inc("V", i)
                    S.wait(v, "V")
                    i = v.reduce_sum(rrs[:, h * 2:(h + 1) * 2],
                                     ogt[:, :].rearrange("p (c k) -> p c k", c=2),
                                     AX.X)
                    S.inc("V", i)
            rr_tok = S.tok("V")

            @blk.scalar
            def _(s):
                S.wait(s, "V", rr_tok)
                i = s.activation(rrs[:, :], rrs[:, :], AF.Ln, scale=1.0 / 128.0,
                                 bias=C['fc'][:, 2:3])
                S.inc("Sc", i)
                S.wait(s, "Sc")
                i = s.activation(rrs[:, :], rrs[:, :], AF.Exp, scale=-0.5)
                S.inc("Sc", i)
            rstd_tok = S.tok("Sc")

            half = g // (NGD // 2)
            tt0 = (2 * g) % NTT
            for h in range(HH):
                @blk.vector
                def _(v, h=h):
                    S.wait(v, "Sc", rstd_tok)
                    S.wait(v, "V")
                    i = v.tensor_tensor(
                        o_grp[:, h, :].rearrange("p (c k) -> p c k", c=2),
                        o_grp[:, h, :].rearrange("p (c k) -> p c k", c=2),
                        gtm[par][:, :, h, :], op=ALU.mult)
                    S.inc("V", i)
                    S.wait(v, "V")
                    i = v.tensor_tensor(
                        o_grp[:, h, :].rearrange("p (c k) -> p c k", c=2),
                        o_grp[:, h, :].rearrange("p (c k) -> p c k", c=2),
                        bc2(rrs[:, h * 2:(h + 1) * 2], 2), op=ALU.mult)
                    S.inc("V", i)
                    S.wait(v, "V")
                    i = v.scalar_tensor_tensor(
                        o_mine[:, h, tt0:tt0 + 2, :],
                        o_grp[:, h, :].rearrange("p (c k) -> p c k", c=2),
                        C["msel"][:, half:half + 1],
                        o_mine[:, h, tt0:tt0 + 2, :],
                        op0=ALU.mult, op1=ALU.add)
                    S.inc("V", i)

            prev = dict(x_V=l2n_tok, x_T=x_T_tok,
                        end_T=S.tok("T"), end_V=S.tok("V"), end_Sc=S.tok("Sc"))



def _p2(nc, blk, S, C, o_mine, out_acc, xn_s, wdT_s, pb, px, P, xh_d, woT_d):
    HT, NGL, TGL = P["HT"], P["NGL"], P["TGL"]
    with ExitStack() as ctx:
        sb = lambda name, shape, dt=F32: ctx.enter_context(nc.sbuf_tensor(name, shape, dt))
        woT_s = sb("woT_s", [128, DCN, D], BF16)
        xh_tm = sb("xh_tm", [128, 2, D])
        ofm = sb("ofm", [128, DCN, TGL], BF16)
        res1 = sb("res1", [128, DCN, TGL])
        sums = sb("sums", [1, 2 * TGL])
        stat = sb("stat", [1, 3 * TGL])
        lgr = sb("lgr", [8, TGL])
        lgt = sb("lgt", [128, 8])
        prb = sb("prb", [128, 8])
        srt8 = sb("srt8", [128, 8])
        mxb = sb("mxb", [128, 1])
        smr = sb("smr", [128, 1])
        sqv = sb("sqv", [128, TGL])

        @blk.gpsimd
        def _(g):
            i = g.dma_start(woT_s[:, :, :], woT_d[:, :, :])
            S.inc("dWO", i, 16)
        wo_tok = S.tok("dWO")

        prevg = {}
        for gl in range(NGL):
            @blk.sync
            def _(sy):
                if prevg:
                    S.wait(sy, "T", prevg["xh_T"])
                i = sy.dma_start(
                    xh_tm[:, :, :],
                    xh_d[gl * TGL:(gl + 1) * TGL, :].rearrange(
                        "(tt p) d -> p tt d", p=128))
                S.inc("dX", i, 16)
            xh_dma = S.tok("dX")

            for deco in (blk.tensor, blk.vector, blk.scalar):
                def f(eng):
                    S.wait(eng, "dX", xh_dma)
                    S.wait(eng, "dWO", wo_tok)
                    if prevg:
                        S.wait(eng, "T", prevg["end_T"])
                        S.wait(eng, "V", prevg["end_V"])
                        S.wait(eng, "Sc", prevg["end_Sc"])
                deco(f)

            tt0 = gl * (TGL // 128)
            for tt in range(2):
                bank = px[tt]

                @blk.tensor
                def _(t, tt=tt, bank=bank):
                    for h in range(HH):
                        i = t.transpose(bank[:, h * 128:(h + 1) * 128],
                                        o_mine[:, h, tt0 + tt, :], C["identb"][:, :])
                    S.inc("T", i)
                ot_ = S.tok("T")

                @blk.scalar
                def _(s, tt=tt, bank=bank, ot_=ot_):
                    S.wait(s, "T", ot_)
                    i = s.copy(ofm[:, :, tt * 128:(tt + 1) * 128],
                               bank[:, :].rearrange("p (c k) -> p c k", c=HH))
                    S.inc("Sc", i)
            ofm_tok = S.tok("Sc")

            rev = []
            for dv in range(DCN):
                bank = pb[dv % 2]
                bank2 = pb[2 + (dv % 2)]

                @blk.tensor
                def _(t, dv=dv, bank=bank, bank2=bank2):
                    S.wait(t, "Sc", ofm_tok)
                    if len(rev) >= 2:
                        S.wait(t, "V", rev[-2])
                    for tt in range(2):
                        t.transpose(bank2[:, tt * 128:(tt + 1) * 128],
                                    xh_tm[:, tt, dv * 128:(dv + 1) * 128],
                                    C["identf"][:, :])
                    for dc in range(DCN):
                        i = t.matmul(bank[:, 0:TGL],
                                     woT_s[:, dc, dv * 128:(dv + 1) * 128],
                                     ofm[:, dc, :],
                                     start=(dc == 0), stop=(dc == DCN - 1))
                    S.inc("T", i)
                rt = S.tok("T")

                @blk.scalar
                def _(s, dv=dv, bank2=bank2, rt=rt):
                    S.wait(s, "T", rt)
                    i = s.copy(res1[:, dv, :], bank2[:, 0:TGL])
                    S.inc("Sc", i)
                sev = S.tok("Sc")

                @blk.vector
                def _(v, dv=dv, bank=bank, sev=sev):
                    S.wait(v, "Sc", sev)
                    i = v.tensor_tensor(res1[:, dv, :], res1[:, dv, :], bank[:, 0:TGL],
                                        op=ALU.add)
                    S.inc("V", i)
                rev.append(S.tok("V"))
            res1_tok = S.tok("V")
            xh_T_tok = S.tok("T")

            @blk.tensor
            def _(t):
                S.wait(t, "V", res1_tok)
                for dc in range(DCN):
                    i = t.matmul(pb[4][0:1, 0:TGL], C["ones_colf"][:, :],
                                 res1[:, dc, :],
                                 start=(dc == 0), stop=(dc == DCN - 1))
                S.inc("T", i)
            sum_tok = S.tok("T")

            sq_T = sum_tok
            for dc in range(DCN):
                @blk.vector
                def _(v, dc=dc, sq_T=sq_T):
                    S.wait(v, "V", res1_tok)
                    S.wait(v, "T", sq_T)
                    i = v.tensor_mul(sqv[:, :], res1[:, dc, :], res1[:, dc, :])
                    S.inc("V", i)
                vq = S.tok("V")

                @blk.tensor
                def _(t, dc=dc, vq=vq):
                    S.wait(t, "V", vq)
                    i = t.matmul(pb[5][0:1, 0:TGL], C["ones_colf"][:, :], sqv[:, :],
                                 start=(dc == 0), stop=(dc == DCN - 1))
                    S.inc("T", i)
                sq_T = S.tok("T")
            sumsq_tok = sq_T

            @blk.scalar
            def _(s):
                S.wait(s, "T", sumsq_tok)
                i = s.mul(sums[:, 0:TGL], pb[4][0:1, 0:TGL], 1.0 / D)
                S.inc("Sc", i)
                i = s.mul(sums[:, TGL:2 * TGL], pb[5][0:1, 0:TGL], 1.0 / D)
                S.inc("Sc", i)
            mv_tok = S.tok("Sc")

            @blk.vector
            def _(v):
                S.wait(v, "Sc", mv_tok)
                i = v.tensor_mul(stat[:, 0:TGL], sums[:, 0:TGL], sums[:, 0:TGL])
                S.inc("V", i)
                S.wait(v, "V")
                i = v.tensor_sub(stat[:, 0:TGL], sums[:, TGL:2 * TGL], stat[:, 0:TGL])
                S.inc("V", i)
            var_tok = S.tok("V")

            @blk.scalar
            def _(s):
                S.wait(s, "V", var_tok)
                i = s.activation(stat[:, TGL:2 * TGL], stat[:, 0:TGL], AF.Ln,
                                 bias=C['fc'][0:1, 3:4])
                S.inc("Sc", i)
                S.wait(s, "Sc")
                i = s.activation(stat[:, TGL:2 * TGL], stat[:, TGL:2 * TGL], AF.Exp,
                                 scale=-0.5)
                S.inc("Sc", i)
            rstd_tok = S.tok("Sc")

            @blk.vector
            def _(v):
                S.wait(v, "Sc", rstd_tok)
                i = v.tensor_mul(stat[:, 2 * TGL:3 * TGL], sums[:, 0:TGL],
                                 stat[:, TGL:2 * TGL])
                S.inc("V", i)
            mr_tok = S.tok("V")

            @blk.tensor
            def _(t):
                S.wait(t, "V", mr_tok)
                t.matmul(pb[5][:, 0:TGL], C["ones_row"][:, :], stat[:, TGL:2 * TGL],
                         start=True, stop=True)
                i = t.matmul(pb[4][:, 0:TGL], C["ones_row"][:, :],
                             stat[:, 2 * TGL:3 * TGL], start=True, stop=True)
                S.inc("T", i)
            bcast_tok = S.tok("T")

            for dc in range(DCN):
                @blk.vector
                def _(v, dc=dc):
                    S.wait(v, "T", bcast_tok)
                    i = v.tensor_tensor(res1[:, dc, :], res1[:, dc, :],
                                        pb[5][:, 0:TGL], op=ALU.mult)
                    S.inc("V", i)
                    S.wait(v, "V")
                    i = v.tensor_tensor(res1[:, dc, :], res1[:, dc, :],
                                        pb[4][:, 0:TGL], op=ALU.subtract)
                    S.inc("V", i)
                    S.wait(v, "V")
                    i = v.tensor_scalar_mul(
                        out_acc[:, dc, gl * TGL:(gl + 1) * TGL], res1[:, dc, :], 2.0)
                    S.inc("V", i)
                    i = v.tensor_scalar(
                        xn_s[:, dc, gl * TGL:(gl + 1) * TGL], res1[:, dc, :],
                        C["gam"][:, dc:dc + 1], C["betf"][:, dc:dc + 1],
                        op0=ALU.mult, op1=ALU.add)
                    S.inc("V", i)
            apply_tok = S.tok("V")

            @blk.tensor
            def _(t):
                S.wait(t, "V", apply_tok)
                for dc in range(DCN):
                    i = t.matmul(pb[4][0:E, 0:TGL], C["rwg"][:, dc, :],
                                 res1[:, dc, :],
                                 start=(dc == 0), stop=(dc == DCN - 1))
                S.inc("T", i)
            lg_tok = S.tok("T")

            @blk.vector
            def _(v):
                S.wait(v, "T", lg_tok)
                i = v.tensor_scalar_add(lgr[:, :], pb[4][0:E, 0:TGL],
                                        C["rbc"][:, :])
                S.inc("V", i)
            lgr_tok = S.tok("V")

            wtoks = []
            for tt in range(2):
                @blk.tensor
                def _(t, tt=tt, lgr_tok=lgr_tok):
                    S.wait(t, "V", lgr_tok)
                    if wtoks:
                        S.wait(t, "V", wtoks[-1])
                    i = t.transpose(pb[3][0:128, tt * 128:tt * 128 + 8],
                                    lgr[:, tt * 128:(tt + 1) * 128],
                                    C["identf"][0:E, 0:E])
                    S.inc("T", i)
                ltr = S.tok("T")

                @blk.vector
                def _(v, tt=tt, ltr=ltr):
                    S.wait(v, "T", ltr)
                    i = v.tensor_copy(lgt[:, :], pb[3][:, tt * 128:tt * 128 + 8])
                    S.inc("V", i)
                    S.wait(v, "V")
                    i = v.reduce_max(mxb[:, :], lgt[:, :], AX.X)
                    S.inc("V", i)
                    S.wait(v, "V")
                    i = v.tensor_scalar_mul(mxb[:, :], mxb[:, :], -1.0 / TEMP)
                    S.inc("V", i)
                mx_tok = S.tok("V")

                @blk.scalar
                def _(s, mx_tok=mx_tok):
                    S.wait(s, "V", mx_tok)
                    i = s.activation(prb[:, :], lgt[:, :], AF.Exp,
                                     scale=1.0 / TEMP, bias=mxb[:, :])
                    S.inc("Sc", i)
                ex_tok = S.tok("Sc")

                @blk.vector
                def _(v, ex_tok=ex_tok):
                    S.wait(v, "Sc", ex_tok)
                    i = v.reduce_sum(smr[:, :], prb[:, :], AX.X)
                    S.inc("V", i)
                    S.wait(v, "V")
                    i = v.reciprocal(smr[:, :], smr[:, :])
                    S.inc("V", i)
                    S.wait(v, "V")
                    i = v.tensor_scalar_mul(prb[:, :], prb[:, :], smr[:, :])
                    S.inc("V", i)
                    S.wait(v, "V")
                    i = v.max(out=srt8[:, :], in_=prb[:, :])
                    S.inc("V", i)
                    S.wait(v, "V")
                    i = v.tensor_scalar(srt8[:, :], prb[:, :], srt8[:, 1:2], None,
                                        op0=ALU.is_ge)
                    S.inc("V", i)
                    S.wait(v, "V")
                    i = v.tensor_mul(srt8[:, :], srt8[:, :], prb[:, :])
                    S.inc("V", i)
                wd_tok = S.tok("V")

                @blk.tensor
                def _(t, tt=tt, wd_tok=wd_tok):
                    S.wait(t, "V", wd_tok)
                    i = t.transpose(pb[3][0:8, 256 + tt * 128:256 + (tt + 1) * 128],
                                    srt8[:, :], C["identf"][:, :])
                    S.inc("T", i)
                wtr = S.tok("T")

                @blk.vector
                def _(v, tt=tt, wtr=wtr):
                    S.wait(v, "T", wtr)
                    i = v.tensor_copy(
                        wdT_s[:, gl * TGL + tt * 128:gl * TGL + (tt + 1) * 128],
                        pb[3][0:8, 256 + tt * 128:256 + (tt + 1) * 128])
                    S.inc("V", i)
                wtoks.append(S.tok("V"))

            prevg = dict(xh_T=xh_T_tok, end_T=S.tok("T"), end_V=S.tok("V"),
                         end_Sc=S.tok("Sc"))



def _p3(nc, blk, S, C, out_acc, xn_s, wdT_s, pb, P, w1_d, w2_d, dbg=None):
    HT, NGM, TGM = P["HT"], P["NGM"], P["TGM"]
    NE_ = E
    if P.get("p3lim"):
        NE_, NGM = P["p3lim"]
    NF = FF // 128
    with ExitStack() as ctx:
        sb = lambda name, shape, dt=F32: ctx.enter_context(nc.sbuf_tensor(name, shape, dt))
        w1_s = sb("w1s", [128, DCN, FF], BF16)
        w2_s = [sb(f"w2s{i}", [128, 2, D], BF16) for i in range(2)]
        hbuf = sb("hbuf", [128, 4, TGM], BF16)
        wbc = sb("wbc", [128, TGM])
        ytmp = sb("ytmp", [128, TGM])

        w1_tok = [[0] * 8 for _ in range(E)]
        w2_dma = [0, 0]
        w2_rd = [0, 0]
        ring_rd = [0, 0, 0, 0]
        last_y_read = 0

        @blk.gpsimd
        def _(g):
            for pc in range(8):
                i = g.dma_start(w1_s[:, :, pc * 512:(pc + 1) * 512],
                                w1_d[0, :, :, pc * 512:(pc + 1) * 512])
                w1_tok[0][pc] = S.inc(f"dW1p{pc}", i, 16)

        for e in range(NE_):
            for tg in range(NGM):
                t0 = tg * TGM

                @blk.tensor
                def _(t, e=e, t0=t0):
                    S.wait(t, "V", last_y_read)
                    i = t.matmul(pb[0][:, 256:256 + TGM],
                                 C["e8c"][:, e:e + 1].broadcast_to([8, 128]),
                                 wdT_s[:, t0:t0 + TGM], start=True, stop=True)
                    S.inc("T", i)
                wb_tok = S.tok("T")

                @blk.vector
                def _(v, wb_tok=wb_tok, ly=last_y_read):
                    S.wait(v, "T", wb_tok)
                    S.wait(v, "V", ly)
                    i = v.tensor_copy(wbc[:, :], pb[0][:, 256:256 + TGM])
                    S.inc("V", i)
                    for bk in range(1, 5):
                        i = v.memset(pb[bk][:, :], 0.0)
                        S.inc("V", i)
                wbc_tok = S.tok("V")

                mm1_ev = {}
                for f in range(NF):
                    slot = f % 4
                    @blk.tensor
                    def _(t, e=e, f=f, t0=t0, wbc_tok=wbc_tok):
                        S.wait(t, f"dW1p{f // 4}", w1_tok[e][f // 4])
                        if f >= 2:
                            S.wait(t, "Sc", mm1_ev[f - 2])
                        if f == 0:
                            S.wait(t, "V", last_y_read)
                            S.wait(t, "V", wbc_tok)
                        bank = pb[0] if f % 2 == 0 else pb[5]
                        for dc in range(DCN):
                            i = t.matmul(bank[:, 0:TGM],
                                         w1_s[:, dc, f * 128:(f + 1) * 128],
                                         xn_s[:, dc, t0:t0 + TGM],
                                         start=(dc == 0), stop=(dc == DCN - 1))
                        S.inc("T", i)
                    m1_tok = S.tok("T")

                    @blk.scalar
                    def _(s, f=f, slot=slot, m1_tok=m1_tok):
                        S.wait(s, "T", m1_tok)
                        S.wait(s, "T", ring_rd[slot])
                        bank = pb[0] if f % 2 == 0 else pb[5]
                        i = s.activation(hbuf[:, slot, :], bank[:, 0:TGM], AF.Silu)
                        S.inc("Sc", i)
                    mm1_ev[f] = S.tok("Sc")

                    if f % 2 == 0:
                        ws = (f // 2) % 2

                        @blk.gpsimd
                        def _(g, e=e, f=f, ws=ws):
                            S.wait(g, "T", w2_rd[ws])
                            i = g.dma_start(
                                w2_s[ws][:, :, :],
                                w2_d[e, f * 128:(f + 2) * 128, :].rearrange(
                                    "(c p) d -> p c d", p=128))
                            w2_dma[ws] = S.inc(f"dW2s{ws}", i, 16)

                    @blk.tensor
                    def _(t, e=e, f=f, slot=slot, wbc_tok=wbc_tok):
                        S.wait(t, "Sc", mm1_ev[f])
                        if f == 0:
                            S.wait(t, "V", wbc_tok)
                        S.wait(t, f"dW2s{(f // 2) % 2}", w2_dma[(f // 2) % 2])
                        wsl = w2_s[(f // 2) % 2]
                        for dc in range(DCN):
                            i = t.matmul(
                                pb[1 + dc // 2][:, (dc % 2) * 256:(dc % 2) * 256 + TGM],
                                wsl[:, f % 2, dc * 128:(dc + 1) * 128],
                                hbuf[:, slot, :],
                                start=False,
                                stop=(f == NF - 1 and dc % 2 == 1),
                                skip_group_check=True)
                        S.inc("T", i)
                    ring_rd[slot] = S.tok("T")
                    w2_rd[(f // 2) % 2] = S.tok("T")
                mm2_done = S.tok("T")

                if e == 0 and tg == 0 and dbg is not None:
                    @blk.vector
                    def _(v, mm2_done=mm2_done):
                        S.wait(v, "T", mm2_done)
                        i = v.tensor_copy(ytmp[:, :], pb[1][:, 0:TGM])
                        S.inc("V", i)

                    @blk.sync
                    def _(sy, vv=S.tok("V")):
                        S.wait(sy, "V", vv)
                        S.inc("st0", sy.dma_start(dbg[4][:, 0, :], ytmp[:, :]), 16)

                    @blk.vector
                    def _(v, ss=S.tok("st0")):
                        S.wait(v, "st0", ss)
                        S.wait(v, "V")
                        i = v.tensor_copy(ytmp[:, :], pb[2][:, 0:TGM])
                        S.inc("V", i)

                    @blk.sync
                    def _(sy, vv=S.tok("V")):
                        S.wait(sy, "V", vv)
                        S.inc("st0", sy.dma_start(dbg[4][:, 1, :], ytmp[:, :]), 16)

                    @blk.vector
                    def _(v, ss=S.tok("st0")):
                        S.wait(v, "st0", ss)

                for dc in range(DCN):
                    @blk.vector
                    def _(v, dc=dc, t0=t0, mm2_done=mm2_done, wbc_tok=wbc_tok):
                        S.wait(v, "T", mm2_done)
                        S.wait(v, "V", wbc_tok)
                        S.wait(v, "V")
                        i = v.tensor_tensor(
                            ytmp[:, :],
                            pb[1 + dc // 2][:, (dc % 2) * 256:(dc % 2) * 256 + TGM],
                            wbc[:, :], op=ALU.mult)
                        S.inc("V", i)
                        S.wait(v, "V")
                        i = v.tensor_tensor(out_acc[:, dc, t0:t0 + TGM],
                                            out_acc[:, dc, t0:t0 + TGM],
                                            ytmp[:, :], op=ALU.add)
                        S.inc("V", i)
                last_y_read = S.tok("V")

                if e == 0 and tg == 0 and dbg is not None:
                    @blk.sync
                    def _(sy, ly=last_y_read):
                        S.wait(sy, "V", ly)
                        S.inc("st0", sy.dma_start(dbg[0][:, :, :],
                                                  out_acc[:, :, :]), 16)
                        S.inc("st0", sy.dma_start(dbg[1][0], w2_s[0][:, :, :]), 16)
                        S.inc("st0", sy.dma_start(dbg[1][1], w2_s[1][:, :, :]), 16)
                        S.inc("st0", sy.dma_start(dbg[2][:, :, :], hbuf[:, :, :]), 16)
                        S.inc("st0", sy.dma_start(dbg[3][:, :], wbc[:, :]), 16)

                if tg == NGM - 1 and e + 1 < NE_:
                    @blk.gpsimd
                    def _(g, e=e, mm2_done=mm2_done):
                        S.wait(g, "T", mm2_done)
                        for pc in range(8):
                            i = g.dma_start(
                                w1_s[:, :, pc * 512:(pc + 1) * 512],
                                w1_d[e + 1, :, :, pc * 512:(pc + 1) * 512])
                            w1_tok[e + 1][pc] = S.inc(f"dW1p{pc}", i, 16)



def _p4(nc, blk, S, C, out_acc, pb, P, y_d):
    HT = P["HT"]
    TG = min(512, HT)
    NG = HT // TG
    with ExitStack() as ctx:
        sb = lambda name, shape, dt=F32: ctx.enter_context(nc.sbuf_tensor(name, shape, dt))
        sums = sb("sums4", [1, 2 * TG])
        stat = sb("stat4", [1, 3 * TG])
        sqv = sb("sqv4", [128, TG])
        stg = [sb(f"stg4{i}", [128, D]) for i in range(2)]

        stg_st = [0, 0]
        prevo = {}
        for g in range(NG):
            t0 = g * TG

            @blk.tensor
            def _(t):
                if prevo:
                    S.wait(t, "T", prevo["end_T"])
                    S.wait(t, "V", prevo["end_V"])
                    S.wait(t, "Sc", prevo["end_Sc"])
                for dc in range(DCN):
                    i = t.matmul(pb[4][0:1, 0:TG], C["ones_colf"][:, :],
                                 out_acc[:, dc, t0:t0 + TG],
                                 start=(dc == 0), stop=(dc == DCN - 1))
                S.inc("T", i)
            sum_tok = S.tok("T")

            sq_T = sum_tok
            for dc in range(DCN):
                @blk.vector
                def _(v, dc=dc, sq_T=sq_T):
                    if prevo:
                        S.wait(v, "T", prevo["end_T"])
                    S.wait(v, "T", sq_T)
                    i = v.tensor_mul(sqv[:, :], out_acc[:, dc, t0:t0 + TG],
                                     out_acc[:, dc, t0:t0 + TG])
                    S.inc("V", i)
                vq = S.tok("V")

                @blk.tensor
                def _(t, dc=dc, vq=vq):
                    S.wait(t, "V", vq)
                    i = t.matmul(pb[5][0:1, 0:TG], C["ones_colf"][:, :], sqv[:, :],
                                 start=(dc == 0), stop=(dc == DCN - 1))
                    S.inc("T", i)
                sq_T = S.tok("T")
            sumsq_tok = sq_T

            @blk.scalar
            def _(s):
                if prevo:
                    S.wait(s, "Sc", prevo["end_Sc"])
                S.wait(s, "T", sumsq_tok)
                i = s.mul(sums[:, 0:TG], pb[4][0:1, 0:TG], 1.0 / D)
                S.inc("Sc", i)
                i = s.mul(sums[:, TG:2 * TG], pb[5][0:1, 0:TG], 1.0 / D)
                S.inc("Sc", i)
            mv_tok = S.tok("Sc")

            @blk.vector
            def _(v):
                S.wait(v, "Sc", mv_tok)
                i = v.tensor_mul(stat[:, 0:TG], sums[:, 0:TG], sums[:, 0:TG])
                S.inc("V", i)
                S.wait(v, "V")
                i = v.tensor_sub(stat[:, 0:TG], sums[:, TG:2 * TG], stat[:, 0:TG])
                S.inc("V", i)
            var_tok = S.tok("V")

            @blk.scalar
            def _(s):
                S.wait(s, "V", var_tok)
                i = s.activation(stat[:, TG:2 * TG], stat[:, 0:TG], AF.Ln, bias=C['fc'][0:1, 3:4])
                S.inc("Sc", i)
                S.wait(s, "Sc")
                i = s.activation(stat[:, TG:2 * TG], stat[:, TG:2 * TG], AF.Exp,
                                 scale=-0.5)
                S.inc("Sc", i)
            rstd_tok = S.tok("Sc")

            @blk.vector
            def _(v):
                S.wait(v, "Sc", rstd_tok)
                i = v.tensor_mul(stat[:, 2 * TG:3 * TG], sums[:, 0:TG],
                                 stat[:, TG:2 * TG])
                S.inc("V", i)
            mr_tok = S.tok("V")

            @blk.tensor
            def _(t):
                S.wait(t, "V", mr_tok)
                t.matmul(pb[5][:, 0:TG], C["ones_row"][:, :], stat[:, TG:2 * TG],
                         start=True, stop=True)
                i = t.matmul(pb[4][:, 0:TG], C["ones_row"][:, :],
                             stat[:, 2 * TG:3 * TG], start=True, stop=True)
                S.inc("T", i)
            bc_tok = S.tok("T")

            for dc in range(DCN):
                @blk.vector
                def _(v, dc=dc):
                    S.wait(v, "T", bc_tok)
                    i = v.tensor_tensor(out_acc[:, dc, t0:t0 + TG],
                                        out_acc[:, dc, t0:t0 + TG],
                                        pb[5][:, 0:TG], op=ALU.mult)
                    S.inc("V", i)
                    S.wait(v, "V")
                    i = v.tensor_tensor(out_acc[:, dc, t0:t0 + TG],
                                        out_acc[:, dc, t0:t0 + TG],
                                        pb[4][:, 0:TG], op=ALU.subtract)
                    S.inc("V", i)
            nrm_tok = S.tok("V")

            prev_stv = 0
            for tt in range(TG // 128):
                sslot = tt % 2
                for hf in range(2):
                    bank = pb[hf]

                    @blk.tensor
                    def _(t, tt=tt, hf=hf, bank=bank, sslot=sslot,
                          prev_stv=prev_stv):
                        S.wait(t, "V", nrm_tok)
                        S.wait(t, "V", prev_stv)
                        S.wait(t, f"st{sslot}", stg_st[sslot])
                        for q in range(4):
                            dc = hf * 4 + q
                            i = t.transpose(
                                bank[:, q * 128:(q + 1) * 128],
                                out_acc[:, dc,
                                        t0 + tt * 128:t0 + tt * 128 + 128],
                                C["identf"][:, :])
                        S.inc("T", i)
                    tk = S.tok("T")

                    @blk.vector
                    def _(v, hf=hf, bank=bank, tk=tk, sslot=sslot):
                        S.wait(v, "T", tk)
                        S.wait(v, f"st{sslot}", stg_st[sslot])
                        i = v.tensor_copy(stg[sslot][:, hf * 512:(hf + 1) * 512],
                                          bank[:, 0:512])
                        S.inc("V", i)
                stv = S.tok("V")
                prev_stv = stv

                @blk.sync
                def _(sy, tt=tt, stv=stv, sslot=sslot):
                    S.wait(sy, "V", stv)
                    i = sy.dma_start(y_d[t0 + tt * 128:t0 + (tt + 1) * 128, :],
                                     stg[sslot][:, :])
                    S.inc(f"st{sslot}", i, 16)
                stg_st[sslot] = S.tok(f"st{sslot}")
            prevo = dict(end_T=S.tok("T"), end_V=S.tok("V"), end_Sc=S.tok("Sc"))



def kernel(L0_attn_x, L0_attn_Wv, L0_attn_Wg, L0_attn_Wo, L0_attn_Wb, L0_attn_Wa,
           L0_attn_dt_bias, L0_attn_A_log, L0_attn_norm_w, L0_attn_conv_w,
           L0_attn_conv_b, L0_ffn_gamma, L0_ffn_beta, L0_ffn_router_w,
           L0_ffn_W1, L0_ffn_W2):
    T = np.asarray(L0_attn_x).shape[1]
    HT = T // 2
    nc = build(T=T)

    cc = np.ascontiguousarray
    f32 = lambda a: np.asarray(a, np.float32)

    def chunkp(w):
        w = f32(w)
        return cc(w.reshape(DCN, 128, -1).transpose(1, 0, 2))

    wvT = chunkp(f32(L0_attn_Wv).T)
    wgT = chunkp(f32(L0_attn_Wg).T)
    woT = chunkp(f32(L0_attn_Wo).T)
    wbT = chunkp(f32(L0_attn_Wb).T)
    waT = chunkp(f32(L0_attn_Wa).T)
    rwT = chunkp(f32(L0_ffn_router_w).T)
    convw = chunkp(f32(L0_attn_conv_w))
    convb = cc(f32(L0_attn_conv_b).reshape(DCN, 128).T)
    gam = cc(f32(L0_ffn_gamma).reshape(DCN, 128).T)
    betf = cc(f32(L0_ffn_beta).reshape(DCN, 128).T)
    w1 = cc(f32(L0_ffn_W1).reshape(E, DCN, 128, FF).transpose(0, 2, 1, 3))
    w2 = cc(f32(L0_ffn_W2))
    dtb = cc(f32(L0_attn_dt_bias).reshape(HH, 1))
    alog = cc(f32(L0_attn_A_log).reshape(HH, 1))
    normw = cc(f32(L0_attn_norm_w).reshape(128, 1))
    x = f32(L0_attn_x)

    in_maps = []
    for c in range(8):
        b, half = c >> 1, c & 1
        msel = np.zeros((128, 2), np.float32)
        msel[:, half] = 1.0
        in_maps.append(dict(
            xb=cc(x[b]), xh=cc(x[b, half * HT:(half + 1) * HT]),
            wvT=wvT, wgT=wgT, woT=woT, wbT=wbT, waT=waT,
            dtb=dtb, alog=alog, normw=normw, convw=convw, convb=convb,
            gam=gam, betf=betf, rwT=rwT, w1=w1, w2=w2, msel=msel,
        ))

    res = run_bass_kernel_spmd(nc, in_maps, core_ids=list(range(8)))
    out = np.empty((B, T, D), np.float32)
    for c in range(8):
        b, half = c >> 1, c & 1
        out[b, half * HT:(half + 1) * HT] = np.asarray(res.results[c]["y"])
    return out


# version 03
# speedup vs baseline: 1.4043x; 1.4043x over previous
import numpy as np
from contextlib import ExitStack

import concourse.bass as bass
import concourse.mybir as mybir
from concourse.bass_utils import run_bass_kernel_spmd

F32 = mybir.dt.float32
BF16 = mybir.dt.bfloat16
AF = mybir.ActivationFunctionType
ALU = mybir.AluOpType
AX = mybir.AxisListType

B, D, HH, E, FF = 4, 1024, 8, 8, 4096
DCN = D // 128
EPS_L2, EPS_RMS, EPS_LN, TEMP = 1e-6, 1e-5, 1e-5, 1.5
RSK = float(1.0 / np.sqrt(128.0))


class Sems:
    def __init__(self, nc, ctx):
        self.nc, self.ctx, self.m = nc, ctx, {}

    def new(self, *names):
        for n in names:
            self.m[n] = [self.ctx.enter_context(self.nc.semaphore(n)), 0]

    def inc(self, name, instr, k=1):
        e = self.m[name]
        instr.then_inc(e[0], k)
        e[1] += k
        return e[1]

    def tok(self, name):
        return self.m[name][1]

    def wait(self, eng, name, tok=None):
        e = self.m[name]
        v = e[1] if tok is None else tok
        if v > 0:
            eng.wait_ge(e[0], v)

    def toks(self):
        return {k: v[1] for k, v in self.m.items()}


def bc2(ap2d, n, w=128):
    return ap2d.unsqueeze(2).broadcast_to([ap2d.shape[0], n, w])


def mbc(m, n):
    return m[:, :].unsqueeze(1).broadcast_to([128, n, 128])


def build(T=4096, debug=False, p3lim=None):
    TGD = 256
    NGD = T // TGD
    CPG = 2
    HT = T // 2
    NTT = HT // 128
    TGM = 256
    NGM = HT // TGM
    TGL = 256
    NGL = HT // TGL

    nc = bass.Bass("TRN2", target_bir_lowering=False,
                   dynamic_dma_scratch_size=16384)

    din = lambda name, shape, dt=F32: nc.dram_tensor(name, shape, dt, kind="ExternalInput")
    x_d = din("xb", [T, D])
    xh_d = din("xh", [HT, D])
    wvT_d = din("wvT", [128, DCN, D])
    wgT_d = din("wgT", [128, DCN, D])
    woT_d = din("woT", [128, DCN, D])
    wbT_d = din("wbT", [128, DCN, HH])
    waT_d = din("waT", [128, DCN, HH])
    dtb_d = din("dtb", [HH, 1])
    alog_d = din("alog", [HH, 1])
    normw_d = din("normw", [128, 1])
    convw_d = din("convw", [128, DCN, 4])
    convb_d = din("convb", [128, DCN])
    gam_d = din("gam", [128, DCN])
    betf_d = din("betf", [128, DCN])
    rwT_d = din("rwT", [128, DCN, E])
    w1_d = din("w1", [E, 128, DCN, FF])
    w2_d = din("w2", [E, FF, D])
    msel_d = din("msel", [128, 2])
    y_d = nc.dram_tensor("y", [HT, D], F32, kind="ExternalOutput")
    if debug:
        dbg_om = nc.dram_tensor("dbg_om", [128, HH * (HT // 128) * 128], BF16,
                                kind="ExternalOutput")
        dbg_x1 = nc.dram_tensor("dbg_x1", [128, DCN, HT], F32, kind="ExternalOutput")
        dbg_xn = nc.dram_tensor("dbg_xn", [128, DCN, HT], BF16, kind="ExternalOutput")
        dbg_wd = nc.dram_tensor("dbg_wd", [8, HT], F32, kind="ExternalOutput")
        dbg_acc = nc.dram_tensor("dbg_acc", [128, DCN, HT], F32, kind="ExternalOutput")
        dbg_acc0 = nc.dram_tensor("dbg_acc0", [128, DCN, HT], F32, kind="ExternalOutput")
        dbg_w2 = nc.dram_tensor("dbg_w2", [2, 128, 2, D], BF16, kind="ExternalOutput")
        dbg_h = nc.dram_tensor("dbg_h", [128, 4, TGM], BF16, kind="ExternalOutput")
        dbg_wbc = nc.dram_tensor("dbg_wbc", [128, TGM], F32, kind="ExternalOutput")
        dbg_ps = nc.dram_tensor("dbg_ps", [128, 2, TGM], F32, kind="ExternalOutput")

    with ExitStack() as ctx:
        S = Sems(nc, ctx)
        S.new("dX", "dS", "dW0", "dWVG", "dWO", "T", "V", "Sc", "G", "st0", "st1",
              *[f"dW1p{i}" for i in range(8)], "dW2s0", "dW2s1")
        sbR = lambda name, shape, dt=F32: ctx.enter_context(
            nc.sbuf_tensor(name, shape, dt, side="right"))

        identb = sbR("identb", [128, 128], BF16)
        identf = sbR("identf", [128, 128])
        ones_colf = sbR("ones_colf", [128, 1])
        ones_row = sbR("ones_row", [1, 128])
        triuF = sbR("triuF", [128, 128])
        trilP = sbR("trilP", [128, 128])
        strN = sbR("strN", [128, 128])
        ntriP = sbR("ntriP", [128, 128])
        triK = sbR("triK", [128, 128])
        e127 = sbR("e127", [128, 1])
        e8c = sbR("e8c", [8, 8])
        msel_s = sbR("msel_s", [128, 2])
        normw_s = sbR("normw_s", [128, 1])
        dtb_s = sbR("dtb_s", [HH, 1])
        nea_s = sbR("nea_s", [HH, 1])
        convb_s = sbR("convb_s", [128, DCN])
        convw_s = sbR("convw_s", [128, DCN, 4])
        gam_s = sbR("gam_s", [128, DCN])
        betf_s = sbR("betf_s", [128, DCN])
        rwT_s = sbR("rwT_s", [128, DCN, E])
        rwg_s = sbR("rwg_s", [128, DCN, E])
        rbc_s = sbR("rbc_s", [8, 1])
        wbT_s = sbR("wbT_s", [128, DCN, HH], BF16)
        waT_s = sbR("waT_s", [128, DCN, HH], BF16)
        fc = sbR("fconst", [128, 4])
        rb_row = sbR("rb_row", [1, E])
        rwtmp = sbR("rwtmp", [128, E])
        om_ctx = ExitStack()
        o_mine = om_ctx.enter_context(
            nc.sbuf_tensor("o_mine", [128, HH, NTT, 128], BF16, side="right"))

        pb = [ctx.enter_context(nc.psum_tensor(f"pb{i}", [128, 512], F32))
              for i in range(6)]
        pbB0 = pb[0]
        px = [ctx.enter_context(nc.psum_tensor(f"px{i}", [128, 1024], BF16))
              for i in range(2)]

        blk = ctx.enter_context(nc.Block())

        C = dict(identb=identb, identf=identf, ones_colf=ones_colf,
                 ones_row=ones_row, triuF=triuF, trilP=trilP, strN=strN,
                 ntriP=ntriP, triK=triK, e127=e127, e8c=e8c, msel=msel_s,
                 normw=normw_s, dtb=dtb_s, nea=nea_s, convb=convb_s,
                 convw=convw_s, gam=gam_s, betf=betf_s, rwT=rwT_s,
                 rwg=rwg_s, rbc=rbc_s, wbT=wbT_s, waT=waT_s, fc=fc)

        @blk.vector
        def _(v):
            for buf, val in [(identb, 1.0), (identf, 1.0), (ones_colf, 1.0),
                             (ones_row, 1.0), (triuF, 1.0), (trilP, 1.0),
                             (strN, -1.0), (ntriP, -1.0), (triK, RSK),
                             (e127, 1.0), (e8c, 1.0)]:
                v.memset(buf[:, :], val)
            v.memset(fc[:, 0:1], EPS_L2)
            v.memset(fc[:, 1:2], 1.0)
            v.memset(fc[:, 2:3], EPS_RMS)
            v.memset(fc[:, 3:4], EPS_LN)
            i = v.memset(o_mine[:, :, :, :], 0.0)
            S.inc("V", i)

        @blk.gpsimd
        def _(g):
            S.wait(g, "V")
            aff = lambda buf, base, pat, cm, op=ALU.is_ge: g.affine_select(
                out=buf[:, :], in_=buf[:, :], compare_op=op,
                fill=0.0, base=base, pattern=pat, channel_multiplier=cm)
            aff(identb, 0, [[-1, 128]], 1, ALU.is_equal)
            aff(identf, 0, [[-1, 128]], 1, ALU.is_equal)
            aff(triuF, 0, [[1, 128]], -1)
            aff(trilP, 0, [[-1, 128]], 1)
            aff(strN, -1, [[-1, 128]], 1)
            aff(ntriP, 0, [[1, 128]], -1)
            aff(triK, 0, [[1, 128]], -1)
            aff(e127, -127, [[0, 1]], 1, ALU.is_equal)
            i = aff(e8c, 0, [[-1, 8]], 1, ALU.is_equal)
            S.inc("G", i)
            S.inc("dW0", g.dma_start(rwT_s[:, :, :], rwT_d[:, :, :]), 16)
            S.inc("dW0", g.dma_start(wbT_s[:, :, :], wbT_d[:, :, :]), 16)
            S.inc("dW0", g.dma_start(waT_s[:, :, :], waT_d[:, :, :]), 16)

        @blk.sync
        def _(sy):
            for dst, srcd in [(msel_s, msel_d), (normw_s, normw_d),
                              (dtb_s, dtb_d), (nea_s, alog_d),
                              (convb_s, convb_d), (convw_s, convw_d),
                              (gam_s, gam_d), (betf_s, betf_d)]:
                S.inc("dS", sy.dma_start(dst.ap(), srcd.ap()), 16)

        @blk.scalar
        def _(s):
            S.wait(s, "dS")
            i = s.activation(nea_s[:, :], nea_s[:, :], AF.Exp, scale=1.0)
            S.inc("Sc", i)

        @blk.vector
        def _(v):
            S.wait(v, "Sc")
            i = v.tensor_scalar_mul(nea_s[:, :], nea_s[:, :], -1.0)
            S.inc("V", i)
            S.wait(v, "dW0")
            for dc in range(DCN):
                i = v.tensor_scalar_mul(rwg_s[:, dc, :], rwT_s[:, dc, :],
                                        C["gam"][:, dc:dc + 1])
                S.inc("V", i)
        rwg_tok = S.tok("V")

        @blk.vector
        def _(v):
            pass

        @blk.scalar
        def _(s):
            pass

        @blk.tensor
        def _(t):
            S.wait(t, "V", rwg_tok)

        for dc in range(DCN):
            @blk.vector
            def _(v, dc=dc):
                S.wait(v, "T")
                i = v.tensor_scalar_mul(rwtmp[:, :], rwT_s[:, dc, :],
                                        C["betf"][:, dc:dc + 1])
                S.inc("V", i)
            vq = S.tok("V")

            @blk.tensor
            def _(t, dc=dc, vq=vq):
                S.wait(t, "V", vq)
                i = t.matmul(px[0].bitcast(F32)[0:1, 0:E] if False else
                             None, None, None) if False else t.matmul(
                    pbB0[0:1, 0:E], C["ones_colf"][:, :], rwtmp[:, :],
                    start=(dc == 0), stop=(dc == DCN - 1))
                S.inc("T", i)
        rb_T = S.tok("T")

        @blk.scalar
        def _(s):
            S.wait(s, "T", rb_T)
            i = s.copy(rb_row[:, :], pbB0[0:1, 0:E])
            S.inc("Sc", i)
        rb_Sc = S.tok("Sc")

        @blk.tensor
        def _(t):
            S.wait(t, "Sc", rb_Sc)
            i = t.transpose(pbB0[0:8, 64:65], rb_row[:, :], C["identf"][0:1, 0:1])
            S.inc("T", i)
        rbt_T = S.tok("T")

        @blk.vector
        def _(v):
            S.wait(v, "T", rbt_T)
            i = v.tensor_copy(rbc_s[:, :], pbB0[0:8, 64:65])
            S.inc("V", i)

        _p1(nc, blk, S, C, o_mine, pb, px,
            dict(T=T, NGD=NGD, CPG=CPG, TGD=TGD, NTT=NTT), x_d, wvT_d, wgT_d, ctx)

        _barrier(blk, S)

        sbL = lambda name, shape, dt=F32: ctx.enter_context(nc.sbuf_tensor(name, shape, dt))
        out_acc = sbL("out_acc", [128, DCN, HT])
        xn_s = sbL("xn", [128, DCN, HT], BF16)
        wdT_s = sbL("wdT", [8, HT])
        _p2(nc, blk, S, C, o_mine, out_acc, xn_s, wdT_s, pb, px,
            dict(HT=HT, NGL=NGL, TGL=TGL), xh_d, woT_d)

        if debug:
            S.new("dbg")

            @blk.sync
            def _(g):
                for k, v_ in S.toks().items():
                    if k != "dbg":
                        S.wait(g, k, v_)
                S.inc("dbg", g.dma_start(dbg_om[:, :], o_mine[:, :, :, :].rearrange("p a b c -> p (a b c)")), 16)
                S.inc("dbg", g.dma_start(dbg_x1[:, :, :], out_acc[:, :, :]), 16)
                S.inc("dbg", g.dma_start(dbg_xn[:, :, :], xn_s[:, :, :]), 16)
                S.inc("dbg", g.dma_start(dbg_wd[:, :], wdT_s[:, :]), 16)
                S.wait(g, "dbg")

        _barrier(blk, S)
        om_ctx.close()

        _p3(nc, blk, S, C, out_acc, xn_s, wdT_s, pb,
            dict(HT=HT, NGM=NGM, TGM=TGM, p3lim=p3lim), w1_d, w2_d,
            (dbg_acc0, dbg_w2, dbg_h, dbg_wbc, dbg_ps) if debug else None)

        if debug:
            @blk.sync
            def _(g):
                for k, v_ in S.toks().items():
                    if k != "dbg":
                        S.wait(g, k, v_)
                S.inc("dbg", g.dma_start(dbg_acc[:, :, :], out_acc[:, :, :]), 16)
                S.wait(g, "dbg")

        _barrier(blk, S)

        _p4(nc, blk, S, C, out_acc, pb, dict(HT=HT), y_d)

        @blk.sync
        def _(sy):
            S.wait(sy, "st0")
            S.wait(sy, "st1")

    return nc


def _barrier(blk, S):
    t = S.toks()
    for deco in (blk.tensor, blk.vector, blk.scalar, blk.gpsimd, blk.sync):
        def f(eng, _t=t):
            for k, v in _t.items():
                S.wait(eng, k, v)
        deco(f)



def _p1(nc, blk, S, C, o_mine, pb, px, P, x_d, wvT_d, wgT_d, octx):
    T, NGD, CPG, TGD, NTT = P["T"], P["NGD"], P["CPG"], P["TGD"], P["NTT"]
    with ExitStack() as ctx:
        sb = lambda name, shape, dt=F32: ctx.enter_context(nc.sbuf_tensor(name, shape, dt))
        wvT_s = sb("wvT_s", [128, DCN, D], BF16)
        wgT_s = sb("wgT_s", [128, DCN, D], BF16)
        S_s = sb("S_s", [128, HH, 128])
        S16 = sb("S16", [128, HH, 128], BF16)
        x_tm = sb("x_tm", [128, 2, D])
        xfm = [sb(f"xfm{i}", [128, DCN, TGD], BF16) for i in range(2)]
        kmj = [sb(f"kmj{i}", [128, HH, TGD + 1], BF16) for i in range(2)]
        l2tmp = sb("l2tmp", [128, 2, D], BF16)
        sqt = sb("sqt", [128, D])
        ssq = sb("ssq", [128, 2 * HH])
        rsq = sb("rsq", [128, 2 * HH])
        vpre = [sb(f"vpre{i}", [128, DCN, TGD + 3]) for i in range(2)]
        cvt = [sb(f"cvt{i}", [128, TGD]) for i in range(2)]
        vfm = sb("vfm", [128, DCN, TGD], BF16)
        vtm = [sb(f"vtm{i}", [128, 2, HH, 128], BF16) for i in range(2)]
        gfm = sb("gfm", [128, DCN, TGD], BF16)
        gtm = [sb(f"gtm{i}", [128, 2, HH, 128]) for i in range(2)]
        wktm = [sb(f"wktm{i}", [128, 2, HH, 128], BF16) for i in range(2)]
        bdr = sb("bdr", [8, 2, TGD])
        beta_cm = sb("beta_cm", [128, 2 * HH])
        dec_cm = sb("dec_cm", [128, 2 * HH])
        dec_cum = sb("dec_cum", [128, 2 * HH])
        dlast = sb("dlast", [1, 2 * HH])
        elsr = sb("elsr", [1, 2 * HH])
        esl = sb("esl", [128, 2 * HH])
        dwv = sb("dwv", [128, 2 * HH])
        edec = sb("edec", [128, 2 * HH])
        edk = sb("edk", [128, 2 * HH])
        bev = sb("bev", [128, 2 * HH])
        mt1 = sb("mt1", [128, 2, 128])
        mt2 = sb("mt2", [128, 2, 128])
        lsnB = sb("lsnB", [128, 2, 128])
        ltk = sb("ltk", [128, 2, 128])
        abuf = sb("abuf", [128, 2, 128], BF16)
        atbuf = sb("atbuf", [128, 2, 128], BF16)
        pw = [sb(f"pw{i}", [128, 2, 128], BF16) for i in range(2)]
        pwT = [sb(f"pwT{i}", [128, 2, 128], BF16) for i in range(2)]
        gch = sb("gch", [128, 2, 128], BF16)
        vbb = sb("vbb", [128, 2, 128], BF16)
        wkbe = sb("wkbe", [128, 2, 128], BF16)
        vpr = sb("vpr", [128, HH, 2, 128], BF16)
        wkc = sb("wkc", [128, 2, 128], BF16)
        wkcT = sb("wkcT", [128, HH, 2, 128], BF16)
        atn = sb("atn", [128, HH, 2, 128], BF16)
        vnew = sb("vnew", [128, 128], BF16)
        wkdw = sb("wkdw", [128, 128], BF16)
        o_grp = sb("o_grp", [128, HH, TGD])
        ogt = sb("ogt", [128, TGD])
        rrs = sb("rrs", [128, 2 * HH])

        @blk.gpsimd
        def _(gp):
            S.wait(gp, "G")
            S.inc("dWVG", gp.dma_start(wvT_s[:, :, :], wvT_d[:, :, :]), 16)
            S.inc("dWVG", gp.dma_start(wgT_s[:, :, :], wgT_d[:, :, :]), 16)
        wload = S.tok("dWVG")

        @blk.vector
        def _(v):
            v.memset(S_s[:, :, :], 0.0)
            v.memset(S16[:, :, :], 0.0)
            v.memset(kmj[0][:, :, 0:1], 0.0)
            i = v.memset(vpre[0][:, :, 0:3], 0.0)
            S.inc("V", i)
        init_tok = S.tok("V")

        prev = {}

        for g in range(NGD):
            par = g % 2

            @blk.sync
            def _(sy):
                if prev:
                    S.wait(sy, "V", prev["x_V"])
                    S.wait(sy, "T", prev["x_T"])
                i = sy.dma_start(
                    x_tm[:, :, :],
                    x_d[g * TGD:(g + 1) * TGD, :].rearrange("(tt p) d -> p tt d", p=128))
                S.inc("dX", i, 16)
            x_dma = S.tok("dX")

            for deco in (blk.tensor, blk.vector, blk.scalar):
                def f(eng):
                    S.wait(eng, "dX", x_dma)
                    if prev:
                        S.wait(eng, "T", prev["end_T"])
                        S.wait(eng, "V", prev["end_V"])
                        S.wait(eng, "Sc", prev["end_Sc"])
                    else:
                        S.wait(eng, "V", init_tok)
                        S.wait(eng, "G")
                        S.wait(eng, "dWVG", wload)
                        S.wait(eng, "Sc")
                        S.wait(eng, "dS")
                        S.wait(eng, "dW0")
                deco(f)

            @blk.vector
            def _(v):
                for tt in range(2):
                    S.wait(v, "V")
                    i = v.tensor_mul(sqt[:, :], x_tm[:, tt, :], x_tm[:, tt, :])
                    S.inc("V", i)
                    S.wait(v, "V")
                    i = v.reduce_sum(ssq[:, tt * HH:(tt + 1) * HH],
                                     sqt[:, :].rearrange("p (c k) -> p c k", c=HH), AX.X)
                    S.inc("V", i)
            sumsq_tok = S.tok("V")

            @blk.scalar
            def _(s):
                S.wait(s, "V", sumsq_tok)
                i = s.activation(rsq[:, :], ssq[:, :], AF.Ln, bias=C['fc'][:, 0:1])
                S.inc("Sc", i)
                S.wait(s, "Sc")
                i = s.activation(rsq[:, :], rsq[:, :], AF.Exp, scale=-0.5)
                S.inc("Sc", i)
            rsq_tok = S.tok("Sc")

            @blk.vector
            def _(v):
                S.wait(v, "Sc", rsq_tok)
                for tt in range(2):
                    i = v.tensor_tensor(
                        l2tmp[:, tt, :].rearrange("p (c k) -> p c k", c=HH),
                        x_tm[:, tt, :].rearrange("p (c k) -> p c k", c=HH),
                        bc2(rsq[:, tt * HH:(tt + 1) * HH], HH), op=ALU.mult)
                    S.inc("V", i)
            l2n_tok = S.tok("V")

            ev = []
            for tt in range(2):
                for hf in range(2):
                    bank = pb[hf]

                    @blk.tensor
                    def _(t, tt=tt, hf=hf, bank=bank):
                        if len(ev) >= 2:
                            S.wait(t, "Sc", ev[-2])
                        for q in range(4):
                            dc = hf * 4 + q
                            i = t.transpose(bank[:, q * 128:(q + 1) * 128],
                                            x_tm[:, tt, dc * 128:(dc + 1) * 128],
                                            C["identf"][:, :])
                        S.inc("T", i)
                    ttok = S.tok("T")

                    @blk.scalar
                    def _(s, tt=tt, hf=hf, bank=bank, ttok=ttok):
                        S.wait(s, "T", ttok)
                        i = s.copy(
                            xfm[par][:, hf * 4:(hf + 1) * 4, tt * 128:(tt + 1) * 128],
                            bank[:, :].rearrange("p (c k) -> p c k", c=4))
                        S.inc("Sc", i)
                    ev.append(S.tok("Sc"))
            xfm_tok = S.tok("Sc")
            x_T_tok = S.tok("T")

            @blk.scalar
            def _(s):
                if g >= 1:
                    i = s.copy(kmj[par][:, :, 0:1], kmj[1 - par][:, :, TGD:TGD + 1])
                    S.inc("Sc", i)
            halo_tok = S.tok("Sc")

            for tt in range(2):
                bank = px[tt]

                @blk.tensor
                def _(t, tt=tt, bank=bank):
                    S.wait(t, "V", l2n_tok)
                    for h in range(HH):
                        i = t.transpose(bank[:, h * 128:(h + 1) * 128],
                                        l2tmp[:, tt, h * 128:(h + 1) * 128],
                                        C["identb"][:, :])
                    S.inc("T", i)
                kt = S.tok("T")

                @blk.scalar
                def _(s, tt=tt, bank=bank, kt=kt):
                    S.wait(s, "T", kt)
                    i = s.copy(kmj[par][:, :, 1 + tt * 128:1 + (tt + 1) * 128],
                               bank[:, :].rearrange("p (c k) -> p c k", c=HH))
                    S.inc("Sc", i)
            kmj_tok = S.tok("Sc")
            l2_T_tok = S.tok("T")

            @blk.vector
            def _(v):
                if g >= 1:
                    i = v.tensor_copy(vpre[par][:, :, 0:3],
                                      vpre[1 - par][:, :, TGD:TGD + 3])
                    S.inc("V", i)
            vhalo_tok = S.tok("V")

            vev = []
            for dv in range(DCN):
                bank = pb[2 + (dv % 2)]

                @blk.tensor
                def _(t, dv=dv, bank=bank):
                    S.wait(t, "Sc", xfm_tok)
                    if len(vev) >= 2:
                        S.wait(t, "V", vev[-2])
                    for dc in range(DCN):
                        i = t.matmul(bank[:, 0:TGD],
                                     wvT_s[:, dc, dv * 128:(dv + 1) * 128],
                                     xfm[par][:, dc, :],
                                     start=(dc == 0), stop=(dc == DCN - 1))
                    S.inc("T", i)
                vt = S.tok("T")

                @blk.vector
                def _(v, dv=dv, bank=bank, vt=vt):
                    S.wait(v, "T", vt)
                    S.wait(v, "V", vhalo_tok)
                    i = v.tensor_copy(vpre[par][:, dv, 3:3 + TGD], bank[:, 0:TGD])
                    S.inc("V", i)
                vev.append(S.tok("V"))
            vproj_tok = S.tok("V")

            gev = []
            for dv in range(DCN):
                bank = pb[2 + (dv % 2)]

                @blk.tensor
                def _(t, dv=dv, bank=bank):
                    S.wait(t, "V", vev[dv])
                    if len(gev) >= 2:
                        S.wait(t, "Sc", gev[-2])
                    for dc in range(DCN):
                        i = t.matmul(bank[:, 256:512],
                                     wgT_s[:, dc, dv * 128:(dv + 1) * 128],
                                     xfm[par][:, dc, :],
                                     start=(dc == 0), stop=(dc == DCN - 1))
                    S.inc("T", i)
                gt_ = S.tok("T")

                @blk.scalar
                def _(s, dv=dv, bank=bank, gt_=gt_):
                    S.wait(s, "T", gt_)
                    i = s.activation(gfm[:, dv, :], bank[:, 256:512], AF.Silu)
                    S.inc("Sc", i)
                gev.append(S.tok("Sc"))
            gact_tok = S.tok("Sc")

            @blk.vector
            def _(v):
                S.wait(v, "Sc", gact_tok)
                i = v.tensor_scalar_mul(gfm[:, :, :], gfm[:, :, :], C["normw"][:, :])
                S.inc("V", i)
            gnorm_tok = S.tok("V")

            @blk.tensor
            def _(t):
                S.wait(t, "Sc", xfm_tok)
                for dc in range(DCN):
                    t.matmul(pb[4][0:HH, 0:TGD], C["wbT"][:, dc, :], xfm[par][:, dc, :],
                             start=(dc == 0), stop=(dc == DCN - 1))
                for dc in range(DCN):
                    i = t.matmul(pb[4][0:HH, 256:512], C["waT"][:, dc, :],
                                 xfm[par][:, dc, :],
                                 start=(dc == 0), stop=(dc == DCN - 1))
                S.inc("T", i)
            bdz_tok = S.tok("T")

            @blk.scalar
            def _(s):
                S.wait(s, "T", bdz_tok)
                i = s.activation(bdr[:, 0, :], pb[4][0:HH, 0:TGD], AF.Exp, scale=-1.0)
                S.inc("Sc", i)
                i = s.activation(bdr[:, 1, :], pb[4][0:HH, 256:512], AF.Exp,
                                 bias=C["dtb"][:, :], scale=1.0)
                S.inc("Sc", i)
                S.wait(s, "Sc")
                i = s.activation(bdr[:, 1, :], bdr[:, 1, :], AF.Ln, bias=C['fc'][0:HH, 1:2], scale=1.0)
                S.inc("Sc", i)
            bdact_tok = S.tok("Sc")

            @blk.vector
            def _(v):
                S.wait(v, "Sc", bdact_tok)
                i = v.tensor_scalar_add(bdr[:, 0, :], bdr[:, 0, :], 1.0)
                S.inc("V", i)
                S.wait(v, "V")
                i = v.reciprocal(bdr[:, 0, :], bdr[:, 0, :])
                S.inc("V", i)
                i = v.tensor_scalar_mul(bdr[:, 1, :], bdr[:, 1, :], C["nea"][:, :])
                S.inc("V", i)
            bdrows_tok = S.tok("V")

            @blk.tensor
            def _(t):
                S.wait(t, "V", bdrows_tok)
                for cl in range(CPG):
                    t.transpose(pb[5][:, cl * 128:cl * 128 + 8],
                                bdr[:, 0, cl * 128:(cl + 1) * 128],
                                C["identf"][0:HH, 0:HH])
                    i = t.transpose(pb[5][:, 256 + cl * 128:256 + cl * 128 + 8],
                                    bdr[:, 1, cl * 128:(cl + 1) * 128],
                                    C["identf"][0:HH, 0:HH])
                S.inc("T", i)
            bdtr_tok = S.tok("T")

            @blk.vector
            def _(v):
                S.wait(v, "T", bdtr_tok)
                for cl in range(CPG):
                    v.tensor_copy(beta_cm[:, cl * HH:(cl + 1) * HH],
                                  pb[5][:, cl * 128:cl * 128 + 8])
                    i = v.tensor_copy(dec_cm[:, cl * HH:(cl + 1) * HH],
                                      pb[5][:, 256 + cl * 128:256 + cl * 128 + 8])
                S.inc("V", i)
            bdcm_tok = S.tok("V")

            @blk.tensor
            def _(t):
                S.wait(t, "V", bdcm_tok)
                i = t.matmul(pb[5][:, 0:2 * HH], C["triuF"][:, :], dec_cm[:, :],
                             start=True, stop=True)
                S.inc("T", i)
            cum_tok = S.tok("T")

            @blk.vector
            def _(v):
                S.wait(v, "T", cum_tok)
                i = v.tensor_copy(dec_cum[:, :], pb[5][:, 0:2 * HH])
                S.inc("V", i)
            deccum_tok = S.tok("V")

            @blk.tensor
            def _(t):
                S.wait(t, "V", deccum_tok)
                i = t.matmul(pb[5][0:1, 64:64 + 2 * HH], C["e127"][:, :], dec_cum[:, :],
                             start=True, stop=True)
                S.inc("T", i)
            dlast_tok = S.tok("T")

            @blk.scalar
            def _(s):
                S.wait(s, "T", dlast_tok)
                i = s.copy(dlast[:, :], pb[5][0:1, 64:64 + 2 * HH])
                S.inc("Sc", i)
                i = s.activation(elsr[:, :], pb[5][0:1, 64:64 + 2 * HH], AF.Exp)
                S.inc("Sc", i)
            dlrow_tok = S.tok("Sc")

            @blk.tensor
            def _(t):
                S.wait(t, "Sc", dlrow_tok)
                t.matmul(pb[5][:, 128:128 + 2 * HH], C["ones_row"][:, :], elsr[:, :],
                         start=True, stop=True)
                i = t.matmul(pb[5][:, 192:192 + 2 * HH], C["ones_row"][:, :],
                             dlast[:, :], start=True, stop=True)
                S.inc("T", i)
            dbc_tok = S.tok("T")

            @blk.vector
            def _(v):
                S.wait(v, "T", dbc_tok)
                i = v.tensor_copy(esl[:, :], pb[5][:, 128:128 + 2 * HH])
                S.inc("V", i)
                i = v.tensor_sub(dwv[:, :], pb[5][:, 192:192 + 2 * HH], dec_cum[:, :])
                S.inc("V", i)
            dwsub_tok = S.tok("V")

            @blk.scalar
            def _(s):
                S.wait(s, "V", dwsub_tok)
                i = s.activation(dwv[:, :], dwv[:, :], AF.Exp)
                S.inc("Sc", i)
                i = s.activation(edec[:, :], dec_cum[:, :], AF.Exp)
                S.inc("Sc", i)
            dfam_sc = S.tok("Sc")

            @blk.vector
            def _(v):
                S.wait(v, "Sc", dfam_sc)
                i = v.tensor_scalar_mul(edk[:, :], edec[:, :], RSK)
                S.inc("V", i)
                i = v.tensor_mul(bev[:, :], beta_cm[:, :], edec[:, :])
                S.inc("V", i)
            dfam_v = S.tok("V")

            csl = []
            for dc in range(DCN):
                @blk.vector
                def _(v, dc=dc):
                    S.wait(v, "V", vproj_tok)
                    if len(csl) >= 2:
                        S.wait(v, "Sc", csl[-2])
                    buf = cvt[dc % 2]
                    i = v.tensor_scalar(buf[:, :], vpre[par][:, dc, 0:TGD],
                                        C["convw"][:, dc, 0:1], None, op0=ALU.mult)
                    S.inc("V", i)
                    for j in range(1, 4):
                        S.wait(v, "V")
                        i = v.scalar_tensor_tensor(
                            buf[:, :], vpre[par][:, dc, j:j + TGD],
                            C["convw"][:, dc, j:j + 1], buf[:, :],
                            op0=ALU.mult, op1=ALU.add)
                        S.inc("V", i)
                cvtok = S.tok("V")

                @blk.scalar
                def _(s, dc=dc, cvtok=cvtok):
                    S.wait(s, "V", cvtok)
                    i = s.activation(vfm[:, dc, :], cvt[dc % 2][:, :], AF.Silu,
                                     bias=C["convb"][:, dc:dc + 1])
                    S.inc("Sc", i)
                csl.append(S.tok("Sc"))
            vsilu_tok = S.tok("Sc")

            tmev = []
            for kind in range(3):
                for tt in range(2):
                    bank = px[tt]

                    @blk.tensor
                    def _(t, kind=kind, tt=tt, bank=bank):
                        if kind == 0:
                            S.wait(t, "Sc", vsilu_tok)
                        elif kind == 1:
                            S.wait(t, "V", gnorm_tok)
                        else:
                            S.wait(t, "Sc", max(kmj_tok, halo_tok))
                        if len(tmev) >= 2:
                            S.wait(t, "V", tmev[-2])
                        for h in range(HH):
                            if kind == 0:
                                src = vfm[:, h, tt * 128:(tt + 1) * 128]
                            elif kind == 1:
                                src = gfm[:, h, tt * 128:(tt + 1) * 128]
                            else:
                                src = kmj[par][:, h, tt * 128:tt * 128 + 128]
                            i = t.transpose(bank[:, h * 128:(h + 1) * 128], src,
                                            C["identb"][:, :])
                        S.inc("T", i)
                    trt = S.tok("T")

                    @blk.vector
                    def _(v, kind=kind, tt=tt, bank=bank, trt=trt):
                        S.wait(v, "T", trt)
                        dst = [vtm, gtm, wktm][kind]
                        i = v.tensor_copy(
                            dst[par][:, tt, :, :],
                            bank[:, :].rearrange("p (c k) -> p c k", c=HH))
                        S.inc("V", i)
                    tmev.append(S.tok("V"))
            tm_tok = S.tok("V")

            pre = []
            for h in range(HH):
                cols = slice(h, 2 * HH, HH)
                startV = pre[-1]["endV"] if pre else max(tm_tok, dfam_v)
                startSc = pre[-1]["endSc"] if pre else S.tok("Sc")

                @blk.tensor
                def _(t, h=h, startV=startV, startSc=startSc):
                    S.wait(t, "V", max(startV, tm_tok, dfam_v))
                    S.wait(t, "Sc", startSc)
                    for cl in range(CPG):
                        i = t.transpose(
                            pb[3][:, cl * 128:(cl + 1) * 128],
                            dec_cum[:, cl * HH + h:cl * HH + h + 1].broadcast_to(
                                [128, 128]),
                            C["identf"][:, :])
                    S.inc("T", i)
                mtr_tok = S.tok("T")

                @blk.vector
                def _(v, cols=cols, mtr_tok=mtr_tok):
                    S.wait(v, "T", mtr_tok)
                    i = v.tensor_tensor(mt1[:, :, :], bc2(dec_cum[:, cols], CPG),
                                        pb[3][:, 0:256].rearrange(
                                            "p (c k) -> p c k", c=CPG),
                                        op=ALU.subtract)
                    S.inc("V", i)
                    S.wait(v, "V")
                    i = v.tensor_mul(mt2[:, :, :], mt1[:, :, :], mbc(C["ntriP"], CPG))
                    S.inc("V", i)
                    i = v.tensor_mul(mt1[:, :, :], mt1[:, :, :], mbc(C["trilP"], CPG))
                    S.inc("V", i)
                mmul_tok = S.tok("V")

                @blk.scalar
                def _(s, mmul_tok=mmul_tok):
                    S.wait(s, "V", mmul_tok)
                    i = s.activation(mt2[:, :, :], mt2[:, :, :], AF.Exp)
                    S.inc("Sc", i)
                    i = s.activation(mt1[:, :, :], mt1[:, :, :], AF.Exp)
                    S.inc("Sc", i)
                mexp_tok = S.tok("Sc")

                @blk.vector
                def _(v, cols=cols, mexp_tok=mexp_tok):
                    S.wait(v, "Sc", mexp_tok)
                    i = v.tensor_mul(ltk[:, :, :], mt2[:, :, :], mbc(C["triK"], CPG))
                    S.inc("V", i)
                    i = v.tensor_mul(lsnB[:, :, :], mt1[:, :, :], mbc(C["strN"], CPG))
                    S.inc("V", i)
                    S.wait(v, "V")
                    i = v.tensor_tensor(lsnB[:, :, :], lsnB[:, :, :],
                                        bc2(beta_cm[:, cols], CPG), op=ALU.mult)
                    S.inc("V", i)
                mask_V = S.tok("V")

                @blk.tensor
                def _(t, h=h):
                    S.wait(t, "Sc", bdact_tok)
                    for cl in range(CPG):
                        wkk = kmj[par][:, h, cl * 128:cl * 128 + 128]
                        i = t.matmul(pb[4][:, cl * 128:(cl + 1) * 128], wkk, wkk,
                                     start=True, stop=True)
                    S.inc("T", i)
                raw_tok = S.tok("T")

                @blk.vector
                def _(v, raw_tok=raw_tok, mask_V=mask_V):
                    S.wait(v, "T", raw_tok)
                    S.wait(v, "V", mask_V)
                    i = v.tensor_tensor(abuf[:, :, :],
                                        pb[4][:, 0:256].rearrange(
                                            "p (c k) -> p c k", c=CPG),
                                        lsnB[:, :, :], op=ALU.mult)
                    S.inc("V", i)
                A_V = S.tok("V")

                @blk.tensor
                def _(t, A_V=A_V):
                    S.wait(t, "V", A_V)
                    for cl in range(CPG):
                        i = t.transpose(px[0][:, cl * 128:(cl + 1) * 128],
                                        abuf[:, cl, :], C["identb"][:, :])
                    S.inc("T", i)
                at_tok = S.tok("T")

                @blk.scalar
                def _(s, at_tok=at_tok):
                    S.wait(s, "T", at_tok)
                    i = s.copy(atbuf[:, :, :],
                               px[0][:, 0:256].rearrange("p (c k) -> p c k", c=CPG))
                    S.inc("Sc", i)
                atev_tok = S.tok("Sc")

                @blk.vector
                def _(v, atev_tok=atev_tok):
                    S.wait(v, "Sc", atev_tok)
                    i = v.tensor_tensor(gch[:, :, :], atbuf[:, :, :],
                                        mbc(C["identb"], CPG), op=ALU.add)
                    S.inc("V", i)
                g_tok = S.tok("V")

                cur, curT, cur_tok = abuf, atbuf, atev_tok
                for lvl in range(1, 7):
                    @blk.tensor
                    def _(t, cur=cur, curT=curT, lvl=lvl, cur_tok=cur_tok,
                          g_tok=g_tok):
                        S.wait(t, "Sc", cur_tok)
                        S.wait(t, "V", g_tok)
                        for cl in range(CPG):
                            i = t.matmul(pb[4][:, 256 + cl * 128:256 + (cl + 1) * 128],
                                         curT[:, cl, :], cur[:, cl, :],
                                         start=True, stop=True)
                        if lvl < 6:
                            for cl in range(CPG):
                                i = t.matmul(pb[4][:, cl * 128:(cl + 1) * 128],
                                             cur[:, cl, :], curT[:, cl, :],
                                             start=True, stop=True)
                        S.inc("T", i)
                    pw_tok = S.tok("T")

                    @blk.scalar
                    def _(s, lvl=lvl, pw_tok=pw_tok):
                        S.wait(s, "T", pw_tok)
                        i = s.copy(pw[lvl % 2][:, :, :],
                                   pb[4][:, 256:512].rearrange(
                                       "p (c k) -> p c k", c=CPG))
                        S.inc("Sc", i)
                        if lvl < 6:
                            i = s.copy(pwT[lvl % 2][:, :, :],
                                       pb[4][:, 0:256].rearrange(
                                           "p (c k) -> p c k", c=CPG))
                            S.inc("Sc", i)
                    pwev_tok = S.tok("Sc")

                    @blk.tensor
                    def _(t, lvl=lvl, pwev_tok=pwev_tok):
                        S.wait(t, "Sc", pwev_tok)
                        for cl in range(CPG):
                            i = t.matmul(pb[5][:, 256 + cl * 128:256 + (cl + 1) * 128],
                                         pw[lvl % 2][:, cl, :], gch[:, cl, :],
                                         start=True, stop=True)
                        S.inc("T", i)
                    gm_tok = S.tok("T")

                    @blk.vector
                    def _(v, gm_tok=gm_tok):
                        S.wait(v, "T", gm_tok)
                        i = v.tensor_tensor(
                            gch[:, :, :], gch[:, :, :],
                            pb[5][:, 256:512].rearrange("p (c k) -> p c k", c=CPG),
                            op=ALU.add)
                        S.inc("V", i)
                    g_tok = S.tok("V")
                    cur, curT, cur_tok = pw[lvl % 2], pwT[lvl % 2], pwev_tok

                @blk.vector
                def _(v, h=h, cols=cols, g_tok=g_tok):
                    S.wait(v, "V", g_tok)
                    i = v.tensor_tensor(vbb[:, :, :], vtm[par][:, :, h, :],
                                        bc2(beta_cm[:, cols], CPG), op=ALU.mult)
                    S.inc("V", i)
                    i = v.tensor_tensor(wkbe[:, :, :], wktm[par][:, :, h, :],
                                        bc2(bev[:, cols], CPG), op=ALU.mult)
                    S.inc("V", i)
                vbw_tok = S.tok("V")

                @blk.tensor
                def _(t, vbw_tok=vbw_tok):
                    S.wait(t, "V", vbw_tok)
                    for cl in range(CPG):
                        t.matmul(pb[4][:, cl * 128:(cl + 1) * 128],
                                 gch[:, cl, :], vbb[:, cl, :], start=True, stop=True)
                    for cl in range(CPG):
                        i = t.matmul(pb[4][:, 256 + cl * 128:256 + (cl + 1) * 128],
                                     gch[:, cl, :], wkbe[:, cl, :],
                                     start=True, stop=True)
                    S.inc("T", i)
                ap_tok = S.tok("T")

                @blk.scalar
                def _(s, h=h, ap_tok=ap_tok):
                    S.wait(s, "T", ap_tok)
                    i = s.copy(vpr[:, h, :, :],
                               pb[4][:, 0:256].rearrange("p (c k) -> p c k", c=CPG))
                    S.inc("Sc", i)
                    i = s.copy(wkc[:, :, :],
                               pb[4][:, 256:512].rearrange("p (c k) -> p c k", c=CPG))
                    S.inc("Sc", i)
                apev_tok = S.tok("Sc")

                @blk.tensor
                def _(t, apev_tok=apev_tok):
                    S.wait(t, "Sc", apev_tok)
                    for cl in range(CPG):
                        i = t.transpose(px[0][:, 256 + cl * 128:256 + (cl + 1) * 128],
                                        wkc[:, cl, :], C["identb"][:, :])
                    S.inc("T", i)
                wt_tok = S.tok("T")

                @blk.scalar
                def _(s, h=h, wt_tok=wt_tok):
                    S.wait(s, "T", wt_tok)
                    i = s.copy(wkcT[:, h, :, :],
                               px[0][:, 256:512].rearrange("p (c k) -> p c k", c=CPG))
                    S.inc("Sc", i)
                wtev_tok = S.tok("Sc")

                @blk.tensor
                def _(t, h=h):
                    for cl in range(CPG):
                        wkk = kmj[par][:, h, cl * 128:cl * 128 + 128]
                        rkk = kmj[par][:, h, cl * 128 + 1:cl * 128 + 129]
                        i = t.matmul(pb[3][:, cl * 128:(cl + 1) * 128], wkk, rkk,
                                     start=True, stop=True)
                    S.inc("T", i)
                att_tok = S.tok("T")

                @blk.vector
                def _(v, h=h, att_tok=att_tok, mask_V=mask_V):
                    S.wait(v, "T", att_tok)
                    S.wait(v, "V", mask_V)
                    i = v.tensor_tensor(atn[:, h, :, :],
                                        pb[3][:, 0:256].rearrange(
                                            "p (c k) -> p c k", c=CPG),
                                        ltk[:, :, :], op=ALU.mult)
                    S.inc("V", i)
                atn_tok = S.tok("V")

                pre.append(dict(endV=S.tok("V"), endSc=S.tok("Sc"),
                                apev=apev_tok, wkcT=wtev_tok, atn=atn_tok))

            pre_V = S.tok("V")
            pre_Sc = S.tok("Sc")

            last_V = pre_V
            last_Sc = pre_Sc
            for cl in range(CPG):
                for h in range(HH):
                    col = cl * HH + h

                    @blk.tensor
                    def _(t, h=h, cl=cl, last_V=last_V, last_Sc=last_Sc):
                        S.wait(t, "Sc", max(pre[h]["wkcT"], last_Sc))
                        S.wait(t, "V", max(pre[h]["atn"], last_V))
                        i = t.matmul(pb[5][:, 0:128], wkcT[:, h, cl, :], S16[:, h, :],
                                     start=True, stop=True)
                        S.inc("T", i)
                    vn_tok = S.tok("T")

                    @blk.vector
                    def _(v, h=h, cl=cl, vn_tok=vn_tok):
                        S.wait(v, "T", vn_tok)
                        i = v.tensor_tensor(vnew[:, :], vpr[:, h, cl, :],
                                            pb[5][:, 0:128], op=ALU.subtract)
                        S.inc("V", i)
                    vnew_tok = S.tok("V")

                    @blk.scalar
                    def _(s, h=h, cl=cl, col=col, vnew_tok=vnew_tok):
                        S.wait(s, "V", vnew_tok)
                        i = s.activation(wkdw[:, :], wktm[par][:, cl, h, :], AF.Copy,
                                         scale=dwv[:, col:col + 1])
                        S.inc("Sc", i)
                    wkdw_tok = S.tok("Sc")

                    @blk.tensor
                    def _(t, h=h, cl=cl, vnew_tok=vnew_tok, wkdw_tok=wkdw_tok):
                        S.wait(t, "V", vnew_tok)
                        S.wait(t, "Sc", wkdw_tok)
                        rkk = kmj[par][:, h, cl * 128 + 1:cl * 128 + 129]
                        t.matmul(pb[5][:, 128:256], rkk, S16[:, h, :],
                                 start=True, stop=True)
                        t.matmul(pb[5][:, 256:384], atn[:, h, cl, :], vnew[:, :],
                                 start=True, stop=True)
                        i = t.matmul(pb[5][:, 384:512], wkdw[:, :], vnew[:, :],
                                     start=True, stop=True)
                        S.inc("T", i)
                    smm_tok = S.tok("T")

                    @blk.scalar
                    def _(s, col=col, smm_tok=smm_tok):
                        S.wait(s, "T", smm_tok)
                        i = s.activation(ogt[:, 0:128], pb[5][:, 128:256], AF.Copy,
                                         scale=edk[:, col:col + 1])
                        S.inc("Sc", i)
                    o1_tok = S.tok("Sc")

                    @blk.vector
                    def _(v, h=h, cl=cl, col=col, o1_tok=o1_tok):
                        S.wait(v, "Sc", o1_tok)
                        i = v.tensor_tensor(o_grp[:, h, cl * 128:(cl + 1) * 128],
                                            ogt[:, 0:128], pb[5][:, 256:384],
                                            op=ALU.add)
                        S.inc("V", i)
                        i = v.scalar_tensor_tensor(
                            S_s[:, h, :], S_s[:, h, :], esl[:, col:col + 1],
                            pb[5][:, 384:512], op0=ALU.mult, op1=ALU.add)
                        S.inc("V", i)
                        S.wait(v, "V")
                        i = v.tensor_copy(S16[:, h, :], S_s[:, h, :])
                        S.inc("V", i)
                    last_V = S.tok("V")
                    last_Sc = S.tok("Sc")
            scan_V = last_V
            scan_Sc = last_Sc

            @blk.vector
            def _(v):
                S.wait(v, "V", scan_V)
                S.wait(v, "Sc", scan_Sc)
                for h in range(HH):
                    S.wait(v, "V")
                    i = v.tensor_mul(ogt[:, :], o_grp[:, h, :], o_grp[:, h, :])
                    S.inc("V", i)
                    S.wait(v, "V")
                    i = v.reduce_sum(rrs[:, h * 2:(h + 1) * 2],
                                     ogt[:, :].rearrange("p (c k) -> p c k", c=2),
                                     AX.X)
                    S.inc("V", i)
            rr_tok = S.tok("V")

            @blk.scalar
            def _(s):
                S.wait(s, "V", rr_tok)
                i = s.activation(rrs[:, :], rrs[:, :], AF.Ln, scale=1.0 / 128.0,
                                 bias=C['fc'][:, 2:3])
                S.inc("Sc", i)
                S.wait(s, "Sc")
                i = s.activation(rrs[:, :], rrs[:, :], AF.Exp, scale=-0.5)
                S.inc("Sc", i)
            rstd_tok = S.tok("Sc")

            half = g // (NGD // 2)
            tt0 = (2 * g) % NTT
            for h in range(HH):
                @blk.vector
                def _(v, h=h):
                    S.wait(v, "Sc", rstd_tok)
                    S.wait(v, "V")
                    i = v.tensor_tensor(
                        o_grp[:, h, :].rearrange("p (c k) -> p c k", c=2),
                        o_grp[:, h, :].rearrange("p (c k) -> p c k", c=2),
                        gtm[par][:, :, h, :], op=ALU.mult)
                    S.inc("V", i)
                    S.wait(v, "V")
                    i = v.tensor_tensor(
                        o_grp[:, h, :].rearrange("p (c k) -> p c k", c=2),
                        o_grp[:, h, :].rearrange("p (c k) -> p c k", c=2),
                        bc2(rrs[:, h * 2:(h + 1) * 2], 2), op=ALU.mult)
                    S.inc("V", i)
                    S.wait(v, "V")
                    i = v.scalar_tensor_tensor(
                        o_mine[:, h, tt0:tt0 + 2, :],
                        o_grp[:, h, :].rearrange("p (c k) -> p c k", c=2),
                        C["msel"][:, half:half + 1],
                        o_mine[:, h, tt0:tt0 + 2, :],
                        op0=ALU.mult, op1=ALU.add)
                    S.inc("V", i)

            prev = dict(x_V=l2n_tok, x_T=x_T_tok,
                        end_T=S.tok("T"), end_V=S.tok("V"), end_Sc=S.tok("Sc"))



def _p2(nc, blk, S, C, o_mine, out_acc, xn_s, wdT_s, pb, px, P, xh_d, woT_d):
    HT, NGL, TGL = P["HT"], P["NGL"], P["TGL"]
    with ExitStack() as ctx:
        sb = lambda name, shape, dt=F32: ctx.enter_context(nc.sbuf_tensor(name, shape, dt))
        woT_s = sb("woT_s", [128, DCN, D], BF16)
        xh_tm = sb("xh_tm", [128, 2, D])
        ofm = sb("ofm", [128, DCN, TGL], BF16)
        res1 = sb("res1", [128, DCN, TGL])
        sums = sb("sums", [1, 2 * TGL])
        stat = sb("stat", [1, 3 * TGL])
        lgr = sb("lgr", [8, TGL])
        lgt = sb("lgt", [128, 8])
        prb = sb("prb", [128, 8])
        srt8 = sb("srt8", [128, 8])
        mxb = sb("mxb", [128, 1])
        smr = sb("smr", [128, 1])
        sqv = sb("sqv", [128, TGL])

        @blk.gpsimd
        def _(g):
            i = g.dma_start(woT_s[:, :, :], woT_d[:, :, :])
            S.inc("dWO", i, 16)
        wo_tok = S.tok("dWO")

        prevg = {}
        for gl in range(NGL):
            @blk.sync
            def _(sy):
                if prevg:
                    S.wait(sy, "T", prevg["xh_T"])
                i = sy.dma_start(
                    xh_tm[:, :, :],
                    xh_d[gl * TGL:(gl + 1) * TGL, :].rearrange(
                        "(tt p) d -> p tt d", p=128))
                S.inc("dX", i, 16)
            xh_dma = S.tok("dX")

            for deco in (blk.tensor, blk.vector, blk.scalar):
                def f(eng):
                    S.wait(eng, "dX", xh_dma)
                    S.wait(eng, "dWO", wo_tok)
                    if prevg:
                        S.wait(eng, "T", prevg["end_T"])
                        S.wait(eng, "V", prevg["end_V"])
                        S.wait(eng, "Sc", prevg["end_Sc"])
                deco(f)

            tt0 = gl * (TGL // 128)
            for tt in range(2):
                bank = px[tt]

                @blk.tensor
                def _(t, tt=tt, bank=bank):
                    for h in range(HH):
                        i = t.transpose(bank[:, h * 128:(h + 1) * 128],
                                        o_mine[:, h, tt0 + tt, :], C["identb"][:, :])
                    S.inc("T", i)
                ot_ = S.tok("T")

                @blk.scalar
                def _(s, tt=tt, bank=bank, ot_=ot_):
                    S.wait(s, "T", ot_)
                    i = s.copy(ofm[:, :, tt * 128:(tt + 1) * 128],
                               bank[:, :].rearrange("p (c k) -> p c k", c=HH))
                    S.inc("Sc", i)
            ofm_tok = S.tok("Sc")

            rev = []
            for dv in range(DCN):
                bank = pb[dv % 2]
                bank2 = pb[2 + (dv % 2)]

                @blk.tensor
                def _(t, dv=dv, bank=bank, bank2=bank2):
                    S.wait(t, "Sc", ofm_tok)
                    if len(rev) >= 2:
                        S.wait(t, "V", rev[-2])
                    for tt in range(2):
                        t.transpose(bank2[:, tt * 128:(tt + 1) * 128],
                                    xh_tm[:, tt, dv * 128:(dv + 1) * 128],
                                    C["identf"][:, :])
                    for dc in range(DCN):
                        i = t.matmul(bank[:, 0:TGL],
                                     woT_s[:, dc, dv * 128:(dv + 1) * 128],
                                     ofm[:, dc, :],
                                     start=(dc == 0), stop=(dc == DCN - 1))
                    S.inc("T", i)
                rt = S.tok("T")

                @blk.scalar
                def _(s, dv=dv, bank2=bank2, rt=rt):
                    S.wait(s, "T", rt)
                    i = s.copy(res1[:, dv, :], bank2[:, 0:TGL])
                    S.inc("Sc", i)
                sev = S.tok("Sc")

                @blk.vector
                def _(v, dv=dv, bank=bank, sev=sev):
                    S.wait(v, "Sc", sev)
                    i = v.tensor_tensor(res1[:, dv, :], res1[:, dv, :], bank[:, 0:TGL],
                                        op=ALU.add)
                    S.inc("V", i)
                rev.append(S.tok("V"))
            res1_tok = S.tok("V")
            xh_T_tok = S.tok("T")

            @blk.tensor
            def _(t):
                S.wait(t, "V", res1_tok)
                for dc in range(DCN):
                    i = t.matmul(pb[4][0:1, 0:TGL], C["ones_colf"][:, :],
                                 res1[:, dc, :],
                                 start=(dc == 0), stop=(dc == DCN - 1))
                S.inc("T", i)
            sum_tok = S.tok("T")

            sq_T = sum_tok
            for dc in range(DCN):
                @blk.vector
                def _(v, dc=dc, sq_T=sq_T):
                    S.wait(v, "V", res1_tok)
                    S.wait(v, "T", sq_T)
                    i = v.tensor_mul(sqv[:, :], res1[:, dc, :], res1[:, dc, :])
                    S.inc("V", i)
                vq = S.tok("V")

                @blk.tensor
                def _(t, dc=dc, vq=vq):
                    S.wait(t, "V", vq)
                    i = t.matmul(pb[5][0:1, 0:TGL], C["ones_colf"][:, :], sqv[:, :],
                                 start=(dc == 0), stop=(dc == DCN - 1))
                    S.inc("T", i)
                sq_T = S.tok("T")
            sumsq_tok = sq_T

            @blk.scalar
            def _(s):
                S.wait(s, "T", sumsq_tok)
                i = s.mul(sums[:, 0:TGL], pb[4][0:1, 0:TGL], 1.0 / D)
                S.inc("Sc", i)
                i = s.mul(sums[:, TGL:2 * TGL], pb[5][0:1, 0:TGL], 1.0 / D)
                S.inc("Sc", i)
            mv_tok = S.tok("Sc")

            @blk.vector
            def _(v):
                S.wait(v, "Sc", mv_tok)
                i = v.tensor_mul(stat[:, 0:TGL], sums[:, 0:TGL], sums[:, 0:TGL])
                S.inc("V", i)
                S.wait(v, "V")
                i = v.tensor_sub(stat[:, 0:TGL], sums[:, TGL:2 * TGL], stat[:, 0:TGL])
                S.inc("V", i)
            var_tok = S.tok("V")

            @blk.scalar
            def _(s):
                S.wait(s, "V", var_tok)
                i = s.activation(stat[:, TGL:2 * TGL], stat[:, 0:TGL], AF.Ln,
                                 bias=C['fc'][0:1, 3:4])
                S.inc("Sc", i)
                S.wait(s, "Sc")
                i = s.activation(stat[:, TGL:2 * TGL], stat[:, TGL:2 * TGL], AF.Exp,
                                 scale=-0.5)
                S.inc("Sc", i)
            rstd_tok = S.tok("Sc")

            @blk.vector
            def _(v):
                S.wait(v, "Sc", rstd_tok)
                i = v.tensor_mul(stat[:, 2 * TGL:3 * TGL], sums[:, 0:TGL],
                                 stat[:, TGL:2 * TGL])
                S.inc("V", i)
            mr_tok = S.tok("V")

            @blk.tensor
            def _(t):
                S.wait(t, "V", mr_tok)
                t.matmul(pb[5][:, 0:TGL], C["ones_row"][:, :], stat[:, TGL:2 * TGL],
                         start=True, stop=True)
                i = t.matmul(pb[4][:, 0:TGL], C["ones_row"][:, :],
                             stat[:, 2 * TGL:3 * TGL], start=True, stop=True)
                S.inc("T", i)
            bcast_tok = S.tok("T")

            for dc in range(DCN):
                @blk.vector
                def _(v, dc=dc):
                    S.wait(v, "T", bcast_tok)
                    i = v.tensor_tensor(res1[:, dc, :], res1[:, dc, :],
                                        pb[5][:, 0:TGL], op=ALU.mult)
                    S.inc("V", i)
                    S.wait(v, "V")
                    i = v.tensor_tensor(res1[:, dc, :], res1[:, dc, :],
                                        pb[4][:, 0:TGL], op=ALU.subtract)
                    S.inc("V", i)
                    S.wait(v, "V")
                    i = v.tensor_scalar_mul(
                        out_acc[:, dc, gl * TGL:(gl + 1) * TGL], res1[:, dc, :], 2.0)
                    S.inc("V", i)
                    i = v.tensor_scalar(
                        xn_s[:, dc, gl * TGL:(gl + 1) * TGL], res1[:, dc, :],
                        C["gam"][:, dc:dc + 1], C["betf"][:, dc:dc + 1],
                        op0=ALU.mult, op1=ALU.add)
                    S.inc("V", i)
            apply_tok = S.tok("V")

            @blk.tensor
            def _(t):
                S.wait(t, "V", apply_tok)
                for dc in range(DCN):
                    i = t.matmul(pb[4][0:E, 0:TGL], C["rwg"][:, dc, :],
                                 res1[:, dc, :],
                                 start=(dc == 0), stop=(dc == DCN - 1))
                S.inc("T", i)
            lg_tok = S.tok("T")

            @blk.vector
            def _(v):
                S.wait(v, "T", lg_tok)
                i = v.tensor_scalar_add(lgr[:, :], pb[4][0:E, 0:TGL],
                                        C["rbc"][:, :])
                S.inc("V", i)
            lgr_tok = S.tok("V")

            wtoks = []
            for tt in range(2):
                @blk.tensor
                def _(t, tt=tt, lgr_tok=lgr_tok):
                    S.wait(t, "V", lgr_tok)
                    if wtoks:
                        S.wait(t, "V", wtoks[-1])
                    i = t.transpose(pb[3][0:128, tt * 128:tt * 128 + 8],
                                    lgr[:, tt * 128:(tt + 1) * 128],
                                    C["identf"][0:E, 0:E])
                    S.inc("T", i)
                ltr = S.tok("T")

                @blk.vector
                def _(v, tt=tt, ltr=ltr):
                    S.wait(v, "T", ltr)
                    i = v.tensor_copy(lgt[:, :], pb[3][:, tt * 128:tt * 128 + 8])
                    S.inc("V", i)
                    S.wait(v, "V")
                    i = v.reduce_max(mxb[:, :], lgt[:, :], AX.X)
                    S.inc("V", i)
                    S.wait(v, "V")
                    i = v.tensor_scalar_mul(mxb[:, :], mxb[:, :], -1.0 / TEMP)
                    S.inc("V", i)
                mx_tok = S.tok("V")

                @blk.scalar
                def _(s, mx_tok=mx_tok):
                    S.wait(s, "V", mx_tok)
                    i = s.activation(prb[:, :], lgt[:, :], AF.Exp,
                                     scale=1.0 / TEMP, bias=mxb[:, :])
                    S.inc("Sc", i)
                ex_tok = S.tok("Sc")

                @blk.vector
                def _(v, ex_tok=ex_tok):
                    S.wait(v, "Sc", ex_tok)
                    i = v.reduce_sum(smr[:, :], prb[:, :], AX.X)
                    S.inc("V", i)
                    S.wait(v, "V")
                    i = v.reciprocal(smr[:, :], smr[:, :])
                    S.inc("V", i)
                    S.wait(v, "V")
                    i = v.tensor_scalar_mul(prb[:, :], prb[:, :], smr[:, :])
                    S.inc("V", i)
                    S.wait(v, "V")
                    i = v.max(out=srt8[:, :], in_=prb[:, :])
                    S.inc("V", i)
                    S.wait(v, "V")
                    i = v.tensor_scalar(srt8[:, :], prb[:, :], srt8[:, 1:2], None,
                                        op0=ALU.is_ge)
                    S.inc("V", i)
                    S.wait(v, "V")
                    i = v.tensor_mul(srt8[:, :], srt8[:, :], prb[:, :])
                    S.inc("V", i)
                wd_tok = S.tok("V")

                @blk.tensor
                def _(t, tt=tt, wd_tok=wd_tok):
                    S.wait(t, "V", wd_tok)
                    i = t.transpose(pb[3][0:8, 256 + tt * 128:256 + (tt + 1) * 128],
                                    srt8[:, :], C["identf"][:, :])
                    S.inc("T", i)
                wtr = S.tok("T")

                @blk.vector
                def _(v, tt=tt, wtr=wtr):
                    S.wait(v, "T", wtr)
                    i = v.tensor_copy(
                        wdT_s[:, gl * TGL + tt * 128:gl * TGL + (tt + 1) * 128],
                        pb[3][0:8, 256 + tt * 128:256 + (tt + 1) * 128])
                    S.inc("V", i)
                wtoks.append(S.tok("V"))

            prevg = dict(xh_T=xh_T_tok, end_T=S.tok("T"), end_V=S.tok("V"),
                         end_Sc=S.tok("Sc"))



def _p3(nc, blk, S, C, out_acc, xn_s, wdT_s, pb, P, w1_d, w2_d, dbg=None):
    HT, NGM, TGM = P["HT"], P["NGM"], P["TGM"]
    NE_ = E
    if P.get("p3lim"):
        NE_, NGM = P["p3lim"]
    NF = FF // 128
    with ExitStack() as ctx:
        sb = lambda name, shape, dt=F32: ctx.enter_context(nc.sbuf_tensor(name, shape, dt))
        w1_s = sb("w1s", [128, DCN, FF], BF16)
        w2_s = [sb(f"w2s{i}", [128, 2, D], BF16) for i in range(2)]
        hbuf = sb("hbuf", [128, 4, TGM], BF16)
        wbc = sb("wbc", [128, TGM])
        ytmp = sb("ytmp", [128, TGM])

        w1_tok = [[0] * 8 for _ in range(E)]
        w2_dma = [0, 0]
        w2_rd = [0, 0]
        ring_rd = [0, 0, 0, 0]
        last_y_read = 0
        mm1_ev = {}
        prev_tile_sil = [0]
        prev_tile_wbc = [0]

        @blk.gpsimd
        def _(g):
            for pc in range(8):
                i = g.dma_start(w1_s[:, :, pc * 512:(pc + 1) * 512],
                                w1_d[0, :, :, pc * 512:(pc + 1) * 512])
                w1_tok[0][pc] = S.inc(f"dW1p{pc}", i, 16)

        for e in range(NE_):
            for tg in range(NGM):
                t0 = tg * TGM
                prev_sil = mm1_ev.get("last", 0) if isinstance(mm1_ev, dict) else 0

                @blk.tensor
                def _(t, e=e, t0=t0):
                    S.wait(t, "Sc", prev_tile_sil[0])
                    S.wait(t, "V", prev_tile_wbc[0])
                    i = t.matmul(pb[0][:, 256:256 + TGM],
                                 C["e8c"][:, e:e + 1].broadcast_to([8, 128]),
                                 wdT_s[:, t0:t0 + TGM], start=True, stop=True)
                    S.inc("T", i)
                wb_tok = S.tok("T")

                @blk.vector
                def _(v, wb_tok=wb_tok):
                    S.wait(v, "T", wb_tok)
                    i = v.tensor_copy(wbc[:, :], pb[0][:, 256:256 + TGM])
                    S.inc("V", i)
                wbc_tok = S.tok("V")
                prev_tile_wbc[0] = wbc_tok

                mm1_ev = {}

                def emit_mm1(f, e=e, t0=t0, wbc_tok=wbc_tok):
                    @blk.tensor
                    def _(t, f=f):
                        S.wait(t, f"dW1p{f // 4}", w1_tok[e][f // 4])
                        if f >= 2:
                            S.wait(t, "Sc", mm1_ev[f - 2])
                        if f == 0:
                            S.wait(t, "V", wbc_tok)
                        bank = pb[0] if f % 2 == 0 else pb[5]
                        for dc in range(DCN):
                            i = t.matmul(bank[:, 0:TGM],
                                         w1_s[:, dc, f * 128:(f + 1) * 128],
                                         xn_s[:, dc, t0:t0 + TGM],
                                         start=(dc == 0), stop=(dc == DCN - 1))
                        S.inc("T", i)
                    m1_tok = S.tok("T")
                    slot = f % 4

                    @blk.scalar
                    def _(s, f=f, slot=slot, m1_tok=m1_tok, rr=ring_rd[slot]):
                        S.wait(s, "T", m1_tok)
                        S.wait(s, "T", rr)
                        bank = pb[0] if f % 2 == 0 else pb[5]
                        i = s.activation(hbuf[:, slot, :], bank[:, 0:TGM], AF.Silu)
                        S.inc("Sc", i)
                    mm1_ev[f] = S.tok("Sc")

                def emit_w2dma(f, e=e):
                    ws = (f // 2) % 2

                    @blk.gpsimd
                    def _(g, e=e, f=f, ws=ws, wr=w2_rd[ws]):
                        S.wait(g, "T", wr)
                        i = g.dma_start(
                            w2_s[ws][:, :, :],
                            w2_d[e, f * 128:(f + 2) * 128, :].rearrange(
                                "(c p) d -> p c d", p=128))
                        w2_dma[ws] = S.inc(f"dW2s{ws}", i, 16)

                def emit_mm2(f, ly=last_y_read):
                    slot = f % 4

                    @blk.tensor
                    def _(t, f=f, slot=slot, ly=ly):
                        S.wait(t, "Sc", mm1_ev[f])
                        if f == 0:
                            S.wait(t, "V", ly)
                        S.wait(t, f"dW2s{(f // 2) % 2}", w2_dma[(f // 2) % 2])
                        wsl = w2_s[(f // 2) % 2]
                        for dc in range(DCN):
                            i = t.matmul(
                                pb[1 + dc // 2][:, (dc % 2) * 256:(dc % 2) * 256 + TGM],
                                wsl[:, f % 2, dc * 128:(dc + 1) * 128],
                                hbuf[:, slot, :],
                                start=(f == 0 and dc % 2 == 0),
                                stop=(f == NF - 1 and dc % 2 == 1))
                        S.inc("T", i)
                    ring_rd[slot] = S.tok("T")
                    w2_rd[(f // 2) % 2] = S.tok("T")

                emit_w2dma(0)
                emit_mm1(0)
                emit_mm1(1)
                for f in range(NF):
                    if f + 2 < NF and (f + 2) % 2 == 0:
                        emit_w2dma(f + 2)
                    emit_mm2(f)
                    if f + 2 < NF:
                        emit_mm1(f + 2)
                prev_tile_sil[0] = mm1_ev[NF - 1]
                mm2_done = S.tok("T")

                for dc in range(DCN):
                    @blk.vector
                    def _(v, dc=dc, t0=t0, mm2_done=mm2_done, wbc_tok=wbc_tok):
                        S.wait(v, "T", mm2_done)
                        S.wait(v, "V")
                        i = v.tensor_tensor(
                            ytmp[:, :],
                            pb[1 + dc // 2][:, (dc % 2) * 256:(dc % 2) * 256 + TGM],
                            wbc[:, :], op=ALU.mult)
                        S.inc("V", i)
                        S.wait(v, "V")
                        i = v.tensor_tensor(out_acc[:, dc, t0:t0 + TGM],
                                            out_acc[:, dc, t0:t0 + TGM],
                                            ytmp[:, :], op=ALU.add)
                        S.inc("V", i)
                last_y_read = S.tok("V")

                if tg == NGM - 1 and e + 1 < NE_:
                    @blk.gpsimd
                    def _(g, e=e, mm2_done=mm2_done):
                        S.wait(g, "T", mm2_done)
                        for pc in range(8):
                            i = g.dma_start(
                                w1_s[:, :, pc * 512:(pc + 1) * 512],
                                w1_d[e + 1, :, :, pc * 512:(pc + 1) * 512])
                            w1_tok[e + 1][pc] = S.inc(f"dW1p{pc}", i, 16)



def _p4(nc, blk, S, C, out_acc, pb, P, y_d):
    HT = P["HT"]
    TG = min(512, HT)
    NG = HT // TG
    with ExitStack() as ctx:
        sb = lambda name, shape, dt=F32: ctx.enter_context(nc.sbuf_tensor(name, shape, dt))
        sums = sb("sums4", [1, 2 * TG])
        stat = sb("stat4", [1, 3 * TG])
        sqv = sb("sqv4", [128, TG])
        stg = [sb(f"stg4{i}", [128, D]) for i in range(2)]

        stg_st = [0, 0]
        prevo = {}
        for g in range(NG):
            t0 = g * TG

            @blk.tensor
            def _(t):
                if prevo:
                    S.wait(t, "T", prevo["end_T"])
                    S.wait(t, "V", prevo["end_V"])
                    S.wait(t, "Sc", prevo["end_Sc"])
                for dc in range(DCN):
                    i = t.matmul(pb[4][0:1, 0:TG], C["ones_colf"][:, :],
                                 out_acc[:, dc, t0:t0 + TG],
                                 start=(dc == 0), stop=(dc == DCN - 1))
                S.inc("T", i)
            sum_tok = S.tok("T")

            sq_T = sum_tok
            for dc in range(DCN):
                @blk.vector
                def _(v, dc=dc, sq_T=sq_T):
                    if prevo:
                        S.wait(v, "T", prevo["end_T"])
                    S.wait(v, "T", sq_T)
                    i = v.tensor_mul(sqv[:, :], out_acc[:, dc, t0:t0 + TG],
                                     out_acc[:, dc, t0:t0 + TG])
                    S.inc("V", i)
                vq = S.tok("V")

                @blk.tensor
                def _(t, dc=dc, vq=vq):
                    S.wait(t, "V", vq)
                    i = t.matmul(pb[5][0:1, 0:TG], C["ones_colf"][:, :], sqv[:, :],
                                 start=(dc == 0), stop=(dc == DCN - 1))
                    S.inc("T", i)
                sq_T = S.tok("T")
            sumsq_tok = sq_T

            @blk.scalar
            def _(s):
                if prevo:
                    S.wait(s, "Sc", prevo["end_Sc"])
                S.wait(s, "T", sumsq_tok)
                i = s.mul(sums[:, 0:TG], pb[4][0:1, 0:TG], 1.0 / D)
                S.inc("Sc", i)
                i = s.mul(sums[:, TG:2 * TG], pb[5][0:1, 0:TG], 1.0 / D)
                S.inc("Sc", i)
            mv_tok = S.tok("Sc")

            @blk.vector
            def _(v):
                S.wait(v, "Sc", mv_tok)
                i = v.tensor_mul(stat[:, 0:TG], sums[:, 0:TG], sums[:, 0:TG])
                S.inc("V", i)
                S.wait(v, "V")
                i = v.tensor_sub(stat[:, 0:TG], sums[:, TG:2 * TG], stat[:, 0:TG])
                S.inc("V", i)
            var_tok = S.tok("V")

            @blk.scalar
            def _(s):
                S.wait(s, "V", var_tok)
                i = s.activation(stat[:, TG:2 * TG], stat[:, 0:TG], AF.Ln, bias=C['fc'][0:1, 3:4])
                S.inc("Sc", i)
                S.wait(s, "Sc")
                i = s.activation(stat[:, TG:2 * TG], stat[:, TG:2 * TG], AF.Exp,
                                 scale=-0.5)
                S.inc("Sc", i)
            rstd_tok = S.tok("Sc")

            @blk.vector
            def _(v):
                S.wait(v, "Sc", rstd_tok)
                i = v.tensor_mul(stat[:, 2 * TG:3 * TG], sums[:, 0:TG],
                                 stat[:, TG:2 * TG])
                S.inc("V", i)
            mr_tok = S.tok("V")

            @blk.tensor
            def _(t):
                S.wait(t, "V", mr_tok)
                t.matmul(pb[5][:, 0:TG], C["ones_row"][:, :], stat[:, TG:2 * TG],
                         start=True, stop=True)
                i = t.matmul(pb[4][:, 0:TG], C["ones_row"][:, :],
                             stat[:, 2 * TG:3 * TG], start=True, stop=True)
                S.inc("T", i)
            bc_tok = S.tok("T")

            for dc in range(DCN):
                @blk.vector
                def _(v, dc=dc):
                    S.wait(v, "T", bc_tok)
                    i = v.tensor_tensor(out_acc[:, dc, t0:t0 + TG],
                                        out_acc[:, dc, t0:t0 + TG],
                                        pb[5][:, 0:TG], op=ALU.mult)
                    S.inc("V", i)
                    S.wait(v, "V")
                    i = v.tensor_tensor(out_acc[:, dc, t0:t0 + TG],
                                        out_acc[:, dc, t0:t0 + TG],
                                        pb[4][:, 0:TG], op=ALU.subtract)
                    S.inc("V", i)
            nrm_tok = S.tok("V")

            prev_stv = 0
            for tt in range(TG // 128):
                sslot = tt % 2
                for hf in range(2):
                    bank = pb[hf]

                    @blk.tensor
                    def _(t, tt=tt, hf=hf, bank=bank, sslot=sslot,
                          prev_stv=prev_stv):
                        S.wait(t, "V", nrm_tok)
                        S.wait(t, "V", prev_stv)
                        S.wait(t, f"st{sslot}", stg_st[sslot])
                        for q in range(4):
                            dc = hf * 4 + q
                            i = t.transpose(
                                bank[:, q * 128:(q + 1) * 128],
                                out_acc[:, dc,
                                        t0 + tt * 128:t0 + tt * 128 + 128],
                                C["identf"][:, :])
                        S.inc("T", i)
                    tk = S.tok("T")

                    @blk.vector
                    def _(v, hf=hf, bank=bank, tk=tk, sslot=sslot):
                        S.wait(v, "T", tk)
                        S.wait(v, f"st{sslot}", stg_st[sslot])
                        i = v.tensor_copy(stg[sslot][:, hf * 512:(hf + 1) * 512],
                                          bank[:, 0:512])
                        S.inc("V", i)
                stv = S.tok("V")
                prev_stv = stv

                @blk.sync
                def _(sy, tt=tt, stv=stv, sslot=sslot):
                    S.wait(sy, "V", stv)
                    i = sy.dma_start(y_d[t0 + tt * 128:t0 + (tt + 1) * 128, :],
                                     stg[sslot][:, :])
                    S.inc(f"st{sslot}", i, 16)
                stg_st[sslot] = S.tok(f"st{sslot}")
            prevo = dict(end_T=S.tok("T"), end_V=S.tok("V"), end_Sc=S.tok("Sc"))



def kernel(L0_attn_x, L0_attn_Wv, L0_attn_Wg, L0_attn_Wo, L0_attn_Wb, L0_attn_Wa,
           L0_attn_dt_bias, L0_attn_A_log, L0_attn_norm_w, L0_attn_conv_w,
           L0_attn_conv_b, L0_ffn_gamma, L0_ffn_beta, L0_ffn_router_w,
           L0_ffn_W1, L0_ffn_W2):
    T = np.asarray(L0_attn_x).shape[1]
    HT = T // 2
    nc = build(T=T)

    cc = np.ascontiguousarray
    f32 = lambda a: np.asarray(a, np.float32)

    def chunkp(w):
        w = f32(w)
        return cc(w.reshape(DCN, 128, -1).transpose(1, 0, 2))

    wvT = chunkp(f32(L0_attn_Wv).T)
    wgT = chunkp(f32(L0_attn_Wg).T)
    woT = chunkp(f32(L0_attn_Wo).T)
    wbT = chunkp(f32(L0_attn_Wb).T)
    waT = chunkp(f32(L0_attn_Wa).T)
    rwT = chunkp(f32(L0_ffn_router_w).T)
    convw = chunkp(f32(L0_attn_conv_w))
    convb = cc(f32(L0_attn_conv_b).reshape(DCN, 128).T)
    gam = cc(f32(L0_ffn_gamma).reshape(DCN, 128).T)
    betf = cc(f32(L0_ffn_beta).reshape(DCN, 128).T)
    w1 = cc(f32(L0_ffn_W1).reshape(E, DCN, 128, FF).transpose(0, 2, 1, 3))
    w2 = cc(f32(L0_ffn_W2))
    dtb = cc(f32(L0_attn_dt_bias).reshape(HH, 1))
    alog = cc(f32(L0_attn_A_log).reshape(HH, 1))
    normw = cc(f32(L0_attn_norm_w).reshape(128, 1))
    x = f32(L0_attn_x)

    in_maps = []
    for c in range(8):
        b, half = c >> 1, c & 1
        msel = np.zeros((128, 2), np.float32)
        msel[:, half] = 1.0
        in_maps.append(dict(
            xb=cc(x[b]), xh=cc(x[b, half * HT:(half + 1) * HT]),
            wvT=wvT, wgT=wgT, woT=woT, wbT=wbT, waT=waT,
            dtb=dtb, alog=alog, normw=normw, convw=convw, convb=convb,
            gam=gam, betf=betf, rwT=rwT, w1=w1, w2=w2, msel=msel,
        ))

    res = run_bass_kernel_spmd(nc, in_maps, core_ids=list(range(8)))
    out = np.empty((B, T, D), np.float32)
    for c in range(8):
        b, half = c >> 1, c & 1
        out[b, half * HT:(half + 1) * HT] = np.asarray(res.results[c]["y"])
    return out


# version 04
# speedup vs baseline: 1.5372x; 1.0946x over previous
import numpy as np
from contextlib import ExitStack

import concourse.bass as bass
import concourse.mybir as mybir
from concourse.bass_utils import run_bass_kernel_spmd

F32 = mybir.dt.float32
BF16 = mybir.dt.bfloat16
AF = mybir.ActivationFunctionType
ALU = mybir.AluOpType
AX = mybir.AxisListType

B, D, HH, E, FF = 4, 1024, 8, 8, 4096
DCN = D // 128
EPS_L2, EPS_RMS, EPS_LN, TEMP = 1e-6, 1e-5, 1e-5, 1.5
RSK = float(1.0 / np.sqrt(128.0))


class Sems:
    def __init__(self, nc, ctx):
        self.nc, self.ctx, self.m = nc, ctx, {}

    def new(self, *names):
        for n in names:
            self.m[n] = [self.ctx.enter_context(self.nc.semaphore(n)), 0]

    def inc(self, name, instr, k=1):
        e = self.m[name]
        instr.then_inc(e[0], k)
        e[1] += k
        return e[1]

    def tok(self, name):
        return self.m[name][1]

    def wait(self, eng, name, tok=None):
        e = self.m[name]
        v = e[1] if tok is None else tok
        if v > 0:
            eng.wait_ge(e[0], v)

    def toks(self):
        return {k: v[1] for k, v in self.m.items()}


def bc2(ap2d, n, w=128):
    return ap2d.unsqueeze(2).broadcast_to([ap2d.shape[0], n, w])


def mbc(m, n):
    return m[:, :].unsqueeze(1).broadcast_to([128, n, 128])


def build(T=4096, debug=False, p3lim=None):
    TGD = 256
    NGD = T // TGD
    CPG = 2
    HT = T // 2
    NTT = HT // 128
    TGM = 256
    NGM = HT // TGM
    TGL = 256
    NGL = HT // TGL

    nc = bass.Bass("TRN2", target_bir_lowering=False,
                   dynamic_dma_scratch_size=16384)

    din = lambda name, shape, dt=F32: nc.dram_tensor(name, shape, dt, kind="ExternalInput")
    x_d = din("xb", [T, D])
    xh_d = din("xh", [HT, D])
    wvT_d = din("wvT", [128, DCN, D])
    wgT_d = din("wgT", [128, DCN, D])
    woT_d = din("woT", [128, DCN, D])
    wbT_d = din("wbT", [128, DCN, HH])
    waT_d = din("waT", [128, DCN, HH])
    dtb_d = din("dtb", [HH, 1])
    alog_d = din("alog", [HH, 1])
    normw_d = din("normw", [128, 1])
    convw_d = din("convw", [128, DCN, 4])
    convb_d = din("convb", [128, DCN])
    gam_d = din("gam", [128, DCN])
    betf_d = din("betf", [128, DCN])
    rwT_d = din("rwT", [128, DCN, E])
    w1_d = din("w1", [E, 128, DCN, FF])
    w2_d = din("w2", [E, FF, D])
    msel_d = din("msel", [128, 2])
    y_d = nc.dram_tensor("y", [HT, D], F32, kind="ExternalOutput")
    if debug:
        dbg_om = nc.dram_tensor("dbg_om", [128, HH * (HT // 128) * 128], BF16,
                                kind="ExternalOutput")
        dbg_x1 = nc.dram_tensor("dbg_x1", [128, DCN, HT], F32, kind="ExternalOutput")
        dbg_xn = nc.dram_tensor("dbg_xn", [128, DCN, HT], BF16, kind="ExternalOutput")
        dbg_wd = nc.dram_tensor("dbg_wd", [8, HT], F32, kind="ExternalOutput")
        dbg_acc = nc.dram_tensor("dbg_acc", [128, DCN, HT], F32, kind="ExternalOutput")
        dbg_acc0 = nc.dram_tensor("dbg_acc0", [128, DCN, HT], F32, kind="ExternalOutput")
        dbg_w2 = nc.dram_tensor("dbg_w2", [2, 128, 2, D], BF16, kind="ExternalOutput")
        dbg_h = nc.dram_tensor("dbg_h", [128, 4, TGM], BF16, kind="ExternalOutput")
        dbg_wbc = nc.dram_tensor("dbg_wbc", [128, TGM], F32, kind="ExternalOutput")
        dbg_ps = nc.dram_tensor("dbg_ps", [128, 2, TGM], F32, kind="ExternalOutput")

    with ExitStack() as ctx:
        S = Sems(nc, ctx)
        S.new("dX", "dS", "dW0", "dWVG", "dWO", "T", "V", "Sc", "G", "st0", "st1",
              *[f"dW1p{i}" for i in range(8)],
              "dW2s0", "dW2s1", "dW2s2", "dW2s3")
        sbR = lambda name, shape, dt=F32: ctx.enter_context(
            nc.sbuf_tensor(name, shape, dt, side="right"))

        identb = sbR("identb", [128, 128], BF16)
        identf = sbR("identf", [128, 128])
        ones_colf = sbR("ones_colf", [128, 1])
        ones_row = sbR("ones_row", [1, 128])
        triuF = sbR("triuF", [128, 128])
        trilP = sbR("trilP", [128, 128])
        strN = sbR("strN", [128, 128])
        ntriP = sbR("ntriP", [128, 128])
        triK = sbR("triK", [128, 128])
        e127 = sbR("e127", [128, 1])
        e8c = sbR("e8c", [8, 8])
        msel_s = sbR("msel_s", [128, 2])
        normw_s = sbR("normw_s", [128, 1])
        dtb_s = sbR("dtb_s", [HH, 1])
        nea_s = sbR("nea_s", [HH, 1])
        convb_s = sbR("convb_s", [128, DCN])
        convw_s = sbR("convw_s", [128, DCN, 4])
        gam_s = sbR("gam_s", [128, DCN])
        betf_s = sbR("betf_s", [128, DCN])
        rwT_s = sbR("rwT_s", [128, DCN, E])
        rwg_s = sbR("rwg_s", [128, DCN, E])
        rbc_s = sbR("rbc_s", [8, 1])
        wbT_s = sbR("wbT_s", [128, DCN, HH], BF16)
        waT_s = sbR("waT_s", [128, DCN, HH], BF16)
        fc = sbR("fconst", [128, 4])
        rb_row = sbR("rb_row", [1, E])
        rwtmp = sbR("rwtmp", [128, E])
        om_ctx = ExitStack()
        o_mine = om_ctx.enter_context(
            nc.sbuf_tensor("o_mine", [128, HH, NTT, 128], BF16, side="right"))

        pb = [ctx.enter_context(nc.psum_tensor(f"pb{i}", [128, 512], F32))
              for i in range(6)]
        pbB0 = pb[0]
        px = [ctx.enter_context(nc.psum_tensor(f"px{i}", [128, 1024], BF16))
              for i in range(2)]

        blk = ctx.enter_context(nc.Block())

        C = dict(identb=identb, identf=identf, ones_colf=ones_colf,
                 ones_row=ones_row, triuF=triuF, trilP=trilP, strN=strN,
                 ntriP=ntriP, triK=triK, e127=e127, e8c=e8c, msel=msel_s,
                 normw=normw_s, dtb=dtb_s, nea=nea_s, convb=convb_s,
                 convw=convw_s, gam=gam_s, betf=betf_s, rwT=rwT_s,
                 rwg=rwg_s, rbc=rbc_s, wbT=wbT_s, waT=waT_s, fc=fc)

        @blk.vector
        def _(v):
            for buf, val in [(identb, 1.0), (identf, 1.0), (ones_colf, 1.0),
                             (ones_row, 1.0), (triuF, 1.0), (trilP, 1.0),
                             (strN, -1.0), (ntriP, -1.0), (triK, RSK),
                             (e127, 1.0), (e8c, 1.0)]:
                v.memset(buf[:, :], val)
            v.memset(fc[:, 0:1], EPS_L2)
            v.memset(fc[:, 1:2], 1.0)
            v.memset(fc[:, 2:3], EPS_RMS)
            v.memset(fc[:, 3:4], EPS_LN)
            i = v.memset(o_mine[:, :, :, :], 0.0)
            S.inc("V", i)

        @blk.gpsimd
        def _(g):
            S.wait(g, "V")
            aff = lambda buf, base, pat, cm, op=ALU.is_ge: g.affine_select(
                out=buf[:, :], in_=buf[:, :], compare_op=op,
                fill=0.0, base=base, pattern=pat, channel_multiplier=cm)
            aff(identb, 0, [[-1, 128]], 1, ALU.is_equal)
            aff(identf, 0, [[-1, 128]], 1, ALU.is_equal)
            aff(triuF, 0, [[1, 128]], -1)
            aff(trilP, 0, [[-1, 128]], 1)
            aff(strN, -1, [[-1, 128]], 1)
            aff(ntriP, 0, [[1, 128]], -1)
            aff(triK, 0, [[1, 128]], -1)
            aff(e127, -127, [[0, 1]], 1, ALU.is_equal)
            i = aff(e8c, 0, [[-1, 8]], 1, ALU.is_equal)
            S.inc("G", i)
            S.inc("dW0", g.dma_start(rwT_s[:, :, :], rwT_d[:, :, :]), 16)
            S.inc("dW0", g.dma_start(wbT_s[:, :, :], wbT_d[:, :, :]), 16)
            S.inc("dW0", g.dma_start(waT_s[:, :, :], waT_d[:, :, :]), 16)

        @blk.sync
        def _(sy):
            for dst, srcd in [(msel_s, msel_d), (normw_s, normw_d),
                              (dtb_s, dtb_d), (nea_s, alog_d),
                              (convb_s, convb_d), (convw_s, convw_d),
                              (gam_s, gam_d), (betf_s, betf_d)]:
                S.inc("dS", sy.dma_start(dst.ap(), srcd.ap()), 16)

        @blk.scalar
        def _(s):
            S.wait(s, "dS")
            i = s.activation(nea_s[:, :], nea_s[:, :], AF.Exp, scale=1.0)
            S.inc("Sc", i)

        @blk.vector
        def _(v):
            S.wait(v, "Sc")
            i = v.tensor_scalar_mul(nea_s[:, :], nea_s[:, :], -1.0)
            S.inc("V", i)
            S.wait(v, "dW0")
            for dc in range(DCN):
                i = v.tensor_scalar_mul(rwg_s[:, dc, :], rwT_s[:, dc, :],
                                        C["gam"][:, dc:dc + 1])
                S.inc("V", i)
        rwg_tok = S.tok("V")

        @blk.vector
        def _(v):
            pass

        @blk.scalar
        def _(s):
            pass

        @blk.tensor
        def _(t):
            S.wait(t, "V", rwg_tok)

        for dc in range(DCN):
            @blk.vector
            def _(v, dc=dc):
                S.wait(v, "T")
                i = v.tensor_scalar_mul(rwtmp[:, :], rwT_s[:, dc, :],
                                        C["betf"][:, dc:dc + 1])
                S.inc("V", i)
            vq = S.tok("V")

            @blk.tensor
            def _(t, dc=dc, vq=vq):
                S.wait(t, "V", vq)
                i = t.matmul(px[0].bitcast(F32)[0:1, 0:E] if False else
                             None, None, None) if False else t.matmul(
                    pbB0[0:1, 0:E], C["ones_colf"][:, :], rwtmp[:, :],
                    start=(dc == 0), stop=(dc == DCN - 1))
                S.inc("T", i)
        rb_T = S.tok("T")

        @blk.scalar
        def _(s):
            S.wait(s, "T", rb_T)
            i = s.copy(rb_row[:, :], pbB0[0:1, 0:E])
            S.inc("Sc", i)
        rb_Sc = S.tok("Sc")

        @blk.tensor
        def _(t):
            S.wait(t, "Sc", rb_Sc)
            i = t.transpose(pbB0[0:8, 64:65], rb_row[:, :], C["identf"][0:1, 0:1])
            S.inc("T", i)
        rbt_T = S.tok("T")

        @blk.vector
        def _(v):
            S.wait(v, "T", rbt_T)
            i = v.tensor_copy(rbc_s[:, :], pbB0[0:8, 64:65])
            S.inc("V", i)

        _p1(nc, blk, S, C, o_mine, pb, px,
            dict(T=T, NGD=NGD, CPG=CPG, TGD=TGD, NTT=NTT), x_d, wvT_d, wgT_d, ctx)

        _barrier(blk, S)

        sbL = lambda name, shape, dt=F32: ctx.enter_context(nc.sbuf_tensor(name, shape, dt))
        out_acc = sbL("out_acc", [128, DCN, HT])
        xn_s = sbL("xn", [128, DCN, HT], BF16)
        wdT_s = sbL("wdT", [8, HT])
        _p2(nc, blk, S, C, o_mine, out_acc, xn_s, wdT_s, pb, px,
            dict(HT=HT, NGL=NGL, TGL=TGL), xh_d, woT_d)

        if debug:
            S.new("dbg")

            @blk.sync
            def _(g):
                for k, v_ in S.toks().items():
                    if k != "dbg":
                        S.wait(g, k, v_)
                S.inc("dbg", g.dma_start(dbg_om[:, :], o_mine[:, :, :, :].rearrange("p a b c -> p (a b c)")), 16)
                S.inc("dbg", g.dma_start(dbg_x1[:, :, :], out_acc[:, :, :]), 16)
                S.inc("dbg", g.dma_start(dbg_xn[:, :, :], xn_s[:, :, :]), 16)
                S.inc("dbg", g.dma_start(dbg_wd[:, :], wdT_s[:, :]), 16)
                S.wait(g, "dbg")

        _barrier(blk, S)
        om_ctx.close()

        _p3(nc, blk, S, C, out_acc, xn_s, wdT_s, pb,
            dict(HT=HT, NGM=NGM, TGM=TGM, p3lim=p3lim), w1_d, w2_d,
            (dbg_acc0, dbg_w2, dbg_h, dbg_wbc, dbg_ps) if debug else None)

        if debug:
            @blk.sync
            def _(g):
                for k, v_ in S.toks().items():
                    if k != "dbg":
                        S.wait(g, k, v_)
                S.inc("dbg", g.dma_start(dbg_acc[:, :, :], out_acc[:, :, :]), 16)
                S.wait(g, "dbg")

        _barrier(blk, S)

        _p4(nc, blk, S, C, out_acc, pb, dict(HT=HT), y_d)

        @blk.sync
        def _(sy):
            S.wait(sy, "st0")
            S.wait(sy, "st1")

    return nc


def _barrier(blk, S):
    t = S.toks()
    for deco in (blk.tensor, blk.vector, blk.scalar, blk.gpsimd, blk.sync):
        def f(eng, _t=t):
            for k, v in _t.items():
                S.wait(eng, k, v)
        deco(f)



def _p1(nc, blk, S, C, o_mine, pb, px, P, x_d, wvT_d, wgT_d, octx):
    T, NGD, CPG, TGD, NTT = P["T"], P["NGD"], P["CPG"], P["TGD"], P["NTT"]
    with ExitStack() as ctx:
        sb = lambda name, shape, dt=F32: ctx.enter_context(nc.sbuf_tensor(name, shape, dt))
        wvT_s = sb("wvT_s", [128, DCN, D], BF16)
        wgT_s = sb("wgT_s", [128, DCN, D], BF16)
        S_s = sb("S_s", [128, HH, 128])
        S16 = sb("S16", [128, HH, 128], BF16)
        x_tm = sb("x_tm", [128, 2, D])
        xfm = [sb(f"xfm{i}", [128, DCN, TGD], BF16) for i in range(2)]
        kmj = [sb(f"kmj{i}", [128, HH, TGD + 1], BF16) for i in range(2)]
        l2tmp = sb("l2tmp", [128, 2, D], BF16)
        sqt = sb("sqt", [128, D])
        ssq = sb("ssq", [128, 2 * HH])
        rsq = sb("rsq", [128, 2 * HH])
        vpre = [sb(f"vpre{i}", [128, DCN, TGD + 3]) for i in range(2)]
        cvt = [sb(f"cvt{i}", [128, TGD]) for i in range(2)]
        vfm = sb("vfm", [128, DCN, TGD], BF16)
        vtm = [sb(f"vtm{i}", [128, 2, HH, 128], BF16) for i in range(2)]
        gfm = sb("gfm", [128, DCN, TGD], BF16)
        gtm = [sb(f"gtm{i}", [128, 2, HH, 128]) for i in range(2)]
        wktm = [sb(f"wktm{i}", [128, 2, HH, 128], BF16) for i in range(2)]
        bdr = sb("bdr", [8, 2, TGD])
        beta_cm = sb("beta_cm", [128, 2 * HH])
        dec_cm = sb("dec_cm", [128, 2 * HH])
        dec_cum = sb("dec_cum", [128, 2 * HH])
        dlast = sb("dlast", [1, 2 * HH])
        elsr = sb("elsr", [1, 2 * HH])
        esl = sb("esl", [128, 2 * HH])
        dwv = sb("dwv", [128, 2 * HH])
        edec = sb("edec", [128, 2 * HH])
        edk = sb("edk", [128, 2 * HH])
        bev = sb("bev", [128, 2 * HH])
        mt1 = sb("mt1", [128, 2, 128])
        mt2 = sb("mt2", [128, 2, 128])
        lsnB = sb("lsnB", [128, 2, 128])
        ltk = sb("ltk", [128, 2, 128])
        abuf = sb("abuf", [128, 2, 128], BF16)
        atbuf = sb("atbuf", [128, 2, 128], BF16)
        pwc = [sb(f"pwc{i}", [128, 2, 256], BF16) for i in range(2)]
        gch = sb("gch", [128, 2, 128], BF16)
        vbb = sb("vbb", [128, 2, 128], BF16)
        wkbe = sb("wkbe", [128, 2, 128], BF16)
        vw = sb("vw", [128, HH, 2, 256], BF16)
        wkcT = sb("wkcT", [128, HH, 2, 128], BF16)
        atn = sb("atn", [128, HH, 2, 128], BF16)
        vnew = sb("vnew", [128, 128], BF16)
        wkdw = sb("wkdw", [128, 128], BF16)
        o_grp = sb("o_grp", [128, HH, TGD])
        ogt = sb("ogt", [128, TGD])
        rrs = sb("rrs", [128, 2 * HH])

        @blk.gpsimd
        def _(gp):
            S.wait(gp, "G")
            S.inc("dWVG", gp.dma_start(wvT_s[:, :, :], wvT_d[:, :, :]), 16)
            S.inc("dWVG", gp.dma_start(wgT_s[:, :, :], wgT_d[:, :, :]), 16)
        wload = S.tok("dWVG")

        @blk.vector
        def _(v):
            v.memset(S_s[:, :, :], 0.0)
            v.memset(S16[:, :, :], 0.0)
            v.memset(kmj[0][:, :, 0:1], 0.0)
            i = v.memset(vpre[0][:, :, 0:3], 0.0)
            S.inc("V", i)
        init_tok = S.tok("V")

        prev = {}

        for g in range(NGD):
            par = g % 2

            @blk.sync
            def _(sy):
                if prev:
                    S.wait(sy, "V", prev["x_V"])
                    S.wait(sy, "T", prev["x_T"])
                i = sy.dma_start(
                    x_tm[:, :, :],
                    x_d[g * TGD:(g + 1) * TGD, :].rearrange("(tt p) d -> p tt d", p=128))
                S.inc("dX", i, 16)
            x_dma = S.tok("dX")

            for deco in (blk.tensor, blk.vector, blk.scalar):
                def f(eng):
                    S.wait(eng, "dX", x_dma)
                    if prev:
                        S.wait(eng, "T", prev["end_T"])
                        S.wait(eng, "V", prev["end_V"])
                        S.wait(eng, "Sc", prev["end_Sc"])
                    else:
                        S.wait(eng, "V", init_tok)
                        S.wait(eng, "G")
                        S.wait(eng, "dWVG", wload)
                        S.wait(eng, "Sc")
                        S.wait(eng, "dS")
                        S.wait(eng, "dW0")
                deco(f)

            @blk.vector
            def _(v):
                for tt in range(2):
                    S.wait(v, "V")
                    i = v.tensor_mul(sqt[:, :], x_tm[:, tt, :], x_tm[:, tt, :])
                    S.inc("V", i)
                    S.wait(v, "V")
                    i = v.reduce_sum(ssq[:, tt * HH:(tt + 1) * HH],
                                     sqt[:, :].rearrange("p (c k) -> p c k", c=HH), AX.X)
                    S.inc("V", i)
            sumsq_tok = S.tok("V")

            @blk.scalar
            def _(s):
                S.wait(s, "V", sumsq_tok)
                i = s.activation(rsq[:, :], ssq[:, :], AF.Ln, bias=C['fc'][:, 0:1])
                S.inc("Sc", i)
                S.wait(s, "Sc")
                i = s.activation(rsq[:, :], rsq[:, :], AF.Exp, scale=-0.5)
                S.inc("Sc", i)
            rsq_tok = S.tok("Sc")

            @blk.vector
            def _(v):
                S.wait(v, "Sc", rsq_tok)
                for tt in range(2):
                    i = v.tensor_tensor(
                        l2tmp[:, tt, :].rearrange("p (c k) -> p c k", c=HH),
                        x_tm[:, tt, :].rearrange("p (c k) -> p c k", c=HH),
                        bc2(rsq[:, tt * HH:(tt + 1) * HH], HH), op=ALU.mult)
                    S.inc("V", i)
            l2n_tok = S.tok("V")

            ev = []
            for tt in range(2):
                for hf in range(2):
                    bank = pb[hf]

                    @blk.tensor
                    def _(t, tt=tt, hf=hf, bank=bank):
                        if len(ev) >= 2:
                            S.wait(t, "Sc", ev[-2])
                        for q in range(4):
                            dc = hf * 4 + q
                            i = t.transpose(bank[:, q * 128:(q + 1) * 128],
                                            x_tm[:, tt, dc * 128:(dc + 1) * 128],
                                            C["identf"][:, :])
                        S.inc("T", i)
                    ttok = S.tok("T")

                    @blk.scalar
                    def _(s, tt=tt, hf=hf, bank=bank, ttok=ttok):
                        S.wait(s, "T", ttok)
                        i = s.copy(
                            xfm[par][:, hf * 4:(hf + 1) * 4, tt * 128:(tt + 1) * 128],
                            bank[:, :].rearrange("p (c k) -> p c k", c=4))
                        S.inc("Sc", i)
                    ev.append(S.tok("Sc"))
            xfm_tok = S.tok("Sc")
            x_T_tok = S.tok("T")

            @blk.scalar
            def _(s):
                if g >= 1:
                    i = s.copy(kmj[par][:, :, 0:1], kmj[1 - par][:, :, TGD:TGD + 1])
                    S.inc("Sc", i)
            halo_tok = S.tok("Sc")

            for tt in range(2):
                bank = px[tt]

                @blk.tensor
                def _(t, tt=tt, bank=bank):
                    S.wait(t, "V", l2n_tok)
                    for h in range(HH):
                        i = t.transpose(bank[:, h * 128:(h + 1) * 128],
                                        l2tmp[:, tt, h * 128:(h + 1) * 128],
                                        C["identb"][:, :])
                    S.inc("T", i)
                kt = S.tok("T")

                @blk.scalar
                def _(s, tt=tt, bank=bank, kt=kt):
                    S.wait(s, "T", kt)
                    i = s.copy(kmj[par][:, :, 1 + tt * 128:1 + (tt + 1) * 128],
                               bank[:, :].rearrange("p (c k) -> p c k", c=HH))
                    S.inc("Sc", i)
            kmj_tok = S.tok("Sc")
            l2_T_tok = S.tok("T")

            @blk.vector
            def _(v):
                if g >= 1:
                    i = v.tensor_copy(vpre[par][:, :, 0:3],
                                      vpre[1 - par][:, :, TGD:TGD + 3])
                    S.inc("V", i)
            vhalo_tok = S.tok("V")

            vev = []
            for dv in range(DCN):
                bank = pb[2 + (dv % 2)]

                @blk.tensor
                def _(t, dv=dv, bank=bank):
                    S.wait(t, "Sc", xfm_tok)
                    if len(vev) >= 2:
                        S.wait(t, "V", vev[-2])
                    for dc in range(DCN):
                        i = t.matmul(bank[:, 0:TGD],
                                     wvT_s[:, dc, dv * 128:(dv + 1) * 128],
                                     xfm[par][:, dc, :],
                                     start=(dc == 0), stop=(dc == DCN - 1))
                    S.inc("T", i)
                vt = S.tok("T")

                @blk.vector
                def _(v, dv=dv, bank=bank, vt=vt):
                    S.wait(v, "T", vt)
                    S.wait(v, "V", vhalo_tok)
                    i = v.tensor_copy(vpre[par][:, dv, 3:3 + TGD], bank[:, 0:TGD])
                    S.inc("V", i)
                vev.append(S.tok("V"))
            vproj_tok = S.tok("V")

            gev = []
            for dv in range(DCN):
                bank = pb[2 + (dv % 2)]

                @blk.tensor
                def _(t, dv=dv, bank=bank):
                    S.wait(t, "V", vev[dv])
                    if len(gev) >= 2:
                        S.wait(t, "Sc", gev[-2])
                    for dc in range(DCN):
                        i = t.matmul(bank[:, 256:512],
                                     wgT_s[:, dc, dv * 128:(dv + 1) * 128],
                                     xfm[par][:, dc, :],
                                     start=(dc == 0), stop=(dc == DCN - 1))
                    S.inc("T", i)
                gt_ = S.tok("T")

                @blk.scalar
                def _(s, dv=dv, bank=bank, gt_=gt_):
                    S.wait(s, "T", gt_)
                    i = s.activation(gfm[:, dv, :], bank[:, 256:512], AF.Silu)
                    S.inc("Sc", i)
                gev.append(S.tok("Sc"))
            gact_tok = S.tok("Sc")

            @blk.vector
            def _(v):
                S.wait(v, "Sc", gact_tok)
                i = v.tensor_scalar_mul(gfm[:, :, :], gfm[:, :, :], C["normw"][:, :])
                S.inc("V", i)
            gnorm_tok = S.tok("V")

            @blk.tensor
            def _(t):
                S.wait(t, "Sc", xfm_tok)
                for dc in range(DCN):
                    t.matmul(pb[4][0:HH, 0:TGD], C["wbT"][:, dc, :], xfm[par][:, dc, :],
                             start=(dc == 0), stop=(dc == DCN - 1))
                for dc in range(DCN):
                    i = t.matmul(pb[4][0:HH, 256:512], C["waT"][:, dc, :],
                                 xfm[par][:, dc, :],
                                 start=(dc == 0), stop=(dc == DCN - 1))
                S.inc("T", i)
            bdz_tok = S.tok("T")

            @blk.scalar
            def _(s):
                S.wait(s, "T", bdz_tok)
                i = s.activation(bdr[:, 0, :], pb[4][0:HH, 0:TGD], AF.Exp, scale=-1.0)
                S.inc("Sc", i)
                i = s.activation(bdr[:, 1, :], pb[4][0:HH, 256:512], AF.Exp,
                                 bias=C["dtb"][:, :], scale=1.0)
                S.inc("Sc", i)
                S.wait(s, "Sc")
                i = s.activation(bdr[:, 1, :], bdr[:, 1, :], AF.Ln, bias=C['fc'][0:HH, 1:2], scale=1.0)
                S.inc("Sc", i)
            bdact_tok = S.tok("Sc")

            @blk.vector
            def _(v):
                S.wait(v, "Sc", bdact_tok)
                i = v.tensor_scalar_add(bdr[:, 0, :], bdr[:, 0, :], 1.0)
                S.inc("V", i)
                S.wait(v, "V")
                i = v.reciprocal(bdr[:, 0, :], bdr[:, 0, :])
                S.inc("V", i)
                i = v.tensor_scalar_mul(bdr[:, 1, :], bdr[:, 1, :], C["nea"][:, :])
                S.inc("V", i)
            bdrows_tok = S.tok("V")

            @blk.tensor
            def _(t):
                S.wait(t, "V", bdrows_tok)
                for cl in range(CPG):
                    t.transpose(pb[5][:, cl * 128:cl * 128 + 8],
                                bdr[:, 0, cl * 128:(cl + 1) * 128],
                                C["identf"][0:HH, 0:HH])
                    i = t.transpose(pb[5][:, 256 + cl * 128:256 + cl * 128 + 8],
                                    bdr[:, 1, cl * 128:(cl + 1) * 128],
                                    C["identf"][0:HH, 0:HH])
                S.inc("T", i)
            bdtr_tok = S.tok("T")

            @blk.vector
            def _(v):
                S.wait(v, "T", bdtr_tok)
                for cl in range(CPG):
                    v.tensor_copy(beta_cm[:, cl * HH:(cl + 1) * HH],
                                  pb[5][:, cl * 128:cl * 128 + 8])
                    i = v.tensor_copy(dec_cm[:, cl * HH:(cl + 1) * HH],
                                      pb[5][:, 256 + cl * 128:256 + cl * 128 + 8])
                S.inc("V", i)
            bdcm_tok = S.tok("V")

            @blk.tensor
            def _(t):
                S.wait(t, "V", bdcm_tok)
                i = t.matmul(pb[5][:, 0:2 * HH], C["triuF"][:, :], dec_cm[:, :],
                             start=True, stop=True)
                S.inc("T", i)
            cum_tok = S.tok("T")

            @blk.vector
            def _(v):
                S.wait(v, "T", cum_tok)
                i = v.tensor_copy(dec_cum[:, :], pb[5][:, 0:2 * HH])
                S.inc("V", i)
            deccum_tok = S.tok("V")

            @blk.tensor
            def _(t):
                S.wait(t, "V", deccum_tok)
                i = t.matmul(pb[5][0:1, 64:64 + 2 * HH], C["e127"][:, :], dec_cum[:, :],
                             start=True, stop=True)
                S.inc("T", i)
            dlast_tok = S.tok("T")

            @blk.scalar
            def _(s):
                S.wait(s, "T", dlast_tok)
                i = s.copy(dlast[:, :], pb[5][0:1, 64:64 + 2 * HH])
                S.inc("Sc", i)
                i = s.activation(elsr[:, :], pb[5][0:1, 64:64 + 2 * HH], AF.Exp)
                S.inc("Sc", i)
            dlrow_tok = S.tok("Sc")

            @blk.tensor
            def _(t):
                S.wait(t, "Sc", dlrow_tok)
                t.matmul(pb[5][:, 128:128 + 2 * HH], C["ones_row"][:, :], elsr[:, :],
                         start=True, stop=True)
                i = t.matmul(pb[5][:, 192:192 + 2 * HH], C["ones_row"][:, :],
                             dlast[:, :], start=True, stop=True)
                S.inc("T", i)
            dbc_tok = S.tok("T")

            @blk.vector
            def _(v):
                S.wait(v, "T", dbc_tok)
                i = v.tensor_copy(esl[:, :], pb[5][:, 128:128 + 2 * HH])
                S.inc("V", i)
                i = v.tensor_sub(dwv[:, :], pb[5][:, 192:192 + 2 * HH], dec_cum[:, :])
                S.inc("V", i)
            dwsub_tok = S.tok("V")

            @blk.scalar
            def _(s):
                S.wait(s, "V", dwsub_tok)
                i = s.activation(dwv[:, :], dwv[:, :], AF.Exp)
                S.inc("Sc", i)
                i = s.activation(edec[:, :], dec_cum[:, :], AF.Exp)
                S.inc("Sc", i)
            dfam_sc = S.tok("Sc")

            @blk.vector
            def _(v):
                S.wait(v, "Sc", dfam_sc)
                i = v.tensor_scalar_mul(edk[:, :], edec[:, :], RSK)
                S.inc("V", i)
                i = v.tensor_mul(bev[:, :], beta_cm[:, :], edec[:, :])
                S.inc("V", i)
            dfam_v = S.tok("V")

            csl = []
            for dc in range(DCN):
                @blk.vector
                def _(v, dc=dc):
                    S.wait(v, "V", vproj_tok)
                    if len(csl) >= 2:
                        S.wait(v, "Sc", csl[-2])
                    buf = cvt[dc % 2]
                    i = v.tensor_scalar(buf[:, :], vpre[par][:, dc, 0:TGD],
                                        C["convw"][:, dc, 0:1], None, op0=ALU.mult)
                    S.inc("V", i)
                    for j in range(1, 4):
                        S.wait(v, "V")
                        i = v.scalar_tensor_tensor(
                            buf[:, :], vpre[par][:, dc, j:j + TGD],
                            C["convw"][:, dc, j:j + 1], buf[:, :],
                            op0=ALU.mult, op1=ALU.add)
                        S.inc("V", i)
                cvtok = S.tok("V")

                @blk.scalar
                def _(s, dc=dc, cvtok=cvtok):
                    S.wait(s, "V", cvtok)
                    i = s.activation(vfm[:, dc, :], cvt[dc % 2][:, :], AF.Silu,
                                     bias=C["convb"][:, dc:dc + 1])
                    S.inc("Sc", i)
                csl.append(S.tok("Sc"))
            vsilu_tok = S.tok("Sc")

            tmev = []
            for kind in range(3):
                for tt in range(2):
                    bank = px[tt]

                    @blk.tensor
                    def _(t, kind=kind, tt=tt, bank=bank):
                        if kind == 0:
                            S.wait(t, "Sc", vsilu_tok)
                        elif kind == 1:
                            S.wait(t, "V", gnorm_tok)
                        else:
                            S.wait(t, "Sc", max(kmj_tok, halo_tok))
                        if len(tmev) >= 2:
                            S.wait(t, "V", tmev[-2])
                        for h in range(HH):
                            if kind == 0:
                                src = vfm[:, h, tt * 128:(tt + 1) * 128]
                            elif kind == 1:
                                src = gfm[:, h, tt * 128:(tt + 1) * 128]
                            else:
                                src = kmj[par][:, h, tt * 128:tt * 128 + 128]
                            i = t.transpose(bank[:, h * 128:(h + 1) * 128], src,
                                            C["identb"][:, :])
                        S.inc("T", i)
                    trt = S.tok("T")

                    @blk.vector
                    def _(v, kind=kind, tt=tt, bank=bank, trt=trt):
                        S.wait(v, "T", trt)
                        dst = [vtm, gtm, wktm][kind]
                        i = v.tensor_copy(
                            dst[par][:, tt, :, :],
                            bank[:, :].rearrange("p (c k) -> p c k", c=HH))
                        S.inc("V", i)
                    tmev.append(S.tok("V"))
            tm_tok = S.tok("V")

            pre = []
            for h in range(HH):
                cols = slice(h, 2 * HH, HH)
                startV = pre[-1]["endV"] if pre else max(tm_tok, dfam_v)
                startSc = pre[-1]["endSc"] if pre else S.tok("Sc")

                @blk.tensor
                def _(t, h=h, startV=startV, startSc=startSc):
                    S.wait(t, "V", max(startV, tm_tok, dfam_v))
                    S.wait(t, "Sc", startSc)
                    for cl in range(CPG):
                        i = t.transpose(
                            pb[3][:, cl * 128:(cl + 1) * 128],
                            dec_cum[:, cl * HH + h:cl * HH + h + 1].broadcast_to(
                                [128, 128]),
                            C["identf"][:, :])
                    S.inc("T", i)
                mtr_tok = S.tok("T")

                @blk.vector
                def _(v, cols=cols, mtr_tok=mtr_tok):
                    S.wait(v, "T", mtr_tok)
                    i = v.tensor_tensor(mt1[:, :, :], bc2(dec_cum[:, cols], CPG),
                                        pb[3][:, 0:256].rearrange(
                                            "p (c k) -> p c k", c=CPG),
                                        op=ALU.subtract)
                    S.inc("V", i)
                    S.wait(v, "V")
                    i = v.tensor_mul(mt2[:, :, :], mt1[:, :, :], mbc(C["ntriP"], CPG))
                    S.inc("V", i)
                    i = v.tensor_mul(mt1[:, :, :], mt1[:, :, :], mbc(C["trilP"], CPG))
                    S.inc("V", i)
                mmul_tok = S.tok("V")

                @blk.scalar
                def _(s, mmul_tok=mmul_tok):
                    S.wait(s, "V", mmul_tok)
                    i = s.activation(mt2[:, :, :], mt2[:, :, :], AF.Exp)
                    S.inc("Sc", i)
                    i = s.activation(mt1[:, :, :], mt1[:, :, :], AF.Exp)
                    S.inc("Sc", i)
                mexp_tok = S.tok("Sc")

                @blk.vector
                def _(v, cols=cols, mexp_tok=mexp_tok):
                    S.wait(v, "Sc", mexp_tok)
                    i = v.tensor_mul(ltk[:, :, :], mt2[:, :, :], mbc(C["triK"], CPG))
                    S.inc("V", i)
                    i = v.tensor_mul(lsnB[:, :, :], mt1[:, :, :], mbc(C["strN"], CPG))
                    S.inc("V", i)
                    S.wait(v, "V")
                    i = v.tensor_tensor(lsnB[:, :, :], lsnB[:, :, :],
                                        bc2(beta_cm[:, cols], CPG), op=ALU.mult)
                    S.inc("V", i)
                mask_V = S.tok("V")

                @blk.tensor
                def _(t, h=h):
                    S.wait(t, "Sc", bdact_tok)
                    for cl in range(CPG):
                        wkk = kmj[par][:, h, cl * 128:cl * 128 + 128]
                        i = t.matmul(pb[4][:, cl * 128:(cl + 1) * 128], wkk, wkk,
                                     start=True, stop=True)
                    S.inc("T", i)
                raw_tok = S.tok("T")

                @blk.vector
                def _(v, raw_tok=raw_tok, mask_V=mask_V):
                    S.wait(v, "T", raw_tok)
                    S.wait(v, "V", mask_V)
                    i = v.tensor_tensor(abuf[:, :, :],
                                        pb[4][:, 0:256].rearrange(
                                            "p (c k) -> p c k", c=CPG),
                                        lsnB[:, :, :], op=ALU.mult)
                    S.inc("V", i)
                A_V = S.tok("V")

                @blk.tensor
                def _(t, A_V=A_V):
                    S.wait(t, "V", A_V)
                    for cl in range(CPG):
                        i = t.transpose(px[0][:, cl * 128:(cl + 1) * 128],
                                        abuf[:, cl, :], C["identb"][:, :])
                    S.inc("T", i)
                at_tok = S.tok("T")

                @blk.scalar
                def _(s, at_tok=at_tok):
                    S.wait(s, "T", at_tok)
                    i = s.copy(atbuf[:, :, :],
                               px[0][:, 0:256].rearrange("p (c k) -> p c k", c=CPG))
                    S.inc("Sc", i)
                atev_tok = S.tok("Sc")

                @blk.vector
                def _(v, atev_tok=atev_tok):
                    S.wait(v, "Sc", atev_tok)
                    i = v.tensor_tensor(gch[:, :, :], atbuf[:, :, :],
                                        mbc(C["identb"], CPG), op=ALU.add)
                    S.inc("V", i)
                g_tok = S.tok("V")

                curP = lambda b, cl: (abuf[:, cl, :] if b is None else b[:, cl, 0:128])
                curPT = lambda b, cl: (atbuf[:, cl, :] if b is None else b[:, cl, 128:256])
                buf = None
                cur_tok = atev_tok
                for lvl in range(1, 7):
                    @blk.tensor
                    def _(t, buf=buf, lvl=lvl, cur_tok=cur_tok, g_tok=g_tok):
                        S.wait(t, "Sc", cur_tok)
                        S.wait(t, "V", g_tok)
                        for cl in range(CPG):
                            i = t.matmul(pb[4][:, cl * 256:cl * 256 + 128],
                                         curPT(buf, cl), curP(buf, cl),
                                         start=True, stop=True)
                            if lvl < 6:
                                i = t.matmul(pb[4][:, cl * 256 + 128:cl * 256 + 256],
                                             curP(buf, cl), curPT(buf, cl),
                                             start=True, stop=True)
                        S.inc("T", i)
                    pw_tok = S.tok("T")

                    @blk.scalar
                    def _(s, lvl=lvl, pw_tok=pw_tok):
                        S.wait(s, "T", pw_tok)
                        i = s.copy(pwc[lvl % 2][:, :, :],
                                   pb[4][:, 0:512].rearrange("p (c k) -> p c k", c=CPG))
                        S.inc("Sc", i)
                    pwev_tok = S.tok("Sc")

                    @blk.tensor
                    def _(t, lvl=lvl, pwev_tok=pwev_tok):
                        S.wait(t, "Sc", pwev_tok)
                        for cl in range(CPG):
                            i = t.matmul(pb[5][:, 256 + cl * 128:256 + (cl + 1) * 128],
                                         pwc[lvl % 2][:, cl, 0:128], gch[:, cl, :],
                                         start=True, stop=True)
                        S.inc("T", i)
                    gm_tok = S.tok("T")

                    @blk.vector
                    def _(v, gm_tok=gm_tok):
                        S.wait(v, "T", gm_tok)
                        i = v.tensor_tensor(
                            gch[:, :, :], gch[:, :, :],
                            pb[5][:, 256:512].rearrange("p (c k) -> p c k", c=CPG),
                            op=ALU.add)
                        S.inc("V", i)
                    g_tok = S.tok("V")
                    buf, cur_tok = pwc[lvl % 2], pwev_tok

                @blk.vector
                def _(v, h=h, cols=cols, g_tok=g_tok):
                    S.wait(v, "V", g_tok)
                    i = v.tensor_tensor(vbb[:, :, :], vtm[par][:, :, h, :],
                                        bc2(beta_cm[:, cols], CPG), op=ALU.mult)
                    S.inc("V", i)
                    i = v.tensor_tensor(wkbe[:, :, :], wktm[par][:, :, h, :],
                                        bc2(bev[:, cols], CPG), op=ALU.mult)
                    S.inc("V", i)
                vbw_tok = S.tok("V")

                @blk.tensor
                def _(t, vbw_tok=vbw_tok):
                    S.wait(t, "V", vbw_tok)
                    for cl in range(CPG):
                        t.matmul(pb[4][:, cl * 256:cl * 256 + 128],
                                 gch[:, cl, :], vbb[:, cl, :], start=True, stop=True)
                        i = t.matmul(pb[4][:, cl * 256 + 128:cl * 256 + 256],
                                     gch[:, cl, :], wkbe[:, cl, :],
                                     start=True, stop=True)
                    S.inc("T", i)
                ap_tok = S.tok("T")

                @blk.scalar
                def _(s, h=h, ap_tok=ap_tok):
                    S.wait(s, "T", ap_tok)
                    i = s.copy(vw[:, h, :, :],
                               pb[4][:, 0:512].rearrange("p (c k) -> p c k", c=CPG))
                    S.inc("Sc", i)
                apev_tok = S.tok("Sc")

                @blk.tensor
                def _(t, h=h, apev_tok=apev_tok):
                    S.wait(t, "Sc", apev_tok)
                    for cl in range(CPG):
                        i = t.transpose(px[0][:, 256 + cl * 128:256 + (cl + 1) * 128],
                                        vw[:, h, cl, 128:256], C["identb"][:, :])
                    S.inc("T", i)
                wt_tok = S.tok("T")

                @blk.scalar
                def _(s, h=h, wt_tok=wt_tok):
                    S.wait(s, "T", wt_tok)
                    i = s.copy(wkcT[:, h, :, :],
                               px[0][:, 256:512].rearrange("p (c k) -> p c k", c=CPG))
                    S.inc("Sc", i)
                wtev_tok = S.tok("Sc")

                @blk.tensor
                def _(t, h=h):
                    for cl in range(CPG):
                        wkk = kmj[par][:, h, cl * 128:cl * 128 + 128]
                        rkk = kmj[par][:, h, cl * 128 + 1:cl * 128 + 129]
                        i = t.matmul(pb[3][:, cl * 128:(cl + 1) * 128], wkk, rkk,
                                     start=True, stop=True)
                    S.inc("T", i)
                att_tok = S.tok("T")

                @blk.vector
                def _(v, h=h, att_tok=att_tok, mask_V=mask_V):
                    S.wait(v, "T", att_tok)
                    S.wait(v, "V", mask_V)
                    i = v.tensor_tensor(atn[:, h, :, :],
                                        pb[3][:, 0:256].rearrange(
                                            "p (c k) -> p c k", c=CPG),
                                        ltk[:, :, :], op=ALU.mult)
                    S.inc("V", i)
                atn_tok = S.tok("V")

                pre.append(dict(endV=S.tok("V"), endSc=S.tok("Sc"),
                                apev=apev_tok, wkcT=wtev_tok, atn=atn_tok))

            pre_V = S.tok("V")
            pre_Sc = S.tok("Sc")

            last_V = pre_V
            last_Sc = pre_Sc
            for cl in range(CPG):
                for h in range(HH):
                    col = cl * HH + h

                    @blk.tensor
                    def _(t, h=h, cl=cl, last_V=last_V, last_Sc=last_Sc):
                        S.wait(t, "Sc", max(pre[h]["wkcT"], last_Sc))
                        S.wait(t, "V", max(pre[h]["atn"], last_V))
                        i = t.matmul(pb[5][:, 0:128], wkcT[:, h, cl, :], S16[:, h, :],
                                     start=True, stop=True)
                        S.inc("T", i)
                    vn_tok = S.tok("T")

                    @blk.vector
                    def _(v, h=h, cl=cl, vn_tok=vn_tok):
                        S.wait(v, "T", vn_tok)
                        i = v.tensor_tensor(vnew[:, :], vw[:, h, cl, 0:128],
                                            pb[5][:, 0:128], op=ALU.subtract)
                        S.inc("V", i)
                    vnew_tok = S.tok("V")

                    @blk.vector
                    def _(v, h=h, cl=cl, col=col, vnew_tok=vnew_tok):
                        S.wait(v, "V")
                        i = v.tensor_scalar_mul(wkdw[:, :], wktm[par][:, cl, h, :],
                                                dwv[:, col:col + 1])
                        S.inc("V", i)
                    wkdw_tok = S.tok("V")

                    @blk.tensor
                    def _(t, h=h, cl=cl, vnew_tok=vnew_tok, wkdw_tok=wkdw_tok):
                        S.wait(t, "V", wkdw_tok)
                        rkk = kmj[par][:, h, cl * 128 + 1:cl * 128 + 129]
                        t.matmul(pb[5][:, 128:256], rkk, S16[:, h, :],
                                 start=True, stop=True)
                        t.matmul(pb[5][:, 256:384], atn[:, h, cl, :], vnew[:, :],
                                 start=True, stop=True)
                        i = t.matmul(pb[5][:, 384:512], wkdw[:, :], vnew[:, :],
                                     start=True, stop=True)
                        S.inc("T", i)
                    smm_tok = S.tok("T")

                    @blk.vector
                    def _(v, h=h, cl=cl, col=col, smm_tok=smm_tok):
                        S.wait(v, "T", smm_tok)
                        i = v.tensor_scalar_mul(ogt[:, 0:128], pb[5][:, 128:256],
                                                edk[:, col:col + 1])
                        S.inc("V", i)
                        S.wait(v, "V")
                        i = v.tensor_tensor(o_grp[:, h, cl * 128:(cl + 1) * 128],
                                            ogt[:, 0:128], pb[5][:, 256:384],
                                            op=ALU.add)
                        S.inc("V", i)
                        i = v.scalar_tensor_tensor(
                            S_s[:, h, :], S_s[:, h, :], esl[:, col:col + 1],
                            pb[5][:, 384:512], op0=ALU.mult, op1=ALU.add)
                        S.inc("V", i)
                        S.wait(v, "V")
                        i = v.tensor_copy(S16[:, h, :], S_s[:, h, :])
                        S.inc("V", i)
                    last_V = S.tok("V")
                    last_Sc = S.tok("Sc")
            scan_V = last_V
            scan_Sc = last_Sc

            @blk.vector
            def _(v):
                S.wait(v, "V", scan_V)
                S.wait(v, "Sc", scan_Sc)
                for h in range(HH):
                    S.wait(v, "V")
                    i = v.tensor_mul(ogt[:, :], o_grp[:, h, :], o_grp[:, h, :])
                    S.inc("V", i)
                    S.wait(v, "V")
                    i = v.reduce_sum(rrs[:, h * 2:(h + 1) * 2],
                                     ogt[:, :].rearrange("p (c k) -> p c k", c=2),
                                     AX.X)
                    S.inc("V", i)
            rr_tok = S.tok("V")

            @blk.scalar
            def _(s):
                S.wait(s, "V", rr_tok)
                i = s.activation(rrs[:, :], rrs[:, :], AF.Ln, scale=1.0 / 128.0,
                                 bias=C['fc'][:, 2:3])
                S.inc("Sc", i)
                S.wait(s, "Sc")
                i = s.activation(rrs[:, :], rrs[:, :], AF.Exp, scale=-0.5)
                S.inc("Sc", i)
            rstd_tok = S.tok("Sc")

            half = g // (NGD // 2)
            tt0 = (2 * g) % NTT
            for h in range(HH):
                @blk.vector
                def _(v, h=h):
                    S.wait(v, "Sc", rstd_tok)
                    S.wait(v, "V")
                    i = v.tensor_tensor(
                        o_grp[:, h, :].rearrange("p (c k) -> p c k", c=2),
                        o_grp[:, h, :].rearrange("p (c k) -> p c k", c=2),
                        gtm[par][:, :, h, :], op=ALU.mult)
                    S.inc("V", i)
                    S.wait(v, "V")
                    i = v.tensor_tensor(
                        o_grp[:, h, :].rearrange("p (c k) -> p c k", c=2),
                        o_grp[:, h, :].rearrange("p (c k) -> p c k", c=2),
                        bc2(rrs[:, h * 2:(h + 1) * 2], 2), op=ALU.mult)
                    S.inc("V", i)
                    S.wait(v, "V")
                    i = v.scalar_tensor_tensor(
                        o_mine[:, h, tt0:tt0 + 2, :],
                        o_grp[:, h, :].rearrange("p (c k) -> p c k", c=2),
                        C["msel"][:, half:half + 1],
                        o_mine[:, h, tt0:tt0 + 2, :],
                        op0=ALU.mult, op1=ALU.add)
                    S.inc("V", i)

            prev = dict(x_V=l2n_tok, x_T=x_T_tok,
                        end_T=S.tok("T"), end_V=S.tok("V"), end_Sc=S.tok("Sc"))



def _p2(nc, blk, S, C, o_mine, out_acc, xn_s, wdT_s, pb, px, P, xh_d, woT_d):
    HT, NGL, TGL = P["HT"], P["NGL"], P["TGL"]
    with ExitStack() as ctx:
        sb = lambda name, shape, dt=F32: ctx.enter_context(nc.sbuf_tensor(name, shape, dt))
        woT_s = sb("woT_s", [128, DCN, D], BF16)
        xh_tm = sb("xh_tm", [128, 2, D])
        ofm = sb("ofm", [128, DCN, TGL], BF16)
        res1 = sb("res1", [128, DCN, TGL])
        sums = sb("sums", [1, 2 * TGL])
        stat = sb("stat", [1, 3 * TGL])
        lgr = sb("lgr", [8, TGL])
        lgt = sb("lgt", [128, 8])
        prb = sb("prb", [128, 8])
        srt8 = sb("srt8", [128, 8])
        mxb = sb("mxb", [128, 1])
        smr = sb("smr", [128, 1])
        sqv = sb("sqv", [128, TGL])

        @blk.gpsimd
        def _(g):
            i = g.dma_start(woT_s[:, :, :], woT_d[:, :, :])
            S.inc("dWO", i, 16)
        wo_tok = S.tok("dWO")

        prevg = {}
        for gl in range(NGL):
            @blk.sync
            def _(sy):
                if prevg:
                    S.wait(sy, "T", prevg["xh_T"])
                i = sy.dma_start(
                    xh_tm[:, :, :],
                    xh_d[gl * TGL:(gl + 1) * TGL, :].rearrange(
                        "(tt p) d -> p tt d", p=128))
                S.inc("dX", i, 16)
            xh_dma = S.tok("dX")

            for deco in (blk.tensor, blk.vector, blk.scalar):
                def f(eng):
                    S.wait(eng, "dX", xh_dma)
                    S.wait(eng, "dWO", wo_tok)
                    if prevg:
                        S.wait(eng, "T", prevg["end_T"])
                        S.wait(eng, "V", prevg["end_V"])
                        S.wait(eng, "Sc", prevg["end_Sc"])
                deco(f)

            tt0 = gl * (TGL // 128)
            for tt in range(2):
                bank = px[tt]

                @blk.tensor
                def _(t, tt=tt, bank=bank):
                    for h in range(HH):
                        i = t.transpose(bank[:, h * 128:(h + 1) * 128],
                                        o_mine[:, h, tt0 + tt, :], C["identb"][:, :])
                    S.inc("T", i)
                ot_ = S.tok("T")

                @blk.scalar
                def _(s, tt=tt, bank=bank, ot_=ot_):
                    S.wait(s, "T", ot_)
                    i = s.copy(ofm[:, :, tt * 128:(tt + 1) * 128],
                               bank[:, :].rearrange("p (c k) -> p c k", c=HH))
                    S.inc("Sc", i)
            ofm_tok = S.tok("Sc")

            rev = []
            for dv in range(DCN):
                bank = pb[dv % 2]
                bank2 = pb[2 + (dv % 2)]

                @blk.tensor
                def _(t, dv=dv, bank=bank, bank2=bank2):
                    S.wait(t, "Sc", ofm_tok)
                    if len(rev) >= 2:
                        S.wait(t, "V", rev[-2])
                    for tt in range(2):
                        t.transpose(bank2[:, tt * 128:(tt + 1) * 128],
                                    xh_tm[:, tt, dv * 128:(dv + 1) * 128],
                                    C["identf"][:, :])
                    for dc in range(DCN):
                        i = t.matmul(bank[:, 0:TGL],
                                     woT_s[:, dc, dv * 128:(dv + 1) * 128],
                                     ofm[:, dc, :],
                                     start=(dc == 0), stop=(dc == DCN - 1))
                    S.inc("T", i)
                rt = S.tok("T")

                @blk.scalar
                def _(s, dv=dv, bank2=bank2, rt=rt):
                    S.wait(s, "T", rt)
                    i = s.copy(res1[:, dv, :], bank2[:, 0:TGL])
                    S.inc("Sc", i)
                sev = S.tok("Sc")

                @blk.vector
                def _(v, dv=dv, bank=bank, sev=sev):
                    S.wait(v, "Sc", sev)
                    i = v.tensor_tensor(res1[:, dv, :], res1[:, dv, :], bank[:, 0:TGL],
                                        op=ALU.add)
                    S.inc("V", i)
                rev.append(S.tok("V"))
            res1_tok = S.tok("V")
            xh_T_tok = S.tok("T")

            @blk.tensor
            def _(t):
                S.wait(t, "V", res1_tok)
                for dc in range(DCN):
                    i = t.matmul(pb[4][0:1, 0:TGL], C["ones_colf"][:, :],
                                 res1[:, dc, :],
                                 start=(dc == 0), stop=(dc == DCN - 1))
                S.inc("T", i)
            sum_tok = S.tok("T")

            sq_T = sum_tok
            for dc in range(DCN):
                @blk.vector
                def _(v, dc=dc, sq_T=sq_T):
                    S.wait(v, "V", res1_tok)
                    S.wait(v, "T", sq_T)
                    i = v.tensor_mul(sqv[:, :], res1[:, dc, :], res1[:, dc, :])
                    S.inc("V", i)
                vq = S.tok("V")

                @blk.tensor
                def _(t, dc=dc, vq=vq):
                    S.wait(t, "V", vq)
                    i = t.matmul(pb[5][0:1, 0:TGL], C["ones_colf"][:, :], sqv[:, :],
                                 start=(dc == 0), stop=(dc == DCN - 1))
                    S.inc("T", i)
                sq_T = S.tok("T")
            sumsq_tok = sq_T

            @blk.scalar
            def _(s):
                S.wait(s, "T", sumsq_tok)
                i = s.mul(sums[:, 0:TGL], pb[4][0:1, 0:TGL], 1.0 / D)
                S.inc("Sc", i)
                i = s.mul(sums[:, TGL:2 * TGL], pb[5][0:1, 0:TGL], 1.0 / D)
                S.inc("Sc", i)
            mv_tok = S.tok("Sc")

            @blk.vector
            def _(v):
                S.wait(v, "Sc", mv_tok)
                i = v.tensor_mul(stat[:, 0:TGL], sums[:, 0:TGL], sums[:, 0:TGL])
                S.inc("V", i)
                S.wait(v, "V")
                i = v.tensor_sub(stat[:, 0:TGL], sums[:, TGL:2 * TGL], stat[:, 0:TGL])
                S.inc("V", i)
            var_tok = S.tok("V")

            @blk.scalar
            def _(s):
                S.wait(s, "V", var_tok)
                i = s.activation(stat[:, TGL:2 * TGL], stat[:, 0:TGL], AF.Ln,
                                 bias=C['fc'][0:1, 3:4])
                S.inc("Sc", i)
                S.wait(s, "Sc")
                i = s.activation(stat[:, TGL:2 * TGL], stat[:, TGL:2 * TGL], AF.Exp,
                                 scale=-0.5)
                S.inc("Sc", i)
            rstd_tok = S.tok("Sc")

            @blk.vector
            def _(v):
                S.wait(v, "Sc", rstd_tok)
                i = v.tensor_mul(stat[:, 2 * TGL:3 * TGL], sums[:, 0:TGL],
                                 stat[:, TGL:2 * TGL])
                S.inc("V", i)
            mr_tok = S.tok("V")

            @blk.tensor
            def _(t):
                S.wait(t, "V", mr_tok)
                t.matmul(pb[5][:, 0:TGL], C["ones_row"][:, :], stat[:, TGL:2 * TGL],
                         start=True, stop=True)
                i = t.matmul(pb[4][:, 0:TGL], C["ones_row"][:, :],
                             stat[:, 2 * TGL:3 * TGL], start=True, stop=True)
                S.inc("T", i)
            bcast_tok = S.tok("T")

            for dc in range(DCN):
                @blk.vector
                def _(v, dc=dc):
                    S.wait(v, "T", bcast_tok)
                    i = v.tensor_tensor(res1[:, dc, :], res1[:, dc, :],
                                        pb[5][:, 0:TGL], op=ALU.mult)
                    S.inc("V", i)
                    S.wait(v, "V")
                    i = v.tensor_tensor(res1[:, dc, :], res1[:, dc, :],
                                        pb[4][:, 0:TGL], op=ALU.subtract)
                    S.inc("V", i)
                    S.wait(v, "V")
                    i = v.tensor_scalar_mul(
                        out_acc[:, dc, gl * TGL:(gl + 1) * TGL], res1[:, dc, :], 2.0)
                    S.inc("V", i)
                    i = v.tensor_scalar(
                        xn_s[:, dc, gl * TGL:(gl + 1) * TGL], res1[:, dc, :],
                        C["gam"][:, dc:dc + 1], C["betf"][:, dc:dc + 1],
                        op0=ALU.mult, op1=ALU.add)
                    S.inc("V", i)
            apply_tok = S.tok("V")

            @blk.tensor
            def _(t):
                S.wait(t, "V", apply_tok)
                for dc in range(DCN):
                    i = t.matmul(pb[4][0:E, 0:TGL], C["rwg"][:, dc, :],
                                 res1[:, dc, :],
                                 start=(dc == 0), stop=(dc == DCN - 1))
                S.inc("T", i)
            lg_tok = S.tok("T")

            @blk.vector
            def _(v):
                S.wait(v, "T", lg_tok)
                i = v.tensor_scalar_add(lgr[:, :], pb[4][0:E, 0:TGL],
                                        C["rbc"][:, :])
                S.inc("V", i)
            lgr_tok = S.tok("V")

            wtoks = []
            for tt in range(2):
                @blk.tensor
                def _(t, tt=tt, lgr_tok=lgr_tok):
                    S.wait(t, "V", lgr_tok)
                    if wtoks:
                        S.wait(t, "V", wtoks[-1])
                    i = t.transpose(pb[3][0:128, tt * 128:tt * 128 + 8],
                                    lgr[:, tt * 128:(tt + 1) * 128],
                                    C["identf"][0:E, 0:E])
                    S.inc("T", i)
                ltr = S.tok("T")

                @blk.vector
                def _(v, tt=tt, ltr=ltr):
                    S.wait(v, "T", ltr)
                    i = v.tensor_copy(lgt[:, :], pb[3][:, tt * 128:tt * 128 + 8])
                    S.inc("V", i)
                    S.wait(v, "V")
                    i = v.reduce_max(mxb[:, :], lgt[:, :], AX.X)
                    S.inc("V", i)
                    S.wait(v, "V")
                    i = v.tensor_scalar_mul(mxb[:, :], mxb[:, :], -1.0 / TEMP)
                    S.inc("V", i)
                mx_tok = S.tok("V")

                @blk.scalar
                def _(s, mx_tok=mx_tok):
                    S.wait(s, "V", mx_tok)
                    i = s.activation(prb[:, :], lgt[:, :], AF.Exp,
                                     scale=1.0 / TEMP, bias=mxb[:, :])
                    S.inc("Sc", i)
                ex_tok = S.tok("Sc")

                @blk.vector
                def _(v, ex_tok=ex_tok):
                    S.wait(v, "Sc", ex_tok)
                    i = v.reduce_sum(smr[:, :], prb[:, :], AX.X)
                    S.inc("V", i)
                    S.wait(v, "V")
                    i = v.reciprocal(smr[:, :], smr[:, :])
                    S.inc("V", i)
                    S.wait(v, "V")
                    i = v.tensor_scalar_mul(prb[:, :], prb[:, :], smr[:, :])
                    S.inc("V", i)
                    S.wait(v, "V")
                    i = v.max(out=srt8[:, :], in_=prb[:, :])
                    S.inc("V", i)
                    S.wait(v, "V")
                    i = v.tensor_scalar(srt8[:, :], prb[:, :], srt8[:, 1:2], None,
                                        op0=ALU.is_ge)
                    S.inc("V", i)
                    S.wait(v, "V")
                    i = v.tensor_mul(srt8[:, :], srt8[:, :], prb[:, :])
                    S.inc("V", i)
                wd_tok = S.tok("V")

                @blk.tensor
                def _(t, tt=tt, wd_tok=wd_tok):
                    S.wait(t, "V", wd_tok)
                    i = t.transpose(pb[3][0:8, 256 + tt * 128:256 + (tt + 1) * 128],
                                    srt8[:, :], C["identf"][:, :])
                    S.inc("T", i)
                wtr = S.tok("T")

                @blk.vector
                def _(v, tt=tt, wtr=wtr):
                    S.wait(v, "T", wtr)
                    i = v.tensor_copy(
                        wdT_s[:, gl * TGL + tt * 128:gl * TGL + (tt + 1) * 128],
                        pb[3][0:8, 256 + tt * 128:256 + (tt + 1) * 128])
                    S.inc("V", i)
                wtoks.append(S.tok("V"))

            prevg = dict(xh_T=xh_T_tok, end_T=S.tok("T"), end_V=S.tok("V"),
                         end_Sc=S.tok("Sc"))



def _p3(nc, blk, S, C, out_acc, xn_s, wdT_s, pb, P, w1_d, w2_d, dbg=None):
    HT, NGM, TGM = P["HT"], P["NGM"], P["TGM"]
    NE_ = E
    if P.get("p3lim"):
        NE_, NGM = P["p3lim"]
    NF = FF // 128
    with ExitStack() as ctx:
        sb = lambda name, shape, dt=F32: ctx.enter_context(nc.sbuf_tensor(name, shape, dt))
        w1_s = sb("w1s", [128, DCN, FF], BF16)
        w2_s = [sb(f"w2s{i}", [128, 2, D], BF16) for i in range(4)]
        hbuf = sb("hbuf", [128, 4, TGM], BF16)
        wbc = sb("wbc", [128, TGM])
        ytmp = sb("ytmp", [128, TGM])

        w1_tok = [[0] * 8 for _ in range(E)]
        w2_dma = [0, 0, 0, 0]
        w2_rd = [0, 0, 0, 0]
        ring_rd = [0, 0, 0, 0]
        last_y_read = 0
        mm1_ev = {}
        prev_tile_sil = [0]
        prev_tile_wbc = [0]

        @blk.gpsimd
        def _(g):
            for pc in range(8):
                i = g.dma_start(w1_s[:, :, pc * 512:(pc + 1) * 512],
                                w1_d[0, :, :, pc * 512:(pc + 1) * 512])
                w1_tok[0][pc] = S.inc(f"dW1p{pc}", i, 16)

        for e in range(NE_):
            for tg in range(NGM):
                t0 = tg * TGM
                prev_sil = mm1_ev.get("last", 0) if isinstance(mm1_ev, dict) else 0

                @blk.tensor
                def _(t, e=e, t0=t0):
                    S.wait(t, "Sc", prev_tile_sil[0])
                    S.wait(t, "V", prev_tile_wbc[0])
                    i = t.matmul(pb[0][:, 256:256 + TGM],
                                 C["e8c"][:, e:e + 1].broadcast_to([8, 128]),
                                 wdT_s[:, t0:t0 + TGM], start=True, stop=True)
                    S.inc("T", i)
                wb_tok = S.tok("T")

                @blk.vector
                def _(v, wb_tok=wb_tok):
                    S.wait(v, "T", wb_tok)
                    i = v.tensor_copy(wbc[:, :], pb[0][:, 256:256 + TGM])
                    S.inc("V", i)
                wbc_tok = S.tok("V")
                prev_tile_wbc[0] = wbc_tok

                mm1_ev = {}

                def emit_mm1(f, e=e, t0=t0, wbc_tok=wbc_tok):
                    @blk.tensor
                    def _(t, f=f):
                        S.wait(t, f"dW1p{f // 4}", w1_tok[e][f // 4])
                        if f >= 2:
                            S.wait(t, "Sc", mm1_ev[f - 2])
                        if f == 0:
                            S.wait(t, "V", wbc_tok)
                        bank = pb[0] if f % 2 == 0 else pb[5]
                        for dc in range(DCN):
                            i = t.matmul(bank[:, 0:TGM],
                                         w1_s[:, dc, f * 128:(f + 1) * 128],
                                         xn_s[:, dc, t0:t0 + TGM],
                                         start=(dc == 0), stop=(dc == DCN - 1))
                        S.inc("T", i)
                    m1_tok = S.tok("T")
                    slot = f % 4

                    @blk.scalar
                    def _(s, f=f, slot=slot, m1_tok=m1_tok, rr=ring_rd[slot]):
                        S.wait(s, "T", m1_tok)
                        S.wait(s, "T", rr)
                        bank = pb[0] if f % 2 == 0 else pb[5]
                        i = s.activation(hbuf[:, slot, :], bank[:, 0:TGM], AF.Silu)
                        S.inc("Sc", i)
                    mm1_ev[f] = S.tok("Sc")

                def emit_w2dma(f, e=e):
                    ws = (f // 2) % 4

                    @blk.gpsimd
                    def _(g, e=e, f=f, ws=ws, wr=w2_rd[ws]):
                        S.wait(g, "T", wr)
                        i = g.dma_start(
                            w2_s[ws][:, :, :],
                            w2_d[e, f * 128:(f + 2) * 128, :].rearrange(
                                "(c p) d -> p c d", p=128))
                        w2_dma[ws] = S.inc(f"dW2s{ws}", i, 16)

                def emit_mm2(f, ly=last_y_read):
                    slot = f % 4

                    @blk.tensor
                    def _(t, f=f, slot=slot, ly=ly):
                        S.wait(t, "Sc", mm1_ev[f])
                        if f == 0:
                            S.wait(t, "V", ly)
                        S.wait(t, f"dW2s{(f // 2) % 4}", w2_dma[(f // 2) % 4])
                        wsl = w2_s[(f // 2) % 4]
                        for dc in range(DCN):
                            i = t.matmul(
                                pb[1 + dc // 2][:, (dc % 2) * 256:(dc % 2) * 256 + TGM],
                                wsl[:, f % 2, dc * 128:(dc + 1) * 128],
                                hbuf[:, slot, :],
                                start=(f == 0 and dc % 2 == 0),
                                stop=(f == NF - 1 and dc % 2 == 1))
                        S.inc("T", i)
                    ring_rd[slot] = S.tok("T")
                    w2_rd[(f // 2) % 4] = S.tok("T")

                emit_w2dma(0)
                emit_w2dma(2)
                emit_mm1(0)
                emit_mm1(1)
                for f in range(NF):
                    if f + 4 < NF and (f + 4) % 2 == 0:
                        emit_w2dma(f + 4)
                    emit_mm2(f)
                    if f + 2 < NF:
                        emit_mm1(f + 2)
                prev_tile_sil[0] = mm1_ev[NF - 1]
                mm2_done = S.tok("T")

                for dc in range(DCN):
                    @blk.vector
                    def _(v, dc=dc, t0=t0, mm2_done=mm2_done, wbc_tok=wbc_tok):
                        S.wait(v, "T", mm2_done)
                        S.wait(v, "V")
                        i = v.tensor_tensor(
                            ytmp[:, :],
                            pb[1 + dc // 2][:, (dc % 2) * 256:(dc % 2) * 256 + TGM],
                            wbc[:, :], op=ALU.mult)
                        S.inc("V", i)
                        S.wait(v, "V")
                        i = v.tensor_tensor(out_acc[:, dc, t0:t0 + TGM],
                                            out_acc[:, dc, t0:t0 + TGM],
                                            ytmp[:, :], op=ALU.add)
                        S.inc("V", i)
                last_y_read = S.tok("V")

                if tg == NGM - 1 and e + 1 < NE_:
                    @blk.gpsimd
                    def _(g, e=e, mm2_done=mm2_done):
                        S.wait(g, "T", mm2_done)
                        for pc in range(8):
                            i = g.dma_start(
                                w1_s[:, :, pc * 512:(pc + 1) * 512],
                                w1_d[e + 1, :, :, pc * 512:(pc + 1) * 512])
                            w1_tok[e + 1][pc] = S.inc(f"dW1p{pc}", i, 16)



def _p4(nc, blk, S, C, out_acc, pb, P, y_d):
    HT = P["HT"]
    TG = min(512, HT)
    NG = HT // TG
    with ExitStack() as ctx:
        sb = lambda name, shape, dt=F32: ctx.enter_context(nc.sbuf_tensor(name, shape, dt))
        sums = sb("sums4", [1, 2 * TG])
        stat = sb("stat4", [1, 3 * TG])
        sqv = sb("sqv4", [128, TG])
        stg = [sb(f"stg4{i}", [128, D]) for i in range(2)]

        stg_st = [0, 0]
        prevo = {}
        for g in range(NG):
            t0 = g * TG

            @blk.tensor
            def _(t):
                if prevo:
                    S.wait(t, "T", prevo["end_T"])
                    S.wait(t, "V", prevo["end_V"])
                    S.wait(t, "Sc", prevo["end_Sc"])
                for dc in range(DCN):
                    i = t.matmul(pb[4][0:1, 0:TG], C["ones_colf"][:, :],
                                 out_acc[:, dc, t0:t0 + TG],
                                 start=(dc == 0), stop=(dc == DCN - 1))
                S.inc("T", i)
            sum_tok = S.tok("T")

            sq_T = sum_tok
            for dc in range(DCN):
                @blk.vector
                def _(v, dc=dc, sq_T=sq_T):
                    if prevo:
                        S.wait(v, "T", prevo["end_T"])
                    S.wait(v, "T", sq_T)
                    i = v.tensor_mul(sqv[:, :], out_acc[:, dc, t0:t0 + TG],
                                     out_acc[:, dc, t0:t0 + TG])
                    S.inc("V", i)
                vq = S.tok("V")

                @blk.tensor
                def _(t, dc=dc, vq=vq):
                    S.wait(t, "V", vq)
                    i = t.matmul(pb[5][0:1, 0:TG], C["ones_colf"][:, :], sqv[:, :],
                                 start=(dc == 0), stop=(dc == DCN - 1))
                    S.inc("T", i)
                sq_T = S.tok("T")
            sumsq_tok = sq_T

            @blk.scalar
            def _(s):
                if prevo:
                    S.wait(s, "Sc", prevo["end_Sc"])
                S.wait(s, "T", sumsq_tok)
                i = s.mul(sums[:, 0:TG], pb[4][0:1, 0:TG], 1.0 / D)
                S.inc("Sc", i)
                i = s.mul(sums[:, TG:2 * TG], pb[5][0:1, 0:TG], 1.0 / D)
                S.inc("Sc", i)
            mv_tok = S.tok("Sc")

            @blk.vector
            def _(v):
                S.wait(v, "Sc", mv_tok)
                i = v.tensor_mul(stat[:, 0:TG], sums[:, 0:TG], sums[:, 0:TG])
                S.inc("V", i)
                S.wait(v, "V")
                i = v.tensor_sub(stat[:, 0:TG], sums[:, TG:2 * TG], stat[:, 0:TG])
                S.inc("V", i)
            var_tok = S.tok("V")

            @blk.scalar
            def _(s):
                S.wait(s, "V", var_tok)
                i = s.activation(stat[:, TG:2 * TG], stat[:, 0:TG], AF.Ln, bias=C['fc'][0:1, 3:4])
                S.inc("Sc", i)
                S.wait(s, "Sc")
                i = s.activation(stat[:, TG:2 * TG], stat[:, TG:2 * TG], AF.Exp,
                                 scale=-0.5)
                S.inc("Sc", i)
            rstd_tok = S.tok("Sc")

            @blk.vector
            def _(v):
                S.wait(v, "Sc", rstd_tok)
                i = v.tensor_mul(stat[:, 2 * TG:3 * TG], sums[:, 0:TG],
                                 stat[:, TG:2 * TG])
                S.inc("V", i)
            mr_tok = S.tok("V")

            @blk.tensor
            def _(t):
                S.wait(t, "V", mr_tok)
                t.matmul(pb[5][:, 0:TG], C["ones_row"][:, :], stat[:, TG:2 * TG],
                         start=True, stop=True)
                i = t.matmul(pb[4][:, 0:TG], C["ones_row"][:, :],
                             stat[:, 2 * TG:3 * TG], start=True, stop=True)
                S.inc("T", i)
            bc_tok = S.tok("T")

            for dc in range(DCN):
                @blk.vector
                def _(v, dc=dc):
                    S.wait(v, "T", bc_tok)
                    i = v.tensor_tensor(out_acc[:, dc, t0:t0 + TG],
                                        out_acc[:, dc, t0:t0 + TG],
                                        pb[5][:, 0:TG], op=ALU.mult)
                    S.inc("V", i)
                    S.wait(v, "V")
                    i = v.tensor_tensor(out_acc[:, dc, t0:t0 + TG],
                                        out_acc[:, dc, t0:t0 + TG],
                                        pb[4][:, 0:TG], op=ALU.subtract)
                    S.inc("V", i)
            nrm_tok = S.tok("V")

            prev_stv = 0
            for tt in range(TG // 128):
                sslot = tt % 2
                for hf in range(2):
                    bank = pb[hf]

                    @blk.tensor
                    def _(t, tt=tt, hf=hf, bank=bank, sslot=sslot,
                          prev_stv=prev_stv):
                        S.wait(t, "V", nrm_tok)
                        S.wait(t, "V", prev_stv)
                        S.wait(t, f"st{sslot}", stg_st[sslot])
                        for q in range(4):
                            dc = hf * 4 + q
                            i = t.transpose(
                                bank[:, q * 128:(q + 1) * 128],
                                out_acc[:, dc,
                                        t0 + tt * 128:t0 + tt * 128 + 128],
                                C["identf"][:, :])
                        S.inc("T", i)
                    tk = S.tok("T")

                    @blk.vector
                    def _(v, hf=hf, bank=bank, tk=tk, sslot=sslot):
                        S.wait(v, "T", tk)
                        S.wait(v, f"st{sslot}", stg_st[sslot])
                        i = v.tensor_copy(stg[sslot][:, hf * 512:(hf + 1) * 512],
                                          bank[:, 0:512])
                        S.inc("V", i)
                stv = S.tok("V")
                prev_stv = stv

                @blk.sync
                def _(sy, tt=tt, stv=stv, sslot=sslot):
                    S.wait(sy, "V", stv)
                    i = sy.dma_start(y_d[t0 + tt * 128:t0 + (tt + 1) * 128, :],
                                     stg[sslot][:, :])
                    S.inc(f"st{sslot}", i, 16)
                stg_st[sslot] = S.tok(f"st{sslot}")
            prevo = dict(end_T=S.tok("T"), end_V=S.tok("V"), end_Sc=S.tok("Sc"))



def kernel(L0_attn_x, L0_attn_Wv, L0_attn_Wg, L0_attn_Wo, L0_attn_Wb, L0_attn_Wa,
           L0_attn_dt_bias, L0_attn_A_log, L0_attn_norm_w, L0_attn_conv_w,
           L0_attn_conv_b, L0_ffn_gamma, L0_ffn_beta, L0_ffn_router_w,
           L0_ffn_W1, L0_ffn_W2):
    T = np.asarray(L0_attn_x).shape[1]
    HT = T // 2
    nc = build(T=T)

    cc = np.ascontiguousarray
    f32 = lambda a: np.asarray(a, np.float32)

    def chunkp(w):
        w = f32(w)
        return cc(w.reshape(DCN, 128, -1).transpose(1, 0, 2))

    wvT = chunkp(f32(L0_attn_Wv).T)
    wgT = chunkp(f32(L0_attn_Wg).T)
    woT = chunkp(f32(L0_attn_Wo).T)
    wbT = chunkp(f32(L0_attn_Wb).T)
    waT = chunkp(f32(L0_attn_Wa).T)
    rwT = chunkp(f32(L0_ffn_router_w).T)
    convw = chunkp(f32(L0_attn_conv_w))
    convb = cc(f32(L0_attn_conv_b).reshape(DCN, 128).T)
    gam = cc(f32(L0_ffn_gamma).reshape(DCN, 128).T)
    betf = cc(f32(L0_ffn_beta).reshape(DCN, 128).T)
    w1 = cc(f32(L0_ffn_W1).reshape(E, DCN, 128, FF).transpose(0, 2, 1, 3))
    w2 = cc(f32(L0_ffn_W2))
    dtb = cc(f32(L0_attn_dt_bias).reshape(HH, 1))
    alog = cc(f32(L0_attn_A_log).reshape(HH, 1))
    normw = cc(f32(L0_attn_norm_w).reshape(128, 1))
    x = f32(L0_attn_x)

    in_maps = []
    for c in range(8):
        b, half = c >> 1, c & 1
        msel = np.zeros((128, 2), np.float32)
        msel[:, half] = 1.0
        in_maps.append(dict(
            xb=cc(x[b]), xh=cc(x[b, half * HT:(half + 1) * HT]),
            wvT=wvT, wgT=wgT, woT=woT, wbT=wbT, waT=waT,
            dtb=dtb, alog=alog, normw=normw, convw=convw, convb=convb,
            gam=gam, betf=betf, rwT=rwT, w1=w1, w2=w2, msel=msel,
        ))

    res = run_bass_kernel_spmd(nc, in_maps, core_ids=list(range(8)))
    out = np.empty((B, T, D), np.float32)
    for c in range(8):
        b, half = c >> 1, c & 1
        out[b, half * HT:(half + 1) * HT] = np.asarray(res.results[c]["y"])
    return out


# version 05 (final)
# speedup vs baseline: 1.6348x; 1.0635x over previous
import numpy as np
from contextlib import ExitStack

import concourse.bass as bass
import concourse.mybir as mybir
from concourse.bass_utils import run_bass_kernel_spmd

F32 = mybir.dt.float32
BF16 = mybir.dt.bfloat16
AF = mybir.ActivationFunctionType
ALU = mybir.AluOpType
AX = mybir.AxisListType

B, D, HH, E, FF = 4, 1024, 8, 8, 4096
DCN = D // 128
EPS_L2, EPS_RMS, EPS_LN, TEMP = 1e-6, 1e-5, 1e-5, 1.5
RSK = float(1.0 / np.sqrt(128.0))


class Sems:
    def __init__(self, nc, ctx):
        self.nc, self.ctx, self.m = nc, ctx, {}

    def new(self, *names):
        for n in names:
            self.m[n] = [self.ctx.enter_context(self.nc.semaphore(n)), 0]

    def inc(self, name, instr, k=1):
        e = self.m[name]
        instr.then_inc(e[0], k)
        e[1] += k
        return e[1]

    def tok(self, name):
        return self.m[name][1]

    def wait(self, eng, name, tok=None):
        e = self.m[name]
        v = e[1] if tok is None else tok
        if v > 0:
            eng.wait_ge(e[0], v)

    def toks(self):
        return {k: v[1] for k, v in self.m.items()}


def bc2(ap2d, n, w=128):
    return ap2d.unsqueeze(2).broadcast_to([ap2d.shape[0], n, w])


def mbc(m, n):
    return m[:, :].unsqueeze(1).broadcast_to([128, n, 128])


def build(T=4096, debug=False, p3lim=None):
    TGD = 256
    NGD = T // TGD
    CPG = 2
    HT = T // 2
    NTT = HT // 128
    TGM = 256
    NGM = HT // TGM
    TGL = 256
    NGL = HT // TGL

    nc = bass.Bass("TRN2", target_bir_lowering=False,
                   dynamic_dma_scratch_size=16384)

    din = lambda name, shape, dt=F32: nc.dram_tensor(name, shape, dt, kind="ExternalInput")
    x_d = din("xb", [T, D])
    xh_d = din("xh", [HT, D])
    wvT_d = din("wvT", [128, DCN, D])
    wgT_d = din("wgT", [128, DCN, D])
    woT_d = din("woT", [128, DCN, D])
    wbT_d = din("wbT", [128, DCN, HH])
    waT_d = din("waT", [128, DCN, HH])
    dtb_d = din("dtb", [HH, 1])
    alog_d = din("alog", [HH, 1])
    normw_d = din("normw", [128, 1])
    convw_d = din("convw", [128, DCN, 4])
    convb_d = din("convb", [128, DCN])
    gam_d = din("gam", [128, DCN])
    betf_d = din("betf", [128, DCN])
    rwT_d = din("rwT", [128, DCN, E])
    w1_d = din("w1", [E, 128, DCN, FF])
    w2_d = din("w2", [E, FF, D])
    msel_d = din("msel", [128, 2])
    y_d = nc.dram_tensor("y", [HT, D], F32, kind="ExternalOutput")
    if debug:
        dbg_om = nc.dram_tensor("dbg_om", [128, HH * (HT // 128) * 128], BF16,
                                kind="ExternalOutput")
        dbg_x1 = nc.dram_tensor("dbg_x1", [128, DCN, HT], F32, kind="ExternalOutput")
        dbg_xn = nc.dram_tensor("dbg_xn", [128, DCN, HT], BF16, kind="ExternalOutput")
        dbg_wd = nc.dram_tensor("dbg_wd", [8, HT], F32, kind="ExternalOutput")
        dbg_acc = nc.dram_tensor("dbg_acc", [128, DCN, HT], F32, kind="ExternalOutput")
        dbg_acc0 = nc.dram_tensor("dbg_acc0", [128, DCN, HT], F32, kind="ExternalOutput")
        dbg_w2 = nc.dram_tensor("dbg_w2", [2, 128, 2, D], BF16, kind="ExternalOutput")
        dbg_h = nc.dram_tensor("dbg_h", [128, 4, TGM], BF16, kind="ExternalOutput")
        dbg_wbc = nc.dram_tensor("dbg_wbc", [128, TGM], F32, kind="ExternalOutput")
        dbg_ps = nc.dram_tensor("dbg_ps", [128, 2, TGM], F32, kind="ExternalOutput")

    with ExitStack() as ctx:
        S = Sems(nc, ctx)
        S.new("dX", "dS", "dW0", "dWVG", "dWO", "T", "V", "Sc", "G", "st0", "st1",
              *[f"dW1p{i}" for i in range(8)],
              "dW2s0", "dW2s1", "dW2s2", "dW2s3")
        sbR = lambda name, shape, dt=F32: ctx.enter_context(
            nc.sbuf_tensor(name, shape, dt, side="right"))

        identb = sbR("identb", [128, 128], BF16)
        identf = sbR("identf", [128, 128])
        ones_colf = sbR("ones_colf", [128, 1])
        ones_row = sbR("ones_row", [1, 128])
        triuF = sbR("triuF", [128, 128])
        trilP = sbR("trilP", [128, 128])
        strN = sbR("strN", [128, 128])
        ntriP = sbR("ntriP", [128, 128])
        triK = sbR("triK", [128, 128])
        e127 = sbR("e127", [128, 1])
        e8c = sbR("e8c", [8, 8])
        msel_s = sbR("msel_s", [128, 2])
        normw_s = sbR("normw_s", [128, 1])
        dtb_s = sbR("dtb_s", [HH, 1])
        nea_s = sbR("nea_s", [HH, 1])
        convb_s = sbR("convb_s", [128, DCN])
        convw_s = sbR("convw_s", [128, DCN, 4])
        gam_s = sbR("gam_s", [128, DCN])
        betf_s = sbR("betf_s", [128, DCN])
        rwT_s = sbR("rwT_s", [128, DCN, E])
        rwg_s = sbR("rwg_s", [128, DCN, E])
        rbc_s = sbR("rbc_s", [8, 1])
        wbT_s = sbR("wbT_s", [128, DCN, HH], BF16)
        waT_s = sbR("waT_s", [128, DCN, HH], BF16)
        fc = sbR("fconst", [128, 4])
        rb_row = sbR("rb_row", [1, E])
        rwtmp = sbR("rwtmp", [128, E])
        om_ctx = ExitStack()
        o_mine = om_ctx.enter_context(
            nc.sbuf_tensor("o_mine", [128, HH, NTT, 128], BF16, side="right"))

        ps_ctx = ExitStack()
        pb = [ps_ctx.enter_context(nc.psum_tensor(f"pb{i}", [128, 512], F32))
              for i in range(6)]
        pbB0 = pb[0]
        px = [ps_ctx.enter_context(nc.psum_tensor(f"px{i}", [128, 1024], BF16))
              for i in range(2)]

        blk = ctx.enter_context(nc.Block())

        C = dict(identb=identb, identf=identf, ones_colf=ones_colf,
                 ones_row=ones_row, triuF=triuF, trilP=trilP, strN=strN,
                 ntriP=ntriP, triK=triK, e127=e127, e8c=e8c, msel=msel_s,
                 normw=normw_s, dtb=dtb_s, nea=nea_s, convb=convb_s,
                 convw=convw_s, gam=gam_s, betf=betf_s, rwT=rwT_s,
                 rwg=rwg_s, rbc=rbc_s, wbT=wbT_s, waT=waT_s, fc=fc)

        @blk.vector
        def _(v):
            for buf, val in [(identb, 1.0), (identf, 1.0), (ones_colf, 1.0),
                             (ones_row, 1.0), (triuF, 1.0), (trilP, 1.0),
                             (strN, -1.0), (ntriP, -1.0), (triK, RSK),
                             (e127, 1.0), (e8c, 1.0)]:
                v.memset(buf[:, :], val)
            v.memset(fc[:, 0:1], EPS_L2)
            v.memset(fc[:, 1:2], 1.0)
            v.memset(fc[:, 2:3], EPS_RMS)
            v.memset(fc[:, 3:4], EPS_LN)
            i = v.memset(o_mine[:, :, :, :], 0.0)
            S.inc("V", i)

        @blk.gpsimd
        def _(g):
            S.wait(g, "V")
            aff = lambda buf, base, pat, cm, op=ALU.is_ge: g.affine_select(
                out=buf[:, :], in_=buf[:, :], compare_op=op,
                fill=0.0, base=base, pattern=pat, channel_multiplier=cm)
            aff(identb, 0, [[-1, 128]], 1, ALU.is_equal)
            aff(identf, 0, [[-1, 128]], 1, ALU.is_equal)
            aff(triuF, 0, [[1, 128]], -1)
            aff(trilP, 0, [[-1, 128]], 1)
            aff(strN, -1, [[-1, 128]], 1)
            aff(ntriP, 0, [[1, 128]], -1)
            aff(triK, 0, [[1, 128]], -1)
            aff(e127, -127, [[0, 1]], 1, ALU.is_equal)
            i = aff(e8c, 0, [[-1, 8]], 1, ALU.is_equal)
            S.inc("G", i)
            S.inc("dW0", g.dma_start(rwT_s[:, :, :], rwT_d[:, :, :]), 16)
            S.inc("dW0", g.dma_start(wbT_s[:, :, :], wbT_d[:, :, :]), 16)
            S.inc("dW0", g.dma_start(waT_s[:, :, :], waT_d[:, :, :]), 16)

        @blk.sync
        def _(sy):
            for dst, srcd in [(msel_s, msel_d), (normw_s, normw_d),
                              (dtb_s, dtb_d), (nea_s, alog_d),
                              (convb_s, convb_d), (convw_s, convw_d),
                              (gam_s, gam_d), (betf_s, betf_d)]:
                S.inc("dS", sy.dma_start(dst.ap(), srcd.ap()), 16)

        @blk.scalar
        def _(s):
            S.wait(s, "dS")
            i = s.activation(nea_s[:, :], nea_s[:, :], AF.Exp, scale=1.0)
            S.inc("Sc", i)

        @blk.vector
        def _(v):
            S.wait(v, "Sc")
            i = v.tensor_scalar_mul(nea_s[:, :], nea_s[:, :], -1.0)
            S.inc("V", i)
            S.wait(v, "dW0")
            for dc in range(DCN):
                i = v.tensor_scalar_mul(rwg_s[:, dc, :], rwT_s[:, dc, :],
                                        C["gam"][:, dc:dc + 1])
                S.inc("V", i)
        rwg_tok = S.tok("V")

        @blk.vector
        def _(v):
            pass

        @blk.scalar
        def _(s):
            pass

        @blk.tensor
        def _(t):
            S.wait(t, "V", rwg_tok)

        for dc in range(DCN):
            @blk.vector
            def _(v, dc=dc):
                S.wait(v, "T")
                i = v.tensor_scalar_mul(rwtmp[:, :], rwT_s[:, dc, :],
                                        C["betf"][:, dc:dc + 1])
                S.inc("V", i)
            vq = S.tok("V")

            @blk.tensor
            def _(t, dc=dc, vq=vq):
                S.wait(t, "V", vq)
                i = t.matmul(px[0].bitcast(F32)[0:1, 0:E] if False else
                             None, None, None) if False else t.matmul(
                    pbB0[0:1, 0:E], C["ones_colf"][:, :], rwtmp[:, :],
                    start=(dc == 0), stop=(dc == DCN - 1))
                S.inc("T", i)
        rb_T = S.tok("T")

        @blk.scalar
        def _(s):
            S.wait(s, "T", rb_T)
            i = s.copy(rb_row[:, :], pbB0[0:1, 0:E])
            S.inc("Sc", i)
        rb_Sc = S.tok("Sc")

        @blk.tensor
        def _(t):
            S.wait(t, "Sc", rb_Sc)
            i = t.transpose(pbB0[0:8, 64:65], rb_row[:, :], C["identf"][0:1, 0:1])
            S.inc("T", i)
        rbt_T = S.tok("T")

        @blk.vector
        def _(v):
            S.wait(v, "T", rbt_T)
            i = v.tensor_copy(rbc_s[:, :], pbB0[0:8, 64:65])
            S.inc("V", i)

        _p1(nc, blk, S, C, o_mine, pb, px,
            dict(T=T, NGD=NGD, CPG=CPG, TGD=TGD, NTT=NTT), x_d, wvT_d, wgT_d, ctx)

        _barrier(blk, S)

        sbL = lambda name, shape, dt=F32: ctx.enter_context(nc.sbuf_tensor(name, shape, dt))
        out_acc = sbL("out_acc", [128, DCN, HT])
        xn_s = sbL("xn", [128, DCN, HT], BF16)
        wdT_s = sbL("wdT", [8, HT])
        _p2(nc, blk, S, C, o_mine, out_acc, xn_s, wdT_s, pb, px,
            dict(HT=HT, NGL=NGL, TGL=TGL), xh_d, woT_d)

        if debug:
            S.new("dbg")

            @blk.sync
            def _(g):
                for k, v_ in S.toks().items():
                    if k != "dbg":
                        S.wait(g, k, v_)
                S.inc("dbg", g.dma_start(dbg_om[:, :], o_mine[:, :, :, :].rearrange("p a b c -> p (a b c)")), 16)
                S.inc("dbg", g.dma_start(dbg_x1[:, :, :], out_acc[:, :, :]), 16)
                S.inc("dbg", g.dma_start(dbg_xn[:, :, :], xn_s[:, :, :]), 16)
                S.inc("dbg", g.dma_start(dbg_wd[:, :], wdT_s[:, :]), 16)
                S.wait(g, "dbg")

        _barrier(blk, S)
        om_ctx.close()
        ps_ctx.close()

        _p3(nc, blk, S, C, out_acc, xn_s, wdT_s,
            dict(HT=HT, NGM=NGM, TGM=TGM, p3lim=p3lim), w1_d, w2_d,
            (dbg_acc0, dbg_w2, dbg_h, dbg_wbc, dbg_ps) if debug else None)

        if debug:
            @blk.sync
            def _(g):
                for k, v_ in S.toks().items():
                    if k != "dbg":
                        S.wait(g, k, v_)
                S.inc("dbg", g.dma_start(dbg_acc[:, :, :], out_acc[:, :, :]), 16)
                S.wait(g, "dbg")

        _barrier(blk, S)

        _p4(nc, blk, S, C, out_acc, dict(HT=HT), y_d)

        @blk.sync
        def _(sy):
            S.wait(sy, "st0")
            S.wait(sy, "st1")

    return nc


def _barrier(blk, S):
    t = S.toks()
    for deco in (blk.tensor, blk.vector, blk.scalar, blk.gpsimd, blk.sync):
        def f(eng, _t=t):
            for k, v in _t.items():
                S.wait(eng, k, v)
        deco(f)



def _p1(nc, blk, S, C, o_mine, pb, px, P, x_d, wvT_d, wgT_d, octx):
    T, NGD, CPG, TGD, NTT = P["T"], P["NGD"], P["CPG"], P["TGD"], P["NTT"]
    with ExitStack() as ctx:
        sb = lambda name, shape, dt=F32: ctx.enter_context(nc.sbuf_tensor(name, shape, dt))
        wvT_s = sb("wvT_s", [128, DCN, D], BF16)
        wgT_s = sb("wgT_s", [128, DCN, D], BF16)
        S_s = sb("S_s", [128, HH, 128])
        S16 = sb("S16", [128, HH, 128], BF16)
        x_tm = sb("x_tm", [128, 2, D])
        xfm = [sb(f"xfm{i}", [128, DCN, TGD], BF16) for i in range(2)]
        kmj = [sb(f"kmj{i}", [128, HH, TGD + 1], BF16) for i in range(2)]
        l2tmp = sb("l2tmp", [128, 2, D], BF16)
        sqt = sb("sqt", [128, D])
        ssq = sb("ssq", [128, 2 * HH])
        rsq = sb("rsq", [128, 2 * HH])
        vpre = [sb(f"vpre{i}", [128, DCN, TGD + 3]) for i in range(2)]
        cvt = [sb(f"cvt{i}", [128, TGD]) for i in range(2)]
        vfm = sb("vfm", [128, DCN, TGD], BF16)
        vtm = [sb(f"vtm{i}", [128, 2, HH, 128], BF16) for i in range(2)]
        gfm = sb("gfm", [128, DCN, TGD], BF16)
        gtm = [sb(f"gtm{i}", [128, 2, HH, 128]) for i in range(2)]
        wktm = [sb(f"wktm{i}", [128, 2, HH, 128], BF16) for i in range(2)]
        bdr = sb("bdr", [8, 2, TGD])
        beta_cm = sb("beta_cm", [128, 2 * HH])
        dec_cm = sb("dec_cm", [128, 2 * HH])
        dec_cum = sb("dec_cum", [128, 2 * HH])
        dlast = sb("dlast", [1, 2 * HH])
        elsr = sb("elsr", [1, 2 * HH])
        esl = sb("esl", [128, 2 * HH])
        dwv = sb("dwv", [128, 2 * HH])
        edec = sb("edec", [128, 2 * HH])
        edk = sb("edk", [128, 2 * HH])
        bev = sb("bev", [128, 2 * HH])
        mt1 = sb("mt1", [128, 2, 128])
        mt2 = sb("mt2", [128, 2, 128])
        lsnB = sb("lsnB", [128, 2, 128])
        ltk = sb("ltk", [128, 2, 128])
        abuf = sb("abuf", [128, 2, 128], BF16)
        atbuf = sb("atbuf", [128, 2, 128], BF16)
        pwc = [sb(f"pwc{i}", [128, 2, 256], BF16) for i in range(2)]
        gch = sb("gch", [128, 2, 128], BF16)
        vbb = sb("vbb", [128, 2, 128], BF16)
        wkbe = sb("wkbe", [128, 2, 128], BF16)
        vw = sb("vw", [128, HH, 2, 256], BF16)
        wkcT = sb("wkcT", [128, HH, 2, 128], BF16)
        atn = sb("atn", [128, HH, 2, 128], BF16)
        vnew = sb("vnew", [128, 128], BF16)
        wkdw = sb("wkdw", [128, 128], BF16)
        o_grp = sb("o_grp", [128, HH, TGD])
        ogt = sb("ogt", [128, TGD])
        rrs = sb("rrs", [128, 2 * HH])

        @blk.gpsimd
        def _(gp):
            S.wait(gp, "G")
            S.inc("dWVG", gp.dma_start(wvT_s[:, :, :], wvT_d[:, :, :]), 16)
            S.inc("dWVG", gp.dma_start(wgT_s[:, :, :], wgT_d[:, :, :]), 16)
        wload = S.tok("dWVG")

        @blk.vector
        def _(v):
            v.memset(S_s[:, :, :], 0.0)
            v.memset(S16[:, :, :], 0.0)
            v.memset(kmj[0][:, :, 0:1], 0.0)
            i = v.memset(vpre[0][:, :, 0:3], 0.0)
            S.inc("V", i)
        init_tok = S.tok("V")

        prev = {}

        for g in range(NGD):
            par = g % 2

            @blk.sync
            def _(sy):
                if prev:
                    S.wait(sy, "V", prev["x_V"])
                    S.wait(sy, "T", prev["x_T"])
                i = sy.dma_start(
                    x_tm[:, :, :],
                    x_d[g * TGD:(g + 1) * TGD, :].rearrange("(tt p) d -> p tt d", p=128))
                S.inc("dX", i, 16)
            x_dma = S.tok("dX")

            for deco in (blk.tensor, blk.vector, blk.scalar):
                def f(eng):
                    S.wait(eng, "dX", x_dma)
                    if prev:
                        S.wait(eng, "T", prev["end_T"])
                        S.wait(eng, "V", prev["end_V"])
                        S.wait(eng, "Sc", prev["end_Sc"])
                    else:
                        S.wait(eng, "V", init_tok)
                        S.wait(eng, "G")
                        S.wait(eng, "dWVG", wload)
                        S.wait(eng, "Sc")
                        S.wait(eng, "dS")
                        S.wait(eng, "dW0")
                deco(f)

            @blk.vector
            def _(v):
                for tt in range(2):
                    S.wait(v, "V")
                    i = v.tensor_mul(sqt[:, :], x_tm[:, tt, :], x_tm[:, tt, :])
                    S.inc("V", i)
                    S.wait(v, "V")
                    i = v.reduce_sum(ssq[:, tt * HH:(tt + 1) * HH],
                                     sqt[:, :].rearrange("p (c k) -> p c k", c=HH), AX.X)
                    S.inc("V", i)
            sumsq_tok = S.tok("V")

            @blk.scalar
            def _(s):
                S.wait(s, "V", sumsq_tok)
                i = s.activation(rsq[:, :], ssq[:, :], AF.Ln, bias=C['fc'][:, 0:1])
                S.inc("Sc", i)
                S.wait(s, "Sc")
                i = s.activation(rsq[:, :], rsq[:, :], AF.Exp, scale=-0.5)
                S.inc("Sc", i)
            rsq_tok = S.tok("Sc")

            @blk.vector
            def _(v):
                S.wait(v, "Sc", rsq_tok)
                for tt in range(2):
                    i = v.tensor_tensor(
                        l2tmp[:, tt, :].rearrange("p (c k) -> p c k", c=HH),
                        x_tm[:, tt, :].rearrange("p (c k) -> p c k", c=HH),
                        bc2(rsq[:, tt * HH:(tt + 1) * HH], HH), op=ALU.mult)
                    S.inc("V", i)
            l2n_tok = S.tok("V")

            ev = []
            for tt in range(2):
                for hf in range(2):
                    bank = pb[hf]

                    @blk.tensor
                    def _(t, tt=tt, hf=hf, bank=bank):
                        if len(ev) >= 2:
                            S.wait(t, "Sc", ev[-2])
                        for q in range(4):
                            dc = hf * 4 + q
                            i = t.transpose(bank[:, q * 128:(q + 1) * 128],
                                            x_tm[:, tt, dc * 128:(dc + 1) * 128],
                                            C["identf"][:, :])
                        S.inc("T", i)
                    ttok = S.tok("T")

                    @blk.scalar
                    def _(s, tt=tt, hf=hf, bank=bank, ttok=ttok):
                        S.wait(s, "T", ttok)
                        i = s.copy(
                            xfm[par][:, hf * 4:(hf + 1) * 4, tt * 128:(tt + 1) * 128],
                            bank[:, :].rearrange("p (c k) -> p c k", c=4))
                        S.inc("Sc", i)
                    ev.append(S.tok("Sc"))
            xfm_tok = S.tok("Sc")
            x_T_tok = S.tok("T")

            @blk.scalar
            def _(s):
                if g >= 1:
                    i = s.copy(kmj[par][:, :, 0:1], kmj[1 - par][:, :, TGD:TGD + 1])
                    S.inc("Sc", i)
            halo_tok = S.tok("Sc")

            for tt in range(2):
                bank = px[tt]

                @blk.tensor
                def _(t, tt=tt, bank=bank):
                    S.wait(t, "V", l2n_tok)
                    for h in range(HH):
                        i = t.transpose(bank[:, h * 128:(h + 1) * 128],
                                        l2tmp[:, tt, h * 128:(h + 1) * 128],
                                        C["identb"][:, :])
                    S.inc("T", i)
                kt = S.tok("T")

                @blk.scalar
                def _(s, tt=tt, bank=bank, kt=kt):
                    S.wait(s, "T", kt)
                    i = s.copy(kmj[par][:, :, 1 + tt * 128:1 + (tt + 1) * 128],
                               bank[:, :].rearrange("p (c k) -> p c k", c=HH))
                    S.inc("Sc", i)
            kmj_tok = S.tok("Sc")
            l2_T_tok = S.tok("T")

            @blk.vector
            def _(v):
                if g >= 1:
                    i = v.tensor_copy(vpre[par][:, :, 0:3],
                                      vpre[1 - par][:, :, TGD:TGD + 3])
                    S.inc("V", i)
            vhalo_tok = S.tok("V")

            vev = []
            for dv in range(DCN):
                bank = pb[2 + (dv % 2)]

                @blk.tensor
                def _(t, dv=dv, bank=bank):
                    S.wait(t, "Sc", xfm_tok)
                    if len(vev) >= 2:
                        S.wait(t, "V", vev[-2])
                    for dc in range(DCN):
                        i = t.matmul(bank[:, 0:TGD],
                                     wvT_s[:, dc, dv * 128:(dv + 1) * 128],
                                     xfm[par][:, dc, :],
                                     start=(dc == 0), stop=(dc == DCN - 1))
                    S.inc("T", i)
                vt = S.tok("T")

                @blk.vector
                def _(v, dv=dv, bank=bank, vt=vt):
                    S.wait(v, "T", vt)
                    S.wait(v, "V", vhalo_tok)
                    i = v.tensor_copy(vpre[par][:, dv, 3:3 + TGD], bank[:, 0:TGD])
                    S.inc("V", i)
                vev.append(S.tok("V"))
            vproj_tok = S.tok("V")

            gev = []
            for dv in range(DCN):
                bank = pb[2 + (dv % 2)]

                @blk.tensor
                def _(t, dv=dv, bank=bank):
                    S.wait(t, "V", vev[dv])
                    if len(gev) >= 2:
                        S.wait(t, "Sc", gev[-2])
                    for dc in range(DCN):
                        i = t.matmul(bank[:, 256:512],
                                     wgT_s[:, dc, dv * 128:(dv + 1) * 128],
                                     xfm[par][:, dc, :],
                                     start=(dc == 0), stop=(dc == DCN - 1))
                    S.inc("T", i)
                gt_ = S.tok("T")

                @blk.scalar
                def _(s, dv=dv, bank=bank, gt_=gt_):
                    S.wait(s, "T", gt_)
                    i = s.activation(gfm[:, dv, :], bank[:, 256:512], AF.Silu)
                    S.inc("Sc", i)
                gev.append(S.tok("Sc"))
            gact_tok = S.tok("Sc")

            @blk.vector
            def _(v):
                S.wait(v, "Sc", gact_tok)
                i = v.tensor_scalar_mul(gfm[:, :, :], gfm[:, :, :], C["normw"][:, :])
                S.inc("V", i)
            gnorm_tok = S.tok("V")

            @blk.tensor
            def _(t):
                S.wait(t, "Sc", xfm_tok)
                for dc in range(DCN):
                    t.matmul(pb[4][0:HH, 0:TGD], C["wbT"][:, dc, :], xfm[par][:, dc, :],
                             start=(dc == 0), stop=(dc == DCN - 1))
                for dc in range(DCN):
                    i = t.matmul(pb[4][0:HH, 256:512], C["waT"][:, dc, :],
                                 xfm[par][:, dc, :],
                                 start=(dc == 0), stop=(dc == DCN - 1))
                S.inc("T", i)
            bdz_tok = S.tok("T")

            @blk.scalar
            def _(s):
                S.wait(s, "T", bdz_tok)
                i = s.activation(bdr[:, 0, :], pb[4][0:HH, 0:TGD], AF.Exp, scale=-1.0)
                S.inc("Sc", i)
                i = s.activation(bdr[:, 1, :], pb[4][0:HH, 256:512], AF.Exp,
                                 bias=C["dtb"][:, :], scale=1.0)
                S.inc("Sc", i)
                S.wait(s, "Sc")
                i = s.activation(bdr[:, 1, :], bdr[:, 1, :], AF.Ln, bias=C['fc'][0:HH, 1:2], scale=1.0)
                S.inc("Sc", i)
            bdact_tok = S.tok("Sc")

            @blk.vector
            def _(v):
                S.wait(v, "Sc", bdact_tok)
                i = v.tensor_scalar_add(bdr[:, 0, :], bdr[:, 0, :], 1.0)
                S.inc("V", i)
                S.wait(v, "V")
                i = v.reciprocal(bdr[:, 0, :], bdr[:, 0, :])
                S.inc("V", i)
                i = v.tensor_scalar_mul(bdr[:, 1, :], bdr[:, 1, :], C["nea"][:, :])
                S.inc("V", i)
            bdrows_tok = S.tok("V")

            @blk.tensor
            def _(t):
                S.wait(t, "V", bdrows_tok)
                for cl in range(CPG):
                    t.transpose(pb[5][:, cl * 128:cl * 128 + 8],
                                bdr[:, 0, cl * 128:(cl + 1) * 128],
                                C["identf"][0:HH, 0:HH])
                    i = t.transpose(pb[5][:, 256 + cl * 128:256 + cl * 128 + 8],
                                    bdr[:, 1, cl * 128:(cl + 1) * 128],
                                    C["identf"][0:HH, 0:HH])
                S.inc("T", i)
            bdtr_tok = S.tok("T")

            @blk.vector
            def _(v):
                S.wait(v, "T", bdtr_tok)
                for cl in range(CPG):
                    v.tensor_copy(beta_cm[:, cl * HH:(cl + 1) * HH],
                                  pb[5][:, cl * 128:cl * 128 + 8])
                    i = v.tensor_copy(dec_cm[:, cl * HH:(cl + 1) * HH],
                                      pb[5][:, 256 + cl * 128:256 + cl * 128 + 8])
                S.inc("V", i)
            bdcm_tok = S.tok("V")

            @blk.tensor
            def _(t):
                S.wait(t, "V", bdcm_tok)
                i = t.matmul(pb[5][:, 0:2 * HH], C["triuF"][:, :], dec_cm[:, :],
                             start=True, stop=True)
                S.inc("T", i)
            cum_tok = S.tok("T")

            @blk.vector
            def _(v):
                S.wait(v, "T", cum_tok)
                i = v.tensor_copy(dec_cum[:, :], pb[5][:, 0:2 * HH])
                S.inc("V", i)
            deccum_tok = S.tok("V")

            @blk.tensor
            def _(t):
                S.wait(t, "V", deccum_tok)
                i = t.matmul(pb[5][0:1, 64:64 + 2 * HH], C["e127"][:, :], dec_cum[:, :],
                             start=True, stop=True)
                S.inc("T", i)
            dlast_tok = S.tok("T")

            @blk.scalar
            def _(s):
                S.wait(s, "T", dlast_tok)
                i = s.copy(dlast[:, :], pb[5][0:1, 64:64 + 2 * HH])
                S.inc("Sc", i)
                i = s.activation(elsr[:, :], pb[5][0:1, 64:64 + 2 * HH], AF.Exp)
                S.inc("Sc", i)
            dlrow_tok = S.tok("Sc")

            @blk.tensor
            def _(t):
                S.wait(t, "Sc", dlrow_tok)
                t.matmul(pb[5][:, 128:128 + 2 * HH], C["ones_row"][:, :], elsr[:, :],
                         start=True, stop=True)
                i = t.matmul(pb[5][:, 192:192 + 2 * HH], C["ones_row"][:, :],
                             dlast[:, :], start=True, stop=True)
                S.inc("T", i)
            dbc_tok = S.tok("T")

            @blk.vector
            def _(v):
                S.wait(v, "T", dbc_tok)
                i = v.tensor_copy(esl[:, :], pb[5][:, 128:128 + 2 * HH])
                S.inc("V", i)
                i = v.tensor_sub(dwv[:, :], pb[5][:, 192:192 + 2 * HH], dec_cum[:, :])
                S.inc("V", i)
            dwsub_tok = S.tok("V")

            @blk.scalar
            def _(s):
                S.wait(s, "V", dwsub_tok)
                i = s.activation(dwv[:, :], dwv[:, :], AF.Exp)
                S.inc("Sc", i)
                i = s.activation(edec[:, :], dec_cum[:, :], AF.Exp)
                S.inc("Sc", i)
            dfam_sc = S.tok("Sc")

            @blk.vector
            def _(v):
                S.wait(v, "Sc", dfam_sc)
                i = v.tensor_scalar_mul(edk[:, :], edec[:, :], RSK)
                S.inc("V", i)
                i = v.tensor_mul(bev[:, :], beta_cm[:, :], edec[:, :])
                S.inc("V", i)
            dfam_v = S.tok("V")

            csl = []
            for dc in range(DCN):
                @blk.vector
                def _(v, dc=dc):
                    S.wait(v, "V", vproj_tok)
                    if len(csl) >= 2:
                        S.wait(v, "Sc", csl[-2])
                    buf = cvt[dc % 2]
                    i = v.tensor_scalar(buf[:, :], vpre[par][:, dc, 0:TGD],
                                        C["convw"][:, dc, 0:1], None, op0=ALU.mult)
                    S.inc("V", i)
                    for j in range(1, 4):
                        S.wait(v, "V")
                        i = v.scalar_tensor_tensor(
                            buf[:, :], vpre[par][:, dc, j:j + TGD],
                            C["convw"][:, dc, j:j + 1], buf[:, :],
                            op0=ALU.mult, op1=ALU.add)
                        S.inc("V", i)
                cvtok = S.tok("V")

                @blk.scalar
                def _(s, dc=dc, cvtok=cvtok):
                    S.wait(s, "V", cvtok)
                    i = s.activation(vfm[:, dc, :], cvt[dc % 2][:, :], AF.Silu,
                                     bias=C["convb"][:, dc:dc + 1])
                    S.inc("Sc", i)
                csl.append(S.tok("Sc"))
            vsilu_tok = S.tok("Sc")

            tmev = []
            for kind in range(3):
                for tt in range(2):
                    bank = px[tt]

                    @blk.tensor
                    def _(t, kind=kind, tt=tt, bank=bank):
                        if kind == 0:
                            S.wait(t, "Sc", vsilu_tok)
                        elif kind == 1:
                            S.wait(t, "V", gnorm_tok)
                        else:
                            S.wait(t, "Sc", max(kmj_tok, halo_tok))
                        if len(tmev) >= 2:
                            S.wait(t, "V", tmev[-2])
                        for h in range(HH):
                            if kind == 0:
                                src = vfm[:, h, tt * 128:(tt + 1) * 128]
                            elif kind == 1:
                                src = gfm[:, h, tt * 128:(tt + 1) * 128]
                            else:
                                src = kmj[par][:, h, tt * 128:tt * 128 + 128]
                            i = t.transpose(bank[:, h * 128:(h + 1) * 128], src,
                                            C["identb"][:, :])
                        S.inc("T", i)
                    trt = S.tok("T")

                    @blk.vector
                    def _(v, kind=kind, tt=tt, bank=bank, trt=trt):
                        S.wait(v, "T", trt)
                        dst = [vtm, gtm, wktm][kind]
                        i = v.tensor_copy(
                            dst[par][:, tt, :, :],
                            bank[:, :].rearrange("p (c k) -> p c k", c=HH))
                        S.inc("V", i)
                    tmev.append(S.tok("V"))
            tm_tok = S.tok("V")

            pre = []
            for h in range(HH):
                cols = slice(h, 2 * HH, HH)
                startV = pre[-1]["endV"] if pre else max(tm_tok, dfam_v)
                startSc = pre[-1]["endSc"] if pre else S.tok("Sc")

                @blk.tensor
                def _(t, h=h, startV=startV, startSc=startSc):
                    S.wait(t, "V", max(startV, tm_tok, dfam_v))
                    S.wait(t, "Sc", startSc)
                    for cl in range(CPG):
                        i = t.transpose(
                            pb[3][:, cl * 128:(cl + 1) * 128],
                            dec_cum[:, cl * HH + h:cl * HH + h + 1].broadcast_to(
                                [128, 128]),
                            C["identf"][:, :])
                    S.inc("T", i)
                mtr_tok = S.tok("T")

                @blk.vector
                def _(v, cols=cols, mtr_tok=mtr_tok):
                    S.wait(v, "T", mtr_tok)
                    i = v.tensor_tensor(mt1[:, :, :], bc2(dec_cum[:, cols], CPG),
                                        pb[3][:, 0:256].rearrange(
                                            "p (c k) -> p c k", c=CPG),
                                        op=ALU.subtract)
                    S.inc("V", i)
                    S.wait(v, "V")
                    i = v.tensor_mul(mt2[:, :, :], mt1[:, :, :], mbc(C["ntriP"], CPG))
                    S.inc("V", i)
                    i = v.tensor_mul(mt1[:, :, :], mt1[:, :, :], mbc(C["trilP"], CPG))
                    S.inc("V", i)
                mmul_tok = S.tok("V")

                @blk.scalar
                def _(s, mmul_tok=mmul_tok):
                    S.wait(s, "V", mmul_tok)
                    i = s.activation(mt2[:, :, :], mt2[:, :, :], AF.Exp)
                    S.inc("Sc", i)
                    i = s.activation(mt1[:, :, :], mt1[:, :, :], AF.Exp)
                    S.inc("Sc", i)
                mexp_tok = S.tok("Sc")

                @blk.vector
                def _(v, cols=cols, mexp_tok=mexp_tok):
                    S.wait(v, "Sc", mexp_tok)
                    i = v.tensor_mul(ltk[:, :, :], mt2[:, :, :], mbc(C["triK"], CPG))
                    S.inc("V", i)
                    i = v.tensor_mul(lsnB[:, :, :], mt1[:, :, :], mbc(C["strN"], CPG))
                    S.inc("V", i)
                    S.wait(v, "V")
                    i = v.tensor_tensor(lsnB[:, :, :], lsnB[:, :, :],
                                        bc2(beta_cm[:, cols], CPG), op=ALU.mult)
                    S.inc("V", i)
                mask_V = S.tok("V")

                @blk.tensor
                def _(t, h=h):
                    S.wait(t, "Sc", bdact_tok)
                    for cl in range(CPG):
                        wkk = kmj[par][:, h, cl * 128:cl * 128 + 128]
                        i = t.matmul(pb[4][:, cl * 128:(cl + 1) * 128], wkk, wkk,
                                     start=True, stop=True)
                    S.inc("T", i)
                raw_tok = S.tok("T")

                @blk.vector
                def _(v, raw_tok=raw_tok, mask_V=mask_V):
                    S.wait(v, "T", raw_tok)
                    S.wait(v, "V", mask_V)
                    i = v.tensor_tensor(abuf[:, :, :],
                                        pb[4][:, 0:256].rearrange(
                                            "p (c k) -> p c k", c=CPG),
                                        lsnB[:, :, :], op=ALU.mult)
                    S.inc("V", i)
                A_V = S.tok("V")

                @blk.tensor
                def _(t, A_V=A_V):
                    S.wait(t, "V", A_V)
                    for cl in range(CPG):
                        i = t.transpose(px[0][:, cl * 128:(cl + 1) * 128],
                                        abuf[:, cl, :], C["identb"][:, :])
                    S.inc("T", i)
                at_tok = S.tok("T")

                @blk.scalar
                def _(s, at_tok=at_tok):
                    S.wait(s, "T", at_tok)
                    i = s.copy(atbuf[:, :, :],
                               px[0][:, 0:256].rearrange("p (c k) -> p c k", c=CPG))
                    S.inc("Sc", i)
                atev_tok = S.tok("Sc")

                @blk.vector
                def _(v, atev_tok=atev_tok):
                    S.wait(v, "Sc", atev_tok)
                    i = v.tensor_tensor(gch[:, :, :], atbuf[:, :, :],
                                        mbc(C["identb"], CPG), op=ALU.add)
                    S.inc("V", i)
                g_tok = S.tok("V")

                curP = lambda b, cl: (abuf[:, cl, :] if b is None else b[:, cl, 0:128])
                curPT = lambda b, cl: (atbuf[:, cl, :] if b is None else b[:, cl, 128:256])
                buf = None
                cur_tok = atev_tok
                for lvl in range(1, 7):
                    @blk.tensor
                    def _(t, buf=buf, lvl=lvl, cur_tok=cur_tok, g_tok=g_tok):
                        S.wait(t, "Sc", cur_tok)
                        S.wait(t, "V", g_tok)
                        for cl in range(CPG):
                            i = t.matmul(pb[4][:, cl * 256:cl * 256 + 128],
                                         curPT(buf, cl), curP(buf, cl),
                                         start=True, stop=True)
                            if lvl < 6:
                                i = t.matmul(pb[4][:, cl * 256 + 128:cl * 256 + 256],
                                             curP(buf, cl), curPT(buf, cl),
                                             start=True, stop=True)
                        S.inc("T", i)
                    pw_tok = S.tok("T")

                    @blk.scalar
                    def _(s, lvl=lvl, pw_tok=pw_tok):
                        S.wait(s, "T", pw_tok)
                        i = s.copy(pwc[lvl % 2][:, :, :],
                                   pb[4][:, 0:512].rearrange("p (c k) -> p c k", c=CPG))
                        S.inc("Sc", i)
                    pwev_tok = S.tok("Sc")

                    @blk.tensor
                    def _(t, lvl=lvl, pwev_tok=pwev_tok):
                        S.wait(t, "Sc", pwev_tok)
                        for cl in range(CPG):
                            i = t.matmul(pb[5][:, 256 + cl * 128:256 + (cl + 1) * 128],
                                         pwc[lvl % 2][:, cl, 0:128], gch[:, cl, :],
                                         start=True, stop=True)
                        S.inc("T", i)
                    gm_tok = S.tok("T")

                    @blk.vector
                    def _(v, gm_tok=gm_tok):
                        S.wait(v, "T", gm_tok)
                        i = v.tensor_tensor(
                            gch[:, :, :], gch[:, :, :],
                            pb[5][:, 256:512].rearrange("p (c k) -> p c k", c=CPG),
                            op=ALU.add)
                        S.inc("V", i)
                    g_tok = S.tok("V")
                    buf, cur_tok = pwc[lvl % 2], pwev_tok

                @blk.vector
                def _(v, h=h, cols=cols, g_tok=g_tok):
                    S.wait(v, "V", g_tok)
                    i = v.tensor_tensor(vbb[:, :, :], vtm[par][:, :, h, :],
                                        bc2(beta_cm[:, cols], CPG), op=ALU.mult)
                    S.inc("V", i)
                    i = v.tensor_tensor(wkbe[:, :, :], wktm[par][:, :, h, :],
                                        bc2(bev[:, cols], CPG), op=ALU.mult)
                    S.inc("V", i)
                vbw_tok = S.tok("V")

                @blk.tensor
                def _(t, vbw_tok=vbw_tok):
                    S.wait(t, "V", vbw_tok)
                    for cl in range(CPG):
                        t.matmul(pb[4][:, cl * 256:cl * 256 + 128],
                                 gch[:, cl, :], vbb[:, cl, :], start=True, stop=True)
                        i = t.matmul(pb[4][:, cl * 256 + 128:cl * 256 + 256],
                                     gch[:, cl, :], wkbe[:, cl, :],
                                     start=True, stop=True)
                    S.inc("T", i)
                ap_tok = S.tok("T")

                @blk.scalar
                def _(s, h=h, ap_tok=ap_tok):
                    S.wait(s, "T", ap_tok)
                    i = s.copy(vw[:, h, :, :],
                               pb[4][:, 0:512].rearrange("p (c k) -> p c k", c=CPG))
                    S.inc("Sc", i)
                apev_tok = S.tok("Sc")

                @blk.tensor
                def _(t, h=h, apev_tok=apev_tok):
                    S.wait(t, "Sc", apev_tok)
                    for cl in range(CPG):
                        i = t.transpose(px[0][:, 256 + cl * 128:256 + (cl + 1) * 128],
                                        vw[:, h, cl, 128:256], C["identb"][:, :])
                    S.inc("T", i)
                wt_tok = S.tok("T")

                @blk.scalar
                def _(s, h=h, wt_tok=wt_tok):
                    S.wait(s, "T", wt_tok)
                    i = s.copy(wkcT[:, h, :, :],
                               px[0][:, 256:512].rearrange("p (c k) -> p c k", c=CPG))
                    S.inc("Sc", i)
                wtev_tok = S.tok("Sc")

                @blk.tensor
                def _(t, h=h):
                    for cl in range(CPG):
                        wkk = kmj[par][:, h, cl * 128:cl * 128 + 128]
                        rkk = kmj[par][:, h, cl * 128 + 1:cl * 128 + 129]
                        i = t.matmul(pb[3][:, cl * 128:(cl + 1) * 128], wkk, rkk,
                                     start=True, stop=True)
                    S.inc("T", i)
                att_tok = S.tok("T")

                @blk.vector
                def _(v, h=h, att_tok=att_tok, mask_V=mask_V):
                    S.wait(v, "T", att_tok)
                    S.wait(v, "V", mask_V)
                    i = v.tensor_tensor(atn[:, h, :, :],
                                        pb[3][:, 0:256].rearrange(
                                            "p (c k) -> p c k", c=CPG),
                                        ltk[:, :, :], op=ALU.mult)
                    S.inc("V", i)
                atn_tok = S.tok("V")

                pre.append(dict(endV=S.tok("V"), endSc=S.tok("Sc"),
                                apev=apev_tok, wkcT=wtev_tok, atn=atn_tok))

            pre_V = S.tok("V")
            pre_Sc = S.tok("Sc")

            last_V = pre_V
            last_Sc = pre_Sc
            for cl in range(CPG):
                for h in range(HH):
                    col = cl * HH + h

                    @blk.tensor
                    def _(t, h=h, cl=cl, last_V=last_V, last_Sc=last_Sc):
                        S.wait(t, "Sc", max(pre[h]["wkcT"], last_Sc))
                        S.wait(t, "V", max(pre[h]["atn"], last_V))
                        i = t.matmul(pb[5][:, 0:128], wkcT[:, h, cl, :], S16[:, h, :],
                                     start=True, stop=True)
                        S.inc("T", i)
                    vn_tok = S.tok("T")

                    @blk.vector
                    def _(v, h=h, cl=cl, vn_tok=vn_tok):
                        S.wait(v, "T", vn_tok)
                        i = v.tensor_tensor(vnew[:, :], vw[:, h, cl, 0:128],
                                            pb[5][:, 0:128], op=ALU.subtract)
                        S.inc("V", i)
                    vnew_tok = S.tok("V")

                    @blk.vector
                    def _(v, h=h, cl=cl, col=col, vnew_tok=vnew_tok):
                        S.wait(v, "V")
                        i = v.tensor_scalar_mul(wkdw[:, :], wktm[par][:, cl, h, :],
                                                dwv[:, col:col + 1])
                        S.inc("V", i)
                    wkdw_tok = S.tok("V")

                    @blk.tensor
                    def _(t, h=h, cl=cl, vnew_tok=vnew_tok, wkdw_tok=wkdw_tok):
                        S.wait(t, "V", wkdw_tok)
                        rkk = kmj[par][:, h, cl * 128 + 1:cl * 128 + 129]
                        t.matmul(pb[5][:, 128:256], rkk, S16[:, h, :],
                                 start=True, stop=True)
                        t.matmul(pb[5][:, 256:384], atn[:, h, cl, :], vnew[:, :],
                                 start=True, stop=True)
                        i = t.matmul(pb[5][:, 384:512], wkdw[:, :], vnew[:, :],
                                     start=True, stop=True)
                        S.inc("T", i)
                    smm_tok = S.tok("T")

                    @blk.vector
                    def _(v, h=h, cl=cl, col=col, smm_tok=smm_tok):
                        S.wait(v, "T", smm_tok)
                        i = v.tensor_scalar_mul(ogt[:, 0:128], pb[5][:, 128:256],
                                                edk[:, col:col + 1])
                        S.inc("V", i)
                        S.wait(v, "V")
                        i = v.tensor_tensor(o_grp[:, h, cl * 128:(cl + 1) * 128],
                                            ogt[:, 0:128], pb[5][:, 256:384],
                                            op=ALU.add)
                        S.inc("V", i)
                        i = v.scalar_tensor_tensor(
                            S_s[:, h, :], S_s[:, h, :], esl[:, col:col + 1],
                            pb[5][:, 384:512], op0=ALU.mult, op1=ALU.add)
                        S.inc("V", i)
                        S.wait(v, "V")
                        i = v.tensor_copy(S16[:, h, :], S_s[:, h, :])
                        S.inc("V", i)
                    last_V = S.tok("V")
                    last_Sc = S.tok("Sc")
            scan_V = last_V
            scan_Sc = last_Sc

            @blk.vector
            def _(v):
                S.wait(v, "V", scan_V)
                S.wait(v, "Sc", scan_Sc)
                for h in range(HH):
                    S.wait(v, "V")
                    i = v.tensor_mul(ogt[:, :], o_grp[:, h, :], o_grp[:, h, :])
                    S.inc("V", i)
                    S.wait(v, "V")
                    i = v.reduce_sum(rrs[:, h * 2:(h + 1) * 2],
                                     ogt[:, :].rearrange("p (c k) -> p c k", c=2),
                                     AX.X)
                    S.inc("V", i)
            rr_tok = S.tok("V")

            @blk.scalar
            def _(s):
                S.wait(s, "V", rr_tok)
                i = s.activation(rrs[:, :], rrs[:, :], AF.Ln, scale=1.0 / 128.0,
                                 bias=C['fc'][:, 2:3])
                S.inc("Sc", i)
                S.wait(s, "Sc")
                i = s.activation(rrs[:, :], rrs[:, :], AF.Exp, scale=-0.5)
                S.inc("Sc", i)
            rstd_tok = S.tok("Sc")

            half = g // (NGD // 2)
            tt0 = (2 * g) % NTT
            for h in range(HH):
                @blk.vector
                def _(v, h=h):
                    S.wait(v, "Sc", rstd_tok)
                    S.wait(v, "V")
                    i = v.tensor_tensor(
                        o_grp[:, h, :].rearrange("p (c k) -> p c k", c=2),
                        o_grp[:, h, :].rearrange("p (c k) -> p c k", c=2),
                        gtm[par][:, :, h, :], op=ALU.mult)
                    S.inc("V", i)
                    S.wait(v, "V")
                    i = v.tensor_tensor(
                        o_grp[:, h, :].rearrange("p (c k) -> p c k", c=2),
                        o_grp[:, h, :].rearrange("p (c k) -> p c k", c=2),
                        bc2(rrs[:, h * 2:(h + 1) * 2], 2), op=ALU.mult)
                    S.inc("V", i)
                    S.wait(v, "V")
                    i = v.scalar_tensor_tensor(
                        o_mine[:, h, tt0:tt0 + 2, :],
                        o_grp[:, h, :].rearrange("p (c k) -> p c k", c=2),
                        C["msel"][:, half:half + 1],
                        o_mine[:, h, tt0:tt0 + 2, :],
                        op0=ALU.mult, op1=ALU.add)
                    S.inc("V", i)

            prev = dict(x_V=l2n_tok, x_T=x_T_tok,
                        end_T=S.tok("T"), end_V=S.tok("V"), end_Sc=S.tok("Sc"))



def _p2(nc, blk, S, C, o_mine, out_acc, xn_s, wdT_s, pb, px, P, xh_d, woT_d):
    HT, NGL, TGL = P["HT"], P["NGL"], P["TGL"]
    with ExitStack() as ctx:
        sb = lambda name, shape, dt=F32: ctx.enter_context(nc.sbuf_tensor(name, shape, dt))
        woT_s = sb("woT_s", [128, DCN, D], BF16)
        xh_tm = sb("xh_tm", [128, 2, D])
        ofm = sb("ofm", [128, DCN, TGL], BF16)
        res1 = sb("res1", [128, DCN, TGL])
        sums = sb("sums", [1, 2 * TGL])
        stat = sb("stat", [1, 3 * TGL])
        lgr = sb("lgr", [8, TGL])
        lgt = sb("lgt", [128, 8])
        prb = sb("prb", [128, 8])
        srt8 = sb("srt8", [128, 8])
        mxb = sb("mxb", [128, 1])
        smr = sb("smr", [128, 1])
        sqv = sb("sqv", [128, TGL])

        @blk.gpsimd
        def _(g):
            i = g.dma_start(woT_s[:, :, :], woT_d[:, :, :])
            S.inc("dWO", i, 16)
        wo_tok = S.tok("dWO")

        prevg = {}
        for gl in range(NGL):
            @blk.sync
            def _(sy):
                if prevg:
                    S.wait(sy, "T", prevg["xh_T"])
                i = sy.dma_start(
                    xh_tm[:, :, :],
                    xh_d[gl * TGL:(gl + 1) * TGL, :].rearrange(
                        "(tt p) d -> p tt d", p=128))
                S.inc("dX", i, 16)
            xh_dma = S.tok("dX")

            for deco in (blk.tensor, blk.vector, blk.scalar):
                def f(eng):
                    S.wait(eng, "dX", xh_dma)
                    S.wait(eng, "dWO", wo_tok)
                    if prevg:
                        S.wait(eng, "T", prevg["end_T"])
                        S.wait(eng, "V", prevg["end_V"])
                        S.wait(eng, "Sc", prevg["end_Sc"])
                deco(f)

            tt0 = gl * (TGL // 128)
            for tt in range(2):
                bank = px[tt]

                @blk.tensor
                def _(t, tt=tt, bank=bank):
                    for h in range(HH):
                        i = t.transpose(bank[:, h * 128:(h + 1) * 128],
                                        o_mine[:, h, tt0 + tt, :], C["identb"][:, :])
                    S.inc("T", i)
                ot_ = S.tok("T")

                @blk.scalar
                def _(s, tt=tt, bank=bank, ot_=ot_):
                    S.wait(s, "T", ot_)
                    i = s.copy(ofm[:, :, tt * 128:(tt + 1) * 128],
                               bank[:, :].rearrange("p (c k) -> p c k", c=HH))
                    S.inc("Sc", i)
            ofm_tok = S.tok("Sc")

            rev = []
            for dv in range(DCN):
                bank = pb[dv % 2]
                bank2 = pb[2 + (dv % 2)]

                @blk.tensor
                def _(t, dv=dv, bank=bank, bank2=bank2):
                    S.wait(t, "Sc", ofm_tok)
                    if len(rev) >= 2:
                        S.wait(t, "V", rev[-2])
                    for tt in range(2):
                        t.transpose(bank2[:, tt * 128:(tt + 1) * 128],
                                    xh_tm[:, tt, dv * 128:(dv + 1) * 128],
                                    C["identf"][:, :])
                    for dc in range(DCN):
                        i = t.matmul(bank[:, 0:TGL],
                                     woT_s[:, dc, dv * 128:(dv + 1) * 128],
                                     ofm[:, dc, :],
                                     start=(dc == 0), stop=(dc == DCN - 1))
                    S.inc("T", i)
                rt = S.tok("T")

                @blk.scalar
                def _(s, dv=dv, bank2=bank2, rt=rt):
                    S.wait(s, "T", rt)
                    i = s.copy(res1[:, dv, :], bank2[:, 0:TGL])
                    S.inc("Sc", i)
                sev = S.tok("Sc")

                @blk.vector
                def _(v, dv=dv, bank=bank, sev=sev):
                    S.wait(v, "Sc", sev)
                    i = v.tensor_tensor(res1[:, dv, :], res1[:, dv, :], bank[:, 0:TGL],
                                        op=ALU.add)
                    S.inc("V", i)
                rev.append(S.tok("V"))
            res1_tok = S.tok("V")
            xh_T_tok = S.tok("T")

            @blk.tensor
            def _(t):
                S.wait(t, "V", res1_tok)
                for dc in range(DCN):
                    i = t.matmul(pb[4][0:1, 0:TGL], C["ones_colf"][:, :],
                                 res1[:, dc, :],
                                 start=(dc == 0), stop=(dc == DCN - 1))
                S.inc("T", i)
            sum_tok = S.tok("T")

            sq_T = sum_tok
            for dc in range(DCN):
                @blk.vector
                def _(v, dc=dc, sq_T=sq_T):
                    S.wait(v, "V", res1_tok)
                    S.wait(v, "T", sq_T)
                    i = v.tensor_mul(sqv[:, :], res1[:, dc, :], res1[:, dc, :])
                    S.inc("V", i)
                vq = S.tok("V")

                @blk.tensor
                def _(t, dc=dc, vq=vq):
                    S.wait(t, "V", vq)
                    i = t.matmul(pb[5][0:1, 0:TGL], C["ones_colf"][:, :], sqv[:, :],
                                 start=(dc == 0), stop=(dc == DCN - 1))
                    S.inc("T", i)
                sq_T = S.tok("T")
            sumsq_tok = sq_T

            @blk.scalar
            def _(s):
                S.wait(s, "T", sumsq_tok)
                i = s.mul(sums[:, 0:TGL], pb[4][0:1, 0:TGL], 1.0 / D)
                S.inc("Sc", i)
                i = s.mul(sums[:, TGL:2 * TGL], pb[5][0:1, 0:TGL], 1.0 / D)
                S.inc("Sc", i)
            mv_tok = S.tok("Sc")

            @blk.vector
            def _(v):
                S.wait(v, "Sc", mv_tok)
                i = v.tensor_mul(stat[:, 0:TGL], sums[:, 0:TGL], sums[:, 0:TGL])
                S.inc("V", i)
                S.wait(v, "V")
                i = v.tensor_sub(stat[:, 0:TGL], sums[:, TGL:2 * TGL], stat[:, 0:TGL])
                S.inc("V", i)
            var_tok = S.tok("V")

            @blk.scalar
            def _(s):
                S.wait(s, "V", var_tok)
                i = s.activation(stat[:, TGL:2 * TGL], stat[:, 0:TGL], AF.Ln,
                                 bias=C['fc'][0:1, 3:4])
                S.inc("Sc", i)
                S.wait(s, "Sc")
                i = s.activation(stat[:, TGL:2 * TGL], stat[:, TGL:2 * TGL], AF.Exp,
                                 scale=-0.5)
                S.inc("Sc", i)
            rstd_tok = S.tok("Sc")

            @blk.vector
            def _(v):
                S.wait(v, "Sc", rstd_tok)
                i = v.tensor_mul(stat[:, 2 * TGL:3 * TGL], sums[:, 0:TGL],
                                 stat[:, TGL:2 * TGL])
                S.inc("V", i)
            mr_tok = S.tok("V")

            @blk.tensor
            def _(t):
                S.wait(t, "V", mr_tok)
                t.matmul(pb[5][:, 0:TGL], C["ones_row"][:, :], stat[:, TGL:2 * TGL],
                         start=True, stop=True)
                i = t.matmul(pb[4][:, 0:TGL], C["ones_row"][:, :],
                             stat[:, 2 * TGL:3 * TGL], start=True, stop=True)
                S.inc("T", i)
            bcast_tok = S.tok("T")

            for dc in range(DCN):
                @blk.vector
                def _(v, dc=dc):
                    S.wait(v, "T", bcast_tok)
                    i = v.tensor_tensor(res1[:, dc, :], res1[:, dc, :],
                                        pb[5][:, 0:TGL], op=ALU.mult)
                    S.inc("V", i)
                    S.wait(v, "V")
                    i = v.tensor_tensor(res1[:, dc, :], res1[:, dc, :],
                                        pb[4][:, 0:TGL], op=ALU.subtract)
                    S.inc("V", i)
                    S.wait(v, "V")
                    i = v.tensor_scalar_mul(
                        out_acc[:, dc, gl * TGL:(gl + 1) * TGL], res1[:, dc, :], 2.0)
                    S.inc("V", i)
                    i = v.tensor_scalar(
                        xn_s[:, dc, gl * TGL:(gl + 1) * TGL], res1[:, dc, :],
                        C["gam"][:, dc:dc + 1], C["betf"][:, dc:dc + 1],
                        op0=ALU.mult, op1=ALU.add)
                    S.inc("V", i)
            apply_tok = S.tok("V")

            @blk.tensor
            def _(t):
                S.wait(t, "V", apply_tok)
                for dc in range(DCN):
                    i = t.matmul(pb[4][0:E, 0:TGL], C["rwg"][:, dc, :],
                                 res1[:, dc, :],
                                 start=(dc == 0), stop=(dc == DCN - 1))
                S.inc("T", i)
            lg_tok = S.tok("T")

            @blk.vector
            def _(v):
                S.wait(v, "T", lg_tok)
                i = v.tensor_scalar_add(lgr[:, :], pb[4][0:E, 0:TGL],
                                        C["rbc"][:, :])
                S.inc("V", i)
            lgr_tok = S.tok("V")

            wtoks = []
            for tt in range(2):
                @blk.tensor
                def _(t, tt=tt, lgr_tok=lgr_tok):
                    S.wait(t, "V", lgr_tok)
                    if wtoks:
                        S.wait(t, "V", wtoks[-1])
                    i = t.transpose(pb[3][0:128, tt * 128:tt * 128 + 8],
                                    lgr[:, tt * 128:(tt + 1) * 128],
                                    C["identf"][0:E, 0:E])
                    S.inc("T", i)
                ltr = S.tok("T")

                @blk.vector
                def _(v, tt=tt, ltr=ltr):
                    S.wait(v, "T", ltr)
                    i = v.tensor_copy(lgt[:, :], pb[3][:, tt * 128:tt * 128 + 8])
                    S.inc("V", i)
                    S.wait(v, "V")
                    i = v.reduce_max(mxb[:, :], lgt[:, :], AX.X)
                    S.inc("V", i)
                    S.wait(v, "V")
                    i = v.tensor_scalar_mul(mxb[:, :], mxb[:, :], -1.0 / TEMP)
                    S.inc("V", i)
                mx_tok = S.tok("V")

                @blk.scalar
                def _(s, mx_tok=mx_tok):
                    S.wait(s, "V", mx_tok)
                    i = s.activation(prb[:, :], lgt[:, :], AF.Exp,
                                     scale=1.0 / TEMP, bias=mxb[:, :])
                    S.inc("Sc", i)
                ex_tok = S.tok("Sc")

                @blk.vector
                def _(v, ex_tok=ex_tok):
                    S.wait(v, "Sc", ex_tok)
                    i = v.reduce_sum(smr[:, :], prb[:, :], AX.X)
                    S.inc("V", i)
                    S.wait(v, "V")
                    i = v.reciprocal(smr[:, :], smr[:, :])
                    S.inc("V", i)
                    S.wait(v, "V")
                    i = v.tensor_scalar_mul(prb[:, :], prb[:, :], smr[:, :])
                    S.inc("V", i)
                    S.wait(v, "V")
                    i = v.max(out=srt8[:, :], in_=prb[:, :])
                    S.inc("V", i)
                    S.wait(v, "V")
                    i = v.tensor_scalar(srt8[:, :], prb[:, :], srt8[:, 1:2], None,
                                        op0=ALU.is_ge)
                    S.inc("V", i)
                    S.wait(v, "V")
                    i = v.tensor_mul(srt8[:, :], srt8[:, :], prb[:, :])
                    S.inc("V", i)
                wd_tok = S.tok("V")

                @blk.tensor
                def _(t, tt=tt, wd_tok=wd_tok):
                    S.wait(t, "V", wd_tok)
                    i = t.transpose(pb[3][0:8, 256 + tt * 128:256 + (tt + 1) * 128],
                                    srt8[:, :], C["identf"][:, :])
                    S.inc("T", i)
                wtr = S.tok("T")

                @blk.vector
                def _(v, tt=tt, wtr=wtr):
                    S.wait(v, "T", wtr)
                    i = v.tensor_copy(
                        wdT_s[:, gl * TGL + tt * 128:gl * TGL + (tt + 1) * 128],
                        pb[3][0:8, 256 + tt * 128:256 + (tt + 1) * 128])
                    S.inc("V", i)
                wtoks.append(S.tok("V"))

            prevg = dict(xh_T=xh_T_tok, end_T=S.tok("T"), end_V=S.tok("V"),
                         end_Sc=S.tok("Sc"))



def _p3(nc, blk, S, C, out_acc, xn_s, wdT_s, P, w1_d, w2_d, dbg=None):
    HT, NGM, TGM = P["HT"], P["NGM"], P["TGM"]
    NE_ = E
    if P.get("p3lim"):
        NE_, NGM = P["p3lim"]
    NF = FF // 128
    NG4 = NF // 4
    with ExitStack() as ctx:
        sb = lambda name, shape, dt=F32: ctx.enter_context(nc.sbuf_tensor(name, shape, dt))
        w1_s = sb("w1s", [128, DCN, FF], BF16)
        w2_s = [sb(f"w2s{i}", [128, 2, D], BF16) for i in range(4)]
        hbuf = [sb(f"hbuf{i}", [128, 4 * TGM], BF16) for i in range(2)]
        wbc_all = sb("wbc_all", [128, NGM * TGM])
        ytmp = sb("ytmp", [128, TGM])
        pbY = [ctx.enter_context(nc.psum_tensor(f"pY{i}", [128, 512], F32))
               for i in range(4)]
        pm = [ctx.enter_context(nc.psum_tensor(f"pm{i}", [128, 1024], F32))
              for i in range(2)]

        w1_tok = [[0] * 8 for _ in range(E)]
        w2_dma = [0, 0, 0, 0]
        w2_rd = [0, 0, 0, 0]
        ring_rd = [0, 0]
        pm_rd = [[], []]
        last_y_read = 0
        wbcev_prev = [0]
        mm1_tok = {}
        sil_tok = {}

        @blk.gpsimd
        def _(g):
            for pc in range(8):
                i = g.dma_start(w1_s[:, :, pc * 512:(pc + 1) * 512],
                                w1_d[0, :, :, pc * 512:(pc + 1) * 512])
                w1_tok[0][pc] = S.inc(f"dW1p{pc}", i, 16)

        for e in range(NE_):
            QW = min(512, NGM * TGM)
            NQ = (NGM * TGM) // QW
            ev_toks = []
            for q in range(NQ):
                @blk.tensor
                def _(t, e=e, q=q):
                    if q == 0:
                        S.wait(t, "V", wbcev_prev[0])
                        for sem, tok in pm_rd[1]:
                            S.wait(t, sem, tok)
                    if q >= 2:
                        S.wait(t, "V", ev_toks[q - 2])
                    half = (q % 2) * 512
                    i = t.matmul(pm[1][:, half:half + QW],
                                 C["e8c"][:, e:e + 1].broadcast_to([8, 128]),
                                 wdT_s[:, q * QW:(q + 1) * QW],
                                 start=True, stop=True)
                    S.inc("T", i)
                wq = S.tok("T")

                @blk.vector
                def _(v, q=q, wq=wq):
                    S.wait(v, "T", wq)
                    i = v.tensor_copy(wbc_all[:, q * QW:(q + 1) * QW],
                                      pm[1][:, (q % 2) * 512:(q % 2) * 512 + QW])
                    S.inc("V", i)
                ev_toks.append(S.tok("V"))
            wbcev_prev[0] = ev_toks[-1]
            pm_rd[1] = [("V", ev_toks[-1])]

            for tg in range(NGM):
                t0 = tg * TGM

                def emit_mm1(g, e=e, t0=t0):
                    @blk.tensor
                    def _(t, g=g):
                        S.wait(t, f"dW1p{g}", w1_tok[e][g])
                        for sem, tok in pm_rd[g % 2]:
                            S.wait(t, sem, tok)
                        bank = pm[g % 2]
                        for f4 in range(4):
                            f = 4 * g + f4
                            col = f4 * 256
                            for dc in range(DCN):
                                i = t.matmul(bank[:, col:col + TGM],
                                             w1_s[:, dc, f * 128:(f + 1) * 128],
                                             xn_s[:, dc, t0:t0 + TGM],
                                             start=(dc == 0 and f4 % 2 == 0),
                                             stop=(dc == DCN - 1 and
                                                   f4 % 2 == 1))
                        S.inc("T", i)
                    mm1_tok[g] = S.tok("T")

                def emit_silu(g):
                    @blk.scalar
                    def _(s, g=g, rr=ring_rd[g % 2]):
                        S.wait(s, "T", mm1_tok[g])
                        S.wait(s, "T", rr)
                        i = s.activation(hbuf[g % 2][:, :], pm[g % 2][:, :],
                                         AF.Silu)
                        S.inc("Sc", i)
                    sil_tok[g] = S.tok("Sc")
                    pm_rd[g % 2] = [("Sc", sil_tok[g])]

                def emit_w2dma(p, e=e):
                    ws = p % 4

                    @blk.gpsimd
                    def _(g_, e=e, p=p, ws=ws, wr=w2_rd[ws]):
                        S.wait(g_, "T", wr)
                        i = g_.dma_start(
                            w2_s[ws][:, :, :],
                            w2_d[e, p * 256:(p + 1) * 256, :].rearrange(
                                "(c p) d -> p c d", p=128))
                        w2_dma[ws] = S.inc(f"dW2s{ws}", i, 16)

                def emit_mm2(g, ly=last_y_read):
                    @blk.tensor
                    def _(t, g=g, ly=ly):
                        S.wait(t, "Sc", sil_tok[g])
                        if g == 0:
                            S.wait(t, "V", ly)
                        S.wait(t, f"dW2s{(2 * g) % 4}", w2_dma[(2 * g) % 4])
                        S.wait(t, f"dW2s{(2 * g + 1) % 4}",
                               w2_dma[(2 * g + 1) % 4])
                        for f4 in range(4):
                            f = 4 * g + f4
                            wsl = w2_s[(f // 2) % 4]
                            for dc in range(DCN):
                                i = t.matmul(
                                    pbY[dc // 2][:, (dc % 2) * 256:(dc % 2) * 256 + TGM],
                                    wsl[:, f % 2, dc * 128:(dc + 1) * 128],
                                    hbuf[g % 2][:, f4 * 256:f4 * 256 + 256],
                                    start=(f == 0 and dc % 2 == 0),
                                    stop=(f == NF - 1 and dc % 2 == 1))
                        S.inc("T", i)
                    tk = S.tok("T")
                    ring_rd[g % 2] = tk
                    w2_rd[(2 * g) % 4] = tk
                    w2_rd[(2 * g + 1) % 4] = tk

                emit_w2dma(0)
                emit_w2dma(1)
                emit_w2dma(2)
                emit_w2dma(3)
                emit_mm1(0)
                for g in range(NG4):
                    emit_silu(g)
                    if g + 1 < NG4:
                        emit_mm1(g + 1)
                    emit_mm2(g)
                    if g + 2 < NG4:
                        emit_w2dma(2 * (g + 2))
                        emit_w2dma(2 * (g + 2) + 1)
                mm2_done = S.tok("T")

                for dc in range(DCN):
                    @blk.vector
                    def _(v, dc=dc, t0=t0, mm2_done=mm2_done):
                        S.wait(v, "T", mm2_done)
                        S.wait(v, "V")
                        i = v.tensor_tensor(
                            ytmp[:, :],
                            pbY[dc // 2][:, (dc % 2) * 256:(dc % 2) * 256 + TGM],
                            wbc_all[:, t0:t0 + TGM], op=ALU.mult)
                        S.inc("V", i)
                        S.wait(v, "V")
                        i = v.tensor_tensor(out_acc[:, dc, t0:t0 + TGM],
                                            out_acc[:, dc, t0:t0 + TGM],
                                            ytmp[:, :], op=ALU.add)
                        S.inc("V", i)
                last_y_read = S.tok("V")

                if tg == NGM - 1 and e + 1 < NE_:
                    for pc in range(8):
                        @blk.gpsimd
                        def _(g_, e=e, pc=pc, mt=mm1_tok[pc]):
                            S.wait(g_, "T", mt)
                            i = g_.dma_start(
                                w1_s[:, :, pc * 512:(pc + 1) * 512],
                                w1_d[e + 1, :, :, pc * 512:(pc + 1) * 512])
                            w1_tok[e + 1][pc] = S.inc(f"dW1p{pc}", i, 16)



def _p4(nc, blk, S, C, out_acc, P, y_d):
    HT = P["HT"]
    TG = min(512, HT)
    NG = HT // TG
    with ExitStack() as ctx:
        sb = lambda name, shape, dt=F32: ctx.enter_context(nc.sbuf_tensor(name, shape, dt))
        sums = sb("sums4", [1, 2 * TG])
        stat = sb("stat4", [1, 3 * TG])
        sqv = sb("sqv4", [128, TG])
        stg = [sb(f"stg4{i}", [128, D]) for i in range(2)]
        pb = [ctx.enter_context(nc.psum_tensor(f"pq{i}", [128, 512], F32))
              for i in range(6)]

        stg_st = [0, 0]
        prevo = {}
        for g in range(NG):
            t0 = g * TG

            @blk.tensor
            def _(t):
                if prevo:
                    S.wait(t, "T", prevo["end_T"])
                    S.wait(t, "V", prevo["end_V"])
                    S.wait(t, "Sc", prevo["end_Sc"])
                for dc in range(DCN):
                    i = t.matmul(pb[4][0:1, 0:TG], C["ones_colf"][:, :],
                                 out_acc[:, dc, t0:t0 + TG],
                                 start=(dc == 0), stop=(dc == DCN - 1))
                S.inc("T", i)
            sum_tok = S.tok("T")

            sq_T = sum_tok
            for dc in range(DCN):
                @blk.vector
                def _(v, dc=dc, sq_T=sq_T):
                    if prevo:
                        S.wait(v, "T", prevo["end_T"])
                    S.wait(v, "T", sq_T)
                    i = v.tensor_mul(sqv[:, :], out_acc[:, dc, t0:t0 + TG],
                                     out_acc[:, dc, t0:t0 + TG])
                    S.inc("V", i)
                vq = S.tok("V")

                @blk.tensor
                def _(t, dc=dc, vq=vq):
                    S.wait(t, "V", vq)
                    i = t.matmul(pb[5][0:1, 0:TG], C["ones_colf"][:, :], sqv[:, :],
                                 start=(dc == 0), stop=(dc == DCN - 1))
                    S.inc("T", i)
                sq_T = S.tok("T")
            sumsq_tok = sq_T

            @blk.scalar
            def _(s):
                if prevo:
                    S.wait(s, "Sc", prevo["end_Sc"])
                S.wait(s, "T", sumsq_tok)
                i = s.mul(sums[:, 0:TG], pb[4][0:1, 0:TG], 1.0 / D)
                S.inc("Sc", i)
                i = s.mul(sums[:, TG:2 * TG], pb[5][0:1, 0:TG], 1.0 / D)
                S.inc("Sc", i)
            mv_tok = S.tok("Sc")

            @blk.vector
            def _(v):
                S.wait(v, "Sc", mv_tok)
                i = v.tensor_mul(stat[:, 0:TG], sums[:, 0:TG], sums[:, 0:TG])
                S.inc("V", i)
                S.wait(v, "V")
                i = v.tensor_sub(stat[:, 0:TG], sums[:, TG:2 * TG], stat[:, 0:TG])
                S.inc("V", i)
            var_tok = S.tok("V")

            @blk.scalar
            def _(s):
                S.wait(s, "V", var_tok)
                i = s.activation(stat[:, TG:2 * TG], stat[:, 0:TG], AF.Ln, bias=C['fc'][0:1, 3:4])
                S.inc("Sc", i)
                S.wait(s, "Sc")
                i = s.activation(stat[:, TG:2 * TG], stat[:, TG:2 * TG], AF.Exp,
                                 scale=-0.5)
                S.inc("Sc", i)
            rstd_tok = S.tok("Sc")

            @blk.vector
            def _(v):
                S.wait(v, "Sc", rstd_tok)
                i = v.tensor_mul(stat[:, 2 * TG:3 * TG], sums[:, 0:TG],
                                 stat[:, TG:2 * TG])
                S.inc("V", i)
            mr_tok = S.tok("V")

            @blk.tensor
            def _(t):
                S.wait(t, "V", mr_tok)
                t.matmul(pb[5][:, 0:TG], C["ones_row"][:, :], stat[:, TG:2 * TG],
                         start=True, stop=True)
                i = t.matmul(pb[4][:, 0:TG], C["ones_row"][:, :],
                             stat[:, 2 * TG:3 * TG], start=True, stop=True)
                S.inc("T", i)
            bc_tok = S.tok("T")

            for dc in range(DCN):
                @blk.vector
                def _(v, dc=dc):
                    S.wait(v, "T", bc_tok)
                    i = v.tensor_tensor(out_acc[:, dc, t0:t0 + TG],
                                        out_acc[:, dc, t0:t0 + TG],
                                        pb[5][:, 0:TG], op=ALU.mult)
                    S.inc("V", i)
                    S.wait(v, "V")
                    i = v.tensor_tensor(out_acc[:, dc, t0:t0 + TG],
                                        out_acc[:, dc, t0:t0 + TG],
                                        pb[4][:, 0:TG], op=ALU.subtract)
                    S.inc("V", i)
            nrm_tok = S.tok("V")

            prev_stv = 0
            for tt in range(TG // 128):
                sslot = tt % 2
                for hf in range(2):
                    bank = pb[hf]

                    @blk.tensor
                    def _(t, tt=tt, hf=hf, bank=bank, sslot=sslot,
                          prev_stv=prev_stv):
                        S.wait(t, "V", nrm_tok)
                        S.wait(t, "V", prev_stv)
                        S.wait(t, f"st{sslot}", stg_st[sslot])
                        for q in range(4):
                            dc = hf * 4 + q
                            i = t.transpose(
                                bank[:, q * 128:(q + 1) * 128],
                                out_acc[:, dc,
                                        t0 + tt * 128:t0 + tt * 128 + 128],
                                C["identf"][:, :])
                        S.inc("T", i)
                    tk = S.tok("T")

                    @blk.vector
                    def _(v, hf=hf, bank=bank, tk=tk, sslot=sslot):
                        S.wait(v, "T", tk)
                        S.wait(v, f"st{sslot}", stg_st[sslot])
                        i = v.tensor_copy(stg[sslot][:, hf * 512:(hf + 1) * 512],
                                          bank[:, 0:512])
                        S.inc("V", i)
                stv = S.tok("V")
                prev_stv = stv

                @blk.sync
                def _(sy, tt=tt, stv=stv, sslot=sslot):
                    S.wait(sy, "V", stv)
                    i = sy.dma_start(y_d[t0 + tt * 128:t0 + (tt + 1) * 128, :],
                                     stg[sslot][:, :])
                    S.inc(f"st{sslot}", i, 16)
                stg_st[sslot] = S.tok(f"st{sslot}")
            prevo = dict(end_T=S.tok("T"), end_V=S.tok("V"), end_Sc=S.tok("Sc"))



def kernel(L0_attn_x, L0_attn_Wv, L0_attn_Wg, L0_attn_Wo, L0_attn_Wb, L0_attn_Wa,
           L0_attn_dt_bias, L0_attn_A_log, L0_attn_norm_w, L0_attn_conv_w,
           L0_attn_conv_b, L0_ffn_gamma, L0_ffn_beta, L0_ffn_router_w,
           L0_ffn_W1, L0_ffn_W2):
    T = np.asarray(L0_attn_x).shape[1]
    HT = T // 2
    nc = build(T=T)

    cc = np.ascontiguousarray
    f32 = lambda a: np.asarray(a, np.float32)

    def chunkp(w):
        w = f32(w)
        return cc(w.reshape(DCN, 128, -1).transpose(1, 0, 2))

    wvT = chunkp(f32(L0_attn_Wv).T)
    wgT = chunkp(f32(L0_attn_Wg).T)
    woT = chunkp(f32(L0_attn_Wo).T)
    wbT = chunkp(f32(L0_attn_Wb).T)
    waT = chunkp(f32(L0_attn_Wa).T)
    rwT = chunkp(f32(L0_ffn_router_w).T)
    convw = chunkp(f32(L0_attn_conv_w))
    convb = cc(f32(L0_attn_conv_b).reshape(DCN, 128).T)
    gam = cc(f32(L0_ffn_gamma).reshape(DCN, 128).T)
    betf = cc(f32(L0_ffn_beta).reshape(DCN, 128).T)
    w1 = cc(f32(L0_ffn_W1).reshape(E, DCN, 128, FF).transpose(0, 2, 1, 3))
    w2 = cc(f32(L0_ffn_W2))
    dtb = cc(f32(L0_attn_dt_bias).reshape(HH, 1))
    alog = cc(f32(L0_attn_A_log).reshape(HH, 1))
    normw = cc(f32(L0_attn_norm_w).reshape(128, 1))
    x = f32(L0_attn_x)

    in_maps = []
    for c in range(8):
        b, half = c >> 1, c & 1
        msel = np.zeros((128, 2), np.float32)
        msel[:, half] = 1.0
        in_maps.append(dict(
            xb=cc(x[b]), xh=cc(x[b, half * HT:(half + 1) * HT]),
            wvT=wvT, wgT=wgT, woT=woT, wbT=wbT, waT=waT,
            dtb=dtb, alog=alog, normw=normw, convw=convw, convb=convb,
            gam=gam, betf=betf, rwT=rwT, w1=w1, w2=w2, msel=msel,
        ))

    res = run_bass_kernel_spmd(nc, in_maps, core_ids=list(range(8)))
    out = np.empty((B, T, D), np.float32)
    for c in range(8):
        b, half = c >> 1, c & 1
        out[b, half * HT:(half + 1) * HT] = np.asarray(res.results[c]["y"])
    return out
